# version 01
# baseline (speedup 1.0000x reference)
import numpy as np
import ml_dtypes
from contextlib import ExitStack

import concourse.bass as bass
import concourse.mybir as mybir
from concourse.bass_utils import run_bass_kernel_spmd

F32 = mybir.dt.float32
BF16 = mybir.dt.bfloat16
I32 = mybir.dt.int32
AF = mybir.ActivationFunctionType
ALU = mybir.AluOpType
AX = mybir.AxisListType

BQ, SQ, HQ, IQ = 2, 4096, 2048, 4096
NH, HD, NS, RK = 2, 2048, 16, 128
TPC = 1024
TW = TPC + 3
MAGIC = float(np.float32(3 * 2.0 ** 22))
BF = ml_dtypes.bfloat16


def bf16_view(ap):
    return ap.bitcast(BF16).rearrange("p (f two) -> p f two", two=2)[:, :, 1:2].squeeze(2)


class Prog:

    ENGINES = ("sync", "tensor", "vector", "scalar", "gpsimd")

    def __init__(self):
        self.events = []
        self.cnt = {}

    def ev(self, engine, fn, waits=(), inc=None):
        assert engine in self.ENGINES
        self.events.append((engine, [(s, c) for (s, c) in waits if c > 0], fn, inc))
        if inc is not None:
            self.cnt[inc[0]] = self.cnt.get(inc[0], 0) + inc[1]

    def c(self, sem):
        return self.cnt.get(sem, 0)

    def emit(self, block, sems):
        for engine in self.ENGINES:
            evs = [(w, f, i) for (e, w, f, i) in self.events if e == engine]
            if not evs:
                continue
            dec = getattr(block, engine)

            def body(eng, evs=evs):
                for waits, fn, inc in evs:
                    for sem, cntv in waits:
                        eng.wait_ge(sems[sem], cntv)
                    instr = fn(eng)
                    if inc is not None:
                        assert instr is not None
                        instr.then_inc(sems[inc[0]], inc[1])
            dec(body)



def build_A():
    nc = bass.Bass("TRN2", target_bir_lowering=False)

    x0T = nc.dram_tensor("x0T", [HQ, TW], F32, kind="ExternalInput").ap()
    wx = nc.dram_tensor("wx", [HQ, IQ], F32, kind="ExternalInput").ap()
    wg = nc.dram_tensor("wg", [HQ, IQ], F32, kind="ExternalInput").ap()
    convw = nc.dram_tensor("convw", [128, 32 * 4], F32, kind="ExternalInput").ap()
    convb = nc.dram_tensor("convb", [128, 32], F32, kind="ExternalInput").ap()
    xpw_dt = nc.dram_tensor("xpw_dt", [128, 32 * RK], F32, kind="ExternalInput").ap()
    xpw_bc = nc.dram_tensor("xpw_bc", [128, 32 * 32], F32, kind="ExternalInput").ap()
    dtw = nc.dram_tensor("dtw", [128, NH * HD], F32, kind="ExternalInput").ap()
    dtb = nc.dram_tensor("dtb", [128, 32], F32, kind="ExternalInput").ap()
    alog = nc.dram_tensor("alog", [128, 32 * NS], F32, kind="ExternalInput").ap()
    dcol = nc.dram_tensor("dcol", [128, 32], F32, kind="ExternalInput").ap()
    onec = nc.dram_tensor("onec", [128, 1], F32, kind="ExternalInput").ap()
    oner = nc.dram_tensor("oner", [1, 128], F32, kind="ExternalInput").ap()
    idn = nc.dram_tensor("idn", [128, 128], F32, kind="ExternalInput").ap()

    y_loc = nc.dram_tensor("y_loc", [IQ, TPC], F32, kind="ExternalOutput").ap()
    cdt_o = nc.dram_tensor("cdt_o", [IQ, TPC], F32, kind="ExternalOutput").ap()
    bmat = nc.dram_tensor("bmat", [NH * NS, TPC], F32, kind="ExternalOutput").ap()
    cmat = nc.dram_tensor("cmat", [NH * NS, TPC], F32, kind="ExternalOutput").ap()
    gate_o = nc.dram_tensor("gate_o", [IQ, TPC], BF16, kind="ExternalOutput").ap()
    hfin = nc.dram_tensor("hfin", [128, 32 * NS], F32, kind="ExternalOutput").ap()
    cdtfin = nc.dram_tensor("cdtfin", [128, 32], F32, kind="ExternalOutput").ap()

    dt_i = nc.dram_tensor("dt_i", [IQ, TPC], F32, kind="Internal").ap()
    xc_i = nc.dram_tensor("xc_i", [IQ, TPC], BF16, kind="Internal").ap()

    p = Prog()
    with ExitStack() as es:
        sb = lambda name, shape, dt=F32: es.enter_context(nc.sbuf_tensor(name, shape, dt))
        psalloc = lambda name, shape, dt=F32: es.enter_context(nc.psum_tensor(name, shape, dt))
        sems = {}
        for s in ["ld", "so", "pe", "ac", "dv", "po"]:
            sems[s] = es.enter_context(nc.semaphore(s))

        x0_sb = sb("x0_sb", [128, 16 * TW])
        xln_sb = sb("xln_sb", [128, 16 * TW], BF16)
        wpair = sb("wpair", [128, 4096])
        dta_v = x0_sb[:, 0:16 * 512]
        h_v = x0_sb[:, 16 * 512:32 * 512]
        brep_v = xln_sb[:, 0:16 * 512]
        crep_v = xln_sb[:, 16 * 512:32 * 512]
        wbufs = [wpair[:, 0:2048], wpair[:, 2048:4096]]
        xca_v = wpair[:].bitcast(BF16)

        onec_sb = sb("onec_sb", [128, 1])
        oner_sb = sb("oner_sb", [1, 128])
        idnb_sb = sb("idnb_sb", [128, 128], BF16)
        idn_sb = sb("idn_sb", [128, 128])
        convw_sb = sb("convw_sb", [128, 32 * 4])
        convb_sb = sb("convb_sb", [128, 32])
        dtb_sb = sb("dtb_sb", [128, 32])
        dcol_sb = sb("dcol_sb", [128, 32])
        aneg_sb = sb("aneg_sb", [128, 32 * NS])
        xpdt_sb = sb("xpdt_sb", [128, 32 * RK])
        xpbc_sb = sb("xpbc_sb", [128, 32 * 32])
        dtw_sb = sb("dtw_sb", [128, NH * HD])
        negc_sb = sb("negc_sb", [128, 1])
        onecol_sb = sb("onecol_sb", [128, 1])
        hst_sb = sb("hst_sb", [128, 32 * NS])
        cdtc_sb = sb("cdtc_sb", [128, 32])

        row_sb = sb("row_sb", [1, 6 * 512])
        xin_sb = sb("xin_sb", [128, TW])
        xcv_sb = sb("xcv_sb", [128, TPC])
        xcb_sb = sb("xcb_sb", [128, TPC], BF16)
        sq_sb = [sb(f"sq_sb{i}", [128, 512]) for i in range(2)]
        dtin_sb = sb("dtin_sb", [128, TPC], BF16)
        bc_sb = sb("bc_sb", [32, TPC])
        sp_sb = [sb(f"sp_sb{i}", [128, 512]) for i in range(3)]
        dt_sb = sb("dt_sb", [128, TPC])
        zero_sb = sp_sb[0]

        bflat_sb = sb("bflat_sb", [1, 512])
        cflat_sb = sb("cflat_sb", [1, 512])
        dtx_sb = sb("dtx_sb", [128, 512], BF16)
        dbu_sb = sb("dbu_sb", [128, 512], BF16)
        g_sb = sb("g_sb", [128, 512], BF16)
        da_sb = xcv_sb[:, 0:512]
        cdt_sb = dt_sb[:, 0:512]
        ya_sb = dt_sb[:, 512:1024]

        pA = psalloc("pA", [128, 512])
        pB = psalloc("pB", [128, 512])
        pC = psalloc("pC", [128, 1024])
        pD = psalloc("pD", [128, 1024])

        def mkload(dst, src):
            return lambda e: e.dma_start(out=dst, in_=src)

        loads = [
            (onec_sb[:], onec[:]),
            (oner_sb[:], oner[:]),
            (idn_sb[:], idn[:]),
            (convw_sb[:], convw[:]),
            (convb_sb[:], convb[:]),
            (dtb_sb[:], dtb[:]),
            (dcol_sb[:], dcol[:]),
            (aneg_sb[:], alog[:]),
            (xpdt_sb[:], xpw_dt[:]),
            (xpbc_sb[:], xpw_bc[:]),
            (dtw_sb[:], dtw[:]),
            (x0_sb[:].rearrange("p (c t) -> p c t", c=16),
             x0T.rearrange("(c p) t -> p c t", p=128)),
        ]
        for dst, src in loads:
            p.ev("sync", mkload(dst, src), inc=("ld", 16))
        LD_CONST = p.c("ld")

        for buf, val in [(negc_sb, -1.0), (onecol_sb, 1.0),
                         (hst_sb, 0.0), (cdtc_sb, 0.0)]:
            p.ev("vector", (lambda b: (lambda e: e.memset(b[:], val)))(buf) if False else
                 (lambda e, b=buf, v=val: e.memset(b[:], v)), inc=("dv", 1))
        p.ev("vector", lambda e: e.tensor_copy(idnb_sb[:], idn_sb[:]),
             waits=[("ld", LD_CONST)], inc=("dv", 1))
        for wt in [xpdt_sb, xpbc_sb, dtw_sb]:
            p.ev("gpsimd", (lambda wt: lambda e: e.tensor_scalar(
                wt[:].bitcast(I32), wt[:].bitcast(I32), 0x8000, None, ALU.add))(wt),
                waits=[("ld", LD_CONST)], inc=("po", 1))
        PO_RTNC = p.c("po")
        p.ev("scalar", lambda e: e.activation(aneg_sb[:], aneg_sb[:], AF.Exp),
             waits=[("ld", LD_CONST)], inc=("ac", 1))
        p.ev("vector", lambda e: e.tensor_scalar_mul(aneg_sb[:], aneg_sb[:], -1.0),
             waits=[("ac", p.c("ac"))], inc=("dv", 1))

        inv_h = 1.0 / HQ
        R_M, R_MSQ, R_VAR, R_RSTD, R_A, R_B = (i * 512 for i in range(6))
        for (c0, w) in [(3, 512), (515, 512), (0, 3)]:
            for j in range(16):
                p.ev("scalar",
                     lambda e, j=j, c0=c0, w=w: e.activation(
                         sq_sb[j % 2][:, 0:w], x0_sb[:, j * TW + c0: j * TW + c0 + w], AF.Square),
                     waits=[("ld", LD_CONST), ("pe", max(0, p.c("pe") - 1)),
                            ("dv", p.c("dv"))],
                     inc=("ac", 1))
                p.ev("tensor",
                     lambda e, j=j, c0=c0, w=w: e.matmul(
                         pA[0:1, 0:w], onec_sb[:], x0_sb[:, j * TW + c0: j * TW + c0 + w],
                         start=(j == 0), stop=(j == 15)),
                     waits=([("ld", LD_CONST), ("dv", p.c("dv"))] if j == 0 else []), inc=None)
                p.ev("tensor",
                     lambda e, j=j, w=w: e.matmul(
                         pB[0:1, 0:w], onec_sb[:], sq_sb[j % 2][:, 0:w],
                         start=(j == 0), stop=(j == 15)),
                     waits=[("ac", p.c("ac"))], inc=("pe", 1))
            PE_STATS = p.c("pe")

            def rowstep(fn, waits):
                p.ev("vector", fn, waits=waits, inc=("dv", 1))
            rowstep(lambda e, w=w: e.tensor_scalar_mul(row_sb[:, R_M:R_M + w], pA[0:1, 0:w], inv_h),
                    [("pe", PE_STATS)])
            rowstep(lambda e, w=w: e.tensor_scalar_mul(row_sb[:, R_MSQ:R_MSQ + w], pB[0:1, 0:w], inv_h),
                    [("dv", p.c("dv"))])
            rowstep(lambda e, w=w: e.tensor_mul(row_sb[:, R_VAR:R_VAR + w], row_sb[:, R_M:R_M + w],
                                                row_sb[:, R_M:R_M + w]), [("dv", p.c("dv"))])
            rowstep(lambda e, w=w: e.tensor_sub(row_sb[:, R_VAR:R_VAR + w], row_sb[:, R_MSQ:R_MSQ + w],
                                                row_sb[:, R_VAR:R_VAR + w]), [("dv", p.c("dv"))])
            rowstep(lambda e, w=w: e.tensor_scalar_add(row_sb[:, R_VAR:R_VAR + w],
                                                       row_sb[:, R_VAR:R_VAR + w], 1e-5),
                    [("dv", p.c("dv"))])
            p.ev("scalar",
                 lambda e, w=w: e.activation(row_sb[:, R_RSTD:R_RSTD + w], row_sb[:, R_VAR:R_VAR + w], AF.Sqrt),
                 waits=[("dv", p.c("dv"))], inc=("ac", 1))
            rowstep(lambda e, w=w: e.reciprocal(row_sb[:, R_A:R_A + w], row_sb[:, R_RSTD:R_RSTD + w]),
                    [("ac", p.c("ac"))])
            rowstep(lambda e, w=w: e.tensor_mul(row_sb[:, R_B:R_B + w], row_sb[:, R_M:R_M + w],
                                                row_sb[:, R_A:R_A + w]), [("dv", p.c("dv"))])
            rowstep(lambda e, w=w: e.tensor_scalar_mul(row_sb[:, R_B:R_B + w], row_sb[:, R_B:R_B + w], -1.0),
                    [("dv", p.c("dv"))])
            DV_ROW = p.c("dv")
            p.ev("tensor",
                 lambda e, w=w: e.matmul(pC[:, 0:w], oner_sb[:], row_sb[:, R_A:R_A + w],
                                         start=True, stop=True),
                 waits=[("dv", DV_ROW)], inc=None)
            p.ev("tensor",
                 lambda e, w=w: e.matmul(pC[:, 512:512 + w], oner_sb[:], row_sb[:, R_B:R_B + w],
                                         start=True, stop=True),
                 waits=[], inc=("pe", 1))
            PE_BC = p.c("pe")
            for j in range(16):
                p.ev("vector",
                     lambda e, j=j, c0=c0, w=w: e.tensor_mul(
                         sq_sb[0][:, 0:w], x0_sb[:, j * TW + c0: j * TW + c0 + w], pC[:, 0:w]),
                     waits=[("pe", PE_BC), ("ac", p.c("ac"))], inc=("dv", 1))
                p.ev("vector",
                     lambda e, j=j, c0=c0, w=w: e.tensor_tensor(
                         xln_sb[:, j * TW + c0: j * TW + c0 + w],
                         sq_sb[0][:, 0:w], pC[:, 512:512 + w], ALU.add),
                     waits=[("dv", p.c("dv"))], inc=("dv", 1))
        DV_LN = p.c("dv")
        import os
        if os.environ.get("DEBUG_A"):
            dbg = nc.dram_tensor("dbg_xln", [128, 16 * TW], BF16, kind="ExternalOutput").ap()
            p.ev("gpsimd", lambda e: e.dma_start(out=dbg[:], in_=xln_sb[:]),
                 waits=[("dv", DV_LN)], inc=("so", 16))
            dbg2 = nc.dram_tensor("dbg_row", [1, 6 * 512], F32, kind="ExternalOutput").ap()
            p.ev("gpsimd", lambda e: e.dma_start(out=dbg2[:], in_=row_sb[:]),
                 waits=[("dv", DV_LN)], inc=("so", 16))

        wuse = {}
        widx = 0
        XGRP = [(0, 512), (512, 512), (1024, 3)]
        GGRP = [(3, 512, 0), (515, 512, 512)]

        for h in range(NH):
            for obl in range(16):
                ob = h * 16 + obl
                buf = wbufs[widx % 2]
                p.ev("sync",
                     lambda e, buf=buf, ob=ob: e.dma_start(
                         out=buf.rearrange("p (kb o) -> p kb o", kb=16),
                         in_=wx.rearrange("(kb p) o -> p kb o", p=128)[:, :, ob * 128:(ob + 1) * 128]),
                     waits=[("pe", wuse.get(widx - 2, 0))], inc=("ld", 16))
                LD_W = p.c("ld")
                p.ev("gpsimd", (lambda buf: lambda e: e.tensor_scalar(
                    buf.bitcast(I32), buf.bitcast(I32), 0x8000, None, ALU.add))(buf),
                    waits=[("ld", LD_W)], inc=("po", 1))
                PO_W = p.c("po")
                for gi, (xc0, wdt) in enumerate(XGRP):
                    pst = [pA, pB, pA][gi]
                    for kb in range(16):
                        p.ev("tensor",
                             lambda e, kb=kb, buf=buf, xc0=xc0, wdt=wdt, pst=pst: e.matmul(
                                 pst[:, 0:wdt],
                                 bf16_view(buf[:, kb * 128:(kb + 1) * 128]),
                                 xln_sb[:, kb * TW + xc0: kb * TW + xc0 + wdt],
                                 start=(kb == 0), stop=(kb == 15)),
                             waits=([("ld", LD_W), ("dv", p.c("dv")), ("po", PO_W)] if kb == 0 else []),
                             inc=(("pe", 1) if kb == 15 else None))
                    p.ev("vector",
                         lambda e, xc0=xc0, wdt=wdt, pst=pst: e.tensor_copy(
                             xin_sb[:, xc0:xc0 + wdt], pst[:, 0:wdt]),
                         waits=[("pe", p.c("pe")), ("po", p.c("po"))], inc=("dv", 1))
                wuse[widx] = p.c("pe")
                DV_XIN = p.c("dv")

                def conv_emit(e, ob=ob):
                    wc = lambda k: convw_sb[:, ob * 4 + k: ob * 4 + k + 1]
                    e.tensor_scalar(xcv_sb[:], xin_sb[:, 0:TPC], wc(0), convb_sb[:, ob:ob + 1],
                                    ALU.mult, ALU.add)
                    e.scalar_tensor_tensor(xcv_sb[:], xin_sb[:, 1:1 + TPC], wc(1), xcv_sb[:], ALU.mult, ALU.add)
                    e.scalar_tensor_tensor(xcv_sb[:], xin_sb[:, 2:2 + TPC], wc(2), xcv_sb[:], ALU.mult, ALU.add)
                    return e.scalar_tensor_tensor(xcv_sb[:], xin_sb[:, 3:3 + TPC], wc(3), xcv_sb[:], ALU.mult, ALU.add)
                p.ev("vector", conv_emit, waits=[("dv", DV_XIN), ("ac", p.c("ac"))], inc=("dv", 1))
                p.ev("scalar", lambda e: e.activation(xcb_sb[:], xcv_sb[:], AF.Silu),
                     waits=[("dv", p.c("dv")), ("pe", p.c("pe")), ("so", p.c("so"))], inc=("ac", 1))
                AC_XC = p.c("ac")
                p.ev("gpsimd",
                     lambda e, ob=ob: e.dma_start(out=xc_i[ob * 128:(ob + 1) * 128, :], in_=xcb_sb[:]),
                     waits=[("ac", AC_XC)], inc=("so", 16))
                for gi, (pc0, wdt) in enumerate([(0, 512), (512, 512)]):
                    p.ev("tensor",
                         lambda e, ob=ob, obl=obl, pc0=pc0, wdt=wdt: e.matmul(
                             pC[:, pc0:pc0 + wdt],
                             bf16_view(xpdt_sb[:, ob * RK:(ob + 1) * RK]),
                             xcb_sb[:, pc0:pc0 + wdt],
                             start=(obl == 0), stop=(obl == 15)),
                         waits=([("ac", AC_XC), ("po", PO_RTNC)] if gi == 0 else []), inc=None)
                    p.ev("tensor",
                         lambda e, ob=ob, obl=obl, pc0=pc0, wdt=wdt: e.matmul(
                             pD[0:32, pc0:pc0 + wdt],
                             bf16_view(xpbc_sb[:, ob * 32:(ob + 1) * 32]),
                             xcb_sb[:, pc0:pc0 + wdt],
                             start=(obl == 0), stop=(obl == 15)),
                         waits=[], inc=(("pe", 1) if gi == 1 else None))
                widx += 1
            PE_XPROJ = p.c("pe")

            p.ev("vector",
                 lambda e, h=h: e.tensor_copy(dtin_sb[:], pC[:]),
                 waits=[("pe", PE_XPROJ), ("so", p.c("so"))], inc=("dv", 1))
            p.ev("vector",
                 lambda e, h=h: e.tensor_copy(bc_sb[:], pD[0:32, :]),
                 waits=[("pe", PE_XPROJ), ("so", p.c("so"))], inc=("dv", 1))
            DV_EVBC = p.c("dv")
            p.ev("gpsimd",
                 lambda e, h=h: e.dma_start(out=bmat[h * NS:(h + 1) * NS, :],
                                            in_=bc_sb[0:16, :]),
                 waits=[("dv", DV_EVBC)], inc=("so", 16))
            p.ev("gpsimd",
                 lambda e, h=h: e.dma_start(out=cmat[h * NS:(h + 1) * NS, :],
                                            in_=bc_sb[16:32, :]),
                 waits=[("dv", DV_EVBC)], inc=("so", 16))

            for db in range(16):
                gdb = h * 16 + db
                for gi, (pc0, wdt) in enumerate([(0, 512), (512, 512)]):
                    pst = [pA, pB][gi]
                    p.ev("tensor",
                         lambda e, h=h, db=db, pc0=pc0, wdt=wdt, pst=pst: e.matmul(
                             pst[:, 0:wdt],
                             bf16_view(dtw_sb[:, h * HD + db * 128: h * HD + (db + 1) * 128]),
                             dtin_sb[:, pc0:pc0 + wdt],
                             start=True, stop=True),
                         waits=[("dv", p.c("dv")), ("ac", p.c("ac"))],
                         inc=("pe", 1))
                    PE_DT = p.c("pe")
                    p.ev("scalar",
                         lambda e, pst=pst, wdt=wdt, gdb=gdb: e.activation(
                             sp_sb[0][:, 0:wdt], pst[:, 0:wdt], AF.Abs, bias=dtb_sb[:, gdb:gdb + 1]),
                         waits=[("pe", PE_DT), ("dv", p.c("dv"))], inc=("ac", 1))
                    p.ev("scalar",
                         lambda e, wdt=wdt: e.activation(
                             sp_sb[1][:, 0:wdt], sp_sb[0][:, 0:wdt], AF.Exp, scale=negc_sb[:, 0:1]),
                         inc=("ac", 1))
                    p.ev("scalar",
                         lambda e, wdt=wdt: e.activation(
                             sp_sb[2][:, 0:wdt], sp_sb[1][:, 0:wdt], AF.Ln, bias=onecol_sb[:, 0:1]),
                         inc=("ac", 1))
                    p.ev("scalar",
                         lambda e, pst=pst, wdt=wdt, gdb=gdb: e.activation(
                             sp_sb[0][:, 0:wdt], pst[:, 0:wdt], AF.Relu, bias=dtb_sb[:, gdb:gdb + 1]),
                         inc=("ac", 1))
                    AC_SP = p.c("ac")
                    p.ev("vector",
                         lambda e, pc0=pc0, wdt=wdt: e.tensor_add(
                             dt_sb[:, pc0:pc0 + wdt], sp_sb[0][:, 0:wdt], sp_sb[2][:, 0:wdt]),
                         waits=[("ac", AC_SP), ("so", p.c("so"))], inc=("dv", 1))
                DV_DT = p.c("dv")
                p.ev("gpsimd",
                     lambda e, gdb=gdb: e.dma_start(out=dt_i[gdb * 128:(gdb + 1) * 128, :], in_=dt_sb[:]),
                     waits=[("dv", DV_DT)], inc=("so", 16))

            for obl in range(16):
                ob = h * 16 + obl
                buf = wbufs[widx % 2]
                p.ev("sync",
                     lambda e, buf=buf, ob=ob: e.dma_start(
                         out=buf.rearrange("p (kb o) -> p kb o", kb=16),
                         in_=wg.rearrange("(kb p) o -> p kb o", p=128)[:, :, ob * 128:(ob + 1) * 128]),
                     waits=[("pe", wuse.get(widx - 2, 0))], inc=("ld", 16))
                LD_W = p.c("ld")
                p.ev("gpsimd", (lambda buf: lambda e: e.tensor_scalar(
                    buf.bitcast(I32), buf.bitcast(I32), 0x8000, None, ALU.add))(buf),
                    waits=[("ld", LD_W)], inc=("po", 1))
                PO_W = p.c("po")
                for gi, (xc0, wdt, oc0) in enumerate(GGRP):
                    pst = [pA, pB][gi]
                    for kb in range(16):
                        p.ev("tensor",
                             lambda e, kb=kb, buf=buf, xc0=xc0, wdt=wdt, pst=pst: e.matmul(
                                 pst[:, 0:wdt],
                                 bf16_view(buf[:, kb * 128:(kb + 1) * 128]),
                                 xln_sb[:, kb * TW + xc0: kb * TW + xc0 + wdt],
                                 start=(kb == 0), stop=(kb == 15)),
                             waits=([("ld", LD_W), ("ac", p.c("ac")), ("po", PO_W)] if kb == 0 else []),
                             inc=(("pe", 1) if kb == 15 else None))
                    p.ev("scalar",
                         lambda e, wdt=wdt, oc0=oc0, pst=pst: e.activation(
                             xcb_sb[:, oc0:oc0 + wdt], pst[:, 0:wdt], AF.Silu),
                         waits=[("pe", p.c("pe")), ("so", p.c("so"))], inc=("ac", 1))
                wuse[widx] = p.c("pe")
                AC_G = p.c("ac")
                p.ev("gpsimd",
                     lambda e, ob=ob: e.dma_start(out=gate_o[ob * 128:(ob + 1) * 128, :],
                                                  in_=xcb_sb[:]),
                     waits=[("ac", AC_G)], inc=("so", 16))
                widx += 1

        p.ev("vector", lambda e: e.memset(zero_sb[:], 0.0),
             waits=[("ac", p.c("ac"))], inc=("dv", 1))
        for ch in range(2):
            tc0 = ch * 512
            for h in range(NH):
                p.ev("sync",
                     lambda e, h=h, tc0=tc0: e.dma_start(
                         out=dta_v.rearrange("p (blk t) -> p blk t", t=512),
                         in_=dt_i[h * HD:(h + 1) * HD, tc0:tc0 + 512].rearrange("(blk p) t -> p blk t", p=128)),
                     waits=[("so", p.c("so")), ("pe", p.c("pe"))], inc=("ld", 16))
                p.ev("sync",
                     lambda e, h=h, tc0=tc0: e.dma_start(
                         out=xca_v.rearrange("p (blk t) -> p blk t", t=512),
                         in_=xc_i[h * HD:(h + 1) * HD, tc0:tc0 + 512].rearrange("(blk p) t -> p blk t", p=128)),
                     inc=("ld", 16))
                LD_A2 = p.c("ld")

                for n in range(NS):
                    p.ev("sync",
                         lambda e, h=h, n=n, tc0=tc0: e.dma_start(
                             out=bflat_sb[:],
                             in_=bmat[h * NS + n: h * NS + n + 1, tc0:tc0 + 512]),
                         waits=[("pe", p.c("pe"))], inc=("ld", 16))
                    p.ev("tensor",
                         lambda e, n=n: e.matmul(pA[:], oner_sb[:], bflat_sb[:],
                                                 start=True, stop=True),
                         waits=[("ld", p.c("ld")), ("ac", p.c("ac")), ("dv", p.c("dv"))],
                         inc=("pe", 1))
                    p.ev("scalar",
                         lambda e, n=n: e.activation(brep_v[:, n * 512:(n + 1) * 512], pA[:], AF.Identity),
                         waits=[("pe", p.c("pe"))], inc=("ac", 1))
                    p.ev("sync",
                         lambda e, h=h, n=n, tc0=tc0: e.dma_start(
                             out=cflat_sb[:],
                             in_=cmat[h * NS + n: h * NS + n + 1, tc0:tc0 + 512]),
                         waits=[("pe", p.c("pe"))], inc=("ld", 16))
                    p.ev("tensor",
                         lambda e, n=n: e.matmul(pA[:], oner_sb[:], cflat_sb[:],
                                                 start=True, stop=True),
                         waits=[("ld", p.c("ld")), ("ac", p.c("ac"))], inc=("pe", 1))
                    p.ev("scalar",
                         lambda e, n=n: e.activation(crep_v[:, n * 512:(n + 1) * 512], pA[:], AF.Identity),
                         waits=[("pe", p.c("pe"))], inc=("ac", 1))
                AC_REP = p.c("ac")

                for db in range(16):
                    blk = h * 16 + db
                    dta = dta_v[:, db * 512:(db + 1) * 512]
                    xca = xca_v[:, db * 512:(db + 1) * 512]
                    p.ev("vector",
                         lambda e, dta=dta, blk=blk: e.tensor_tensor_scan(
                             cdt_sb[:], dta, zero_sb[:], cdtc_sb[:, blk:blk + 1], ALU.add, ALU.add),
                         waits=[("ld", LD_A2), ("so", p.c("so"))], inc=("dv", 1))
                    p.ev("vector",
                         lambda e, blk=blk: e.tensor_copy(cdtc_sb[:, blk:blk + 1], cdt_sb[:, 511:512]),
                         waits=[("dv", p.c("dv"))], inc=("dv", 1))
                    DV_CDT = p.c("dv")
                    p.ev("gpsimd",
                         lambda e, blk=blk, tc0=tc0: e.dma_start(
                             out=cdt_o[blk * 128:(blk + 1) * 128, tc0:tc0 + 512], in_=cdt_sb[:]),
                         waits=[("dv", DV_CDT)], inc=("so", 16))
                    p.ev("gpsimd",
                         lambda e, dta=dta, xca=xca: e.tensor_mul(dtx_sb[:], dta, xca),
                         waits=[("dv", DV_CDT)], inc=("po", 1))
                    PO_DTX = p.c("po")
                    for n in range(NS):
                        p.ev("scalar",
                             lambda e, dta=dta, blk=blk, n=n: e.activation(
                                 da_sb[:], dta, AF.Exp,
                                 scale=aneg_sb[:, blk * NS + n: blk * NS + n + 1]),
                             waits=[("ac", AC_REP), ("dv", p.c("dv")), ("ld", LD_A2)], inc=("ac", 1))
                        p.ev("gpsimd",
                             lambda e, n=n: e.tensor_mul(dbu_sb[:], dtx_sb[:],
                                                         brep_v[:, n * 512:(n + 1) * 512]),
                             waits=[("po", PO_DTX), ("dv", p.c("dv"))], inc=("po", 1))
                        p.ev("vector",
                             lambda e, n=n, blk=blk: e.tensor_tensor_scan(
                                 h_v[:, n * 512:(n + 1) * 512], da_sb[:], dbu_sb[:],
                                 hst_sb[:, blk * NS + n: blk * NS + n + 1], ALU.mult, ALU.add),
                             waits=[("ac", p.c("ac")), ("po", p.c("po")), ("pe", p.c("pe"))],
                             inc=("dv", 1))
                        p.ev("vector",
                             lambda e, n=n: e.tensor_mul(g_sb[:], crep_v[:, n * 512:(n + 1) * 512],
                                                         h_v[:, n * 512:(n + 1) * 512]),
                             waits=[("dv", p.c("dv"))], inc=("dv", 1))
                        p.ev("tensor",
                             lambda e, n=n: e.matmul(pB[:], idnb_sb[:], g_sb[:],
                                                     start=(n == 0), stop=(n == NS - 1)),
                             waits=[("dv", p.c("dv"))], inc=("pe", 1))
                    p.ev("vector",
                         lambda e, blk=blk: e.tensor_copy(
                             hst_sb[:, blk * NS:(blk + 1) * NS],
                             h_v.rearrange("p (n t) -> p n t", n=NS)[:, :, 511:512].squeeze(2)),
                         waits=[("pe", p.c("pe")), ("dv", p.c("dv"))], inc=("dv", 1))
                    p.ev("vector",
                         lambda e, xca=xca, blk=blk: e.scalar_tensor_tensor(
                             ya_sb[:], xca, dcol_sb[:, blk:blk + 1], pB[:], ALU.mult, ALU.add),
                         inc=("dv", 1))
                    DV_Y = p.c("dv")
                    p.ev("gpsimd",
                         lambda e, blk=blk, tc0=tc0: e.dma_start(
                             out=y_loc[blk * 128:(blk + 1) * 128, tc0:tc0 + 512], in_=ya_sb[:]),
                         waits=[("dv", DV_Y)], inc=("so", 16))

        p.ev("gpsimd",
             lambda e: e.dma_start(out=hfin[:], in_=hst_sb[:]),
             waits=[("dv", p.c("dv"))], inc=("so", 16))
        p.ev("gpsimd",
             lambda e: e.dma_start(out=cdtfin[:], in_=cdtc_sb[:]),
             waits=[("dv", p.c("dv"))], inc=("so", 16))
        p.ev("gpsimd", lambda e: None, waits=[("so", p.c("so"))], inc=None)

        with nc.Block() as block:
            p.emit(block, sems)
    return nc


def prep_A(ins):
    x0 = ins["L0_attn_hidden_states"]
    wxT = np.ascontiguousarray(ins["L0_attn_in_proj_w"][0::2, :].T)
    wgT = np.ascontiguousarray(ins["L0_attn_in_proj_w"][1::2, :].T)
    def blkmaj(a, inner):
        return np.ascontiguousarray(
            a.reshape(32, 128, inner).transpose(1, 0, 2).reshape(128, 32 * inner))
    convw = blkmaj(ins["L0_attn_conv_w"].reshape(IQ, 4), 4)
    convb = blkmaj(ins["L0_attn_conv_b"].reshape(IQ, 1), 1)
    xpw = ins["L0_attn_x_proj_weight"]
    xpw_dt = blkmaj(np.ascontiguousarray(xpw[:, :RK, :].transpose(0, 2, 1)).reshape(IQ, RK), RK)
    xpw_bc = blkmaj(np.ascontiguousarray(xpw[:, RK:, :].transpose(0, 2, 1)).reshape(IQ, 32), 32)
    dtwT = np.ascontiguousarray(
        ins["L0_attn_dt_proj_weight"].transpose(0, 2, 1).reshape(NH, RK, HD)
        .transpose(1, 0, 2).reshape(RK, NH * HD))
    dtb = blkmaj(ins["L0_attn_dt_proj_bias"].reshape(IQ, 1), 1)
    alog = blkmaj(np.ascontiguousarray(ins["L0_attn_A_log"].reshape(IQ, NS)), NS)
    dcol = blkmaj(ins["L0_attn_D"].reshape(IQ, 1), 1)
    onec = np.ones((128, 1), np.float32)
    oner = np.ones((1, 128), np.float32)
    idn = np.eye(128, dtype=np.float32)

    in_maps = []
    for c in range(8):
        b, q = c // 4, c % 4
        t0 = q * TPC
        xs = np.zeros((HQ, TW), np.float32)
        lo = max(0, t0 - 3)
        xs[:, 3 - (t0 - lo):] = x0[b, lo:t0 + TPC, :].T
        in_maps.append({
            "x0T": np.ascontiguousarray(xs), "wx": wxT, "wg": wgT,
            "convw": convw, "convb": convb, "xpw_dt": xpw_dt, "xpw_bc": xpw_bc,
            "dtw": dtwT, "dtb": dtb, "alog": alog, "dcol": dcol,
            "onec": onec, "oner": oner, "idn": idn,
        })
    return in_maps



def build_B(stage=6):
    nc = bass.Bass("TRN2", target_bir_lowering=False)
    IN = lambda n, s, dt=F32: nc.dram_tensor(n, s, dt, kind="ExternalInput").ap()
    OUT = lambda n, s, dt=F32: nc.dram_tensor(n, s, dt, kind="ExternalOutput").ap()

    y_loc = IN("y_loc", [IQ, TPC])
    cdt = IN("cdt", [IQ, TPC])
    cmat = IN("cmat", [NH * NS, TPC])
    gate = IN("gate", [IQ, TPC], BF16)
    hprev = IN("hprev", [3 * 128, 32 * NS])
    cprev = IN("cprev", [3 * 128, 32])
    alog = IN("alog", [128, 32 * NS])
    x0T = IN("x0T", [HQ, TPC])
    opw = IN("opw", [IQ, HQ])
    wgT = IN("wgT", [HQ, 8192])
    wuT = IN("wuT", [HQ, 8192])
    wdT = IN("wdT", [8192, HQ])
    wlnc = IN("wlnc", [128, 64])
    wq = IN("wq", [HQ, HQ])
    wk = IN("wk", [HQ, HQ])
    wv = IN("wv", [HQ, HQ])
    cosT = IN("cosT", [128, TPC])
    sinT = IN("sinT", [128, TPC])
    onec = IN("onec", [128, 1])
    oner = IN("oner", [1, 128])
    idn = IN("idn", [128, 128])

    x2o = OUT("x2o", [HQ, TPC])
    qO = OUT("qO", [HQ, TPC], BF16)
    kO = OUT("kO", [HQ, TPC], BF16)
    vO = OUT("vO", [TPC, HQ], BF16)
    x1dbg = OUT("x1dbg", [HQ, TPC]) if stage == 2 else None

    x1_i = nc.dram_tensor("x1_i", [HQ, TPC], F32, kind="Internal").ap()

    p = Prog()
    with ExitStack() as es:
        sb = lambda name, shape, dt=F32: es.enter_context(nc.sbuf_tensor(name, shape, dt))
        psalloc = lambda name, shape, dt=F32: es.enter_context(nc.psum_tensor(name, shape, dt))
        sems = {}
        for s in ["ld", "so", "pe", "ac", "dv", "po"]:
            sems[s] = es.enter_context(nc.semaphore(s))

        BH = sb("BH", [128, 64 * TPC], BF16)
        BHf = BH[:].bitcast(F32)
        qx_sb = sb("qx_sb", [128, 16 * TPC], BF16)
        wgs = sb("wgs", [128, 2048])
        wus = sb("wus", [128, 2048])
        qw_sb = sb("qw_sb", [128, 4096], BF16)
        ytile = BHf[:, 16384:16896]
        ctile = BHf[:, 16896:17408]
        ptile = BHf[:, 17408:17920]
        aneg_sbv = BHf[:, 17920:18432]
        h0_sbv = BHf[:, 18432:18944]
        g2t = BH[:, 37888:38400]
        gtile = BH[:, 38400:38912]
        scpair = sb("scpair", [128, 1024])
        sc1 = scpair[:, 0:512]
        sc2 = scpair[:, 512:1024]
        qs2 = scpair[:].bitcast(BF16)
        sc3 = sb("sc3", [128, 512], BF16)
        x0t_sb = sb("x0t_sb", [128, 512])
        sq2_sb = sb("sq2_sb", [128, 512])
        hp_sb = BHf[:, 0:512]
        pf_sb = BHf[:, 512:1024]
        cp_sb = sb("cp_sb", [128, 32])
        wln_sb = sb("wln_sb", [128, 64])
        onec_sb = sb("onec_sb", [128, 1])
        oner_sb = sb("oner_sb", [1, 128])
        idn_sb = sb("idn_sb", [128, 128])
        idnb_sb = sb("idnb_sb", [128, 128], BF16)
        cflat_sb = BHf[0:1, 19456:19968]
        row_sb = sb("row_sb", [1, 4 * 512])
        qxf = qx_sb[:].bitcast(F32)
        scl_sb = sb("scl_sb", [1, 16])
        thr_sb = sb("thr_sb", [128, 8])
        scbc_sb = sb("scbc_sb", [128, 4 * 512], BF16)
        sxc_sb = sb("sxc_sb", [128, 48])
        pk_sb = sb("pk_sb", [128, 16])
        pr_sb = sb("pr_sb", [16, 128])

        pA = psalloc("pA", [128, 512])
        pB = psalloc("pB", [128, 512])
        pC = psalloc("pC", [128, 1024])
        pD = psalloc("pD", [128, 1024])
        pE = psalloc("pE", [128, 128])

        for dst, src2 in [(onec_sb[:], onec[:]), (oner_sb[:], oner[:]), (idn_sb[:], idn[:]),
                          (aneg_sbv, alog[:]), (wln_sb[:], wlnc[:])]:
            p.ev("sync", (lambda d, s: lambda e: e.dma_start(out=d, in_=s))(dst, src2),
                 inc=("ld", 16))
        LD0 = p.c("ld")
        p.ev("scalar", lambda e: e.activation(aneg_sbv, aneg_sbv, AF.Exp),
             waits=[("ld", LD0)], inc=("ac", 1))
        p.ev("vector", lambda e: e.tensor_scalar_mul(aneg_sbv, aneg_sbv, -1.0),
             waits=[("ac", p.c("ac"))], inc=("dv", 1))
        p.ev("vector", lambda e: e.memset(h0_sbv, 0.0), inc=("dv", 1))
        p.ev("vector", lambda e: e.tensor_copy(idnb_sb[:], idn_sb[:]),
             waits=[("ld", LD0)], inc=("dv", 1))
        for j in range(3):
            p.ev("sync", (lambda j: lambda e: e.dma_start(
                out=hp_sb, in_=hprev[j * 128:(j + 1) * 128, :]))(j),
                waits=[("dv", p.c("dv"))], inc=("ld", 16))
            p.ev("sync", (lambda j: lambda e: e.dma_start(
                out=cp_sb[:], in_=cprev[j * 128:(j + 1) * 128, :]))(j), inc=("ld", 16))
            LDJ = p.c("ld")
            for blk in range(32):
                p.ev("scalar", (lambda blk: lambda e: e.activation(
                    pf_sb[:, blk * NS:(blk + 1) * NS], aneg_sbv[:, blk * NS:(blk + 1) * NS],
                    AF.Exp, scale=cp_sb[:, blk:blk + 1]))(blk),
                    waits=([("ld", LDJ), ("dv", p.c("dv"))] if blk == 0 else []),
                    inc=("ac", 1))
            p.ev("vector", lambda e: e.tensor_mul(h0_sbv, h0_sbv, pf_sb),
                 waits=[("ac", p.c("ac")), ("dv", p.c("dv"))], inc=("dv", 1))
            p.ev("vector", lambda e: e.tensor_add(h0_sbv, h0_sbv, hp_sb),
                 waits=[("dv", p.c("dv"))], inc=("dv", 1))
        DV_H0 = p.c("dv")

        strm_v = BHf[:, 0:8192]
        absv = BHf[:, 8192:16384]
        for m, wsrc in [(0, wgT), (1, wuT), (2, wdT)]:
            nrow, ncol = wsrc.shape
            rpt = (nrow // 128) // 16
            for t in range(16):
                p.ev("sync", (lambda wsrc, t, rpt, ncol: lambda e: e.dma_start(
                    out=strm_v.rearrange("p (r c) -> p r c", c=ncol),
                    in_=wsrc.rearrange("(r p) c -> p r c", p=128)[:, t * rpt:(t + 1) * rpt, :]))(wsrc, t, rpt, ncol),
                    waits=[("ac", p.c("ac"))], inc=("ld", 16))
                p.ev("scalar", lambda e: e.activation(absv, strm_v, AF.Abs),
                     waits=[("ld", p.c("ld"))], inc=("ac", 1))
                AC_ABS = p.c("ac")
                for cc in range(16):
                    p.ev("tensor", (lambda cc, t: lambda e: e.matmul(
                        pA[0:1, 0:512], onec_sb[:], absv[:, cc * 512:(cc + 1) * 512],
                        start=(t == 0 and cc == 0), stop=(t == 15 and cc == 15)))(cc, t),
                        waits=([("ac", AC_ABS)] if cc == 0 else []),
                        inc=(("pe", 1) if (t == 15 and cc == 15) else None))
            PE_SUM = p.c("pe")
            p.ev("vector", lambda e: e.reduce_sum(row_sb[:, 0:1], pA[0:1, 0:512], axis=AX.X),
                 waits=[("pe", PE_SUM), ("dv", p.c("dv"))], inc=("dv", 1))
            p.ev("vector", (lambda m: lambda e: e.tensor_scalar(
                scl_sb[:, m:m + 1], row_sb[:, 0:1], 1.0 / (HQ * 8192), None, ALU.mult))(m),
                waits=[("dv", p.c("dv"))], inc=("dv", 1))
            p.ev("vector", (lambda m: lambda e: e.tensor_scalar(
                scl_sb[:, 4 + m:5 + m], scl_sb[:, m:m + 1], 0.5, None, ALU.mult))(m),
                waits=[("dv", p.c("dv"))], inc=("dv", 1))
            p.ev("vector", (lambda m: lambda e: e.tensor_scalar(
                scl_sb[:, 8 + m:9 + m], scl_sb[:, 4 + m:5 + m], -1.0, None, ALU.mult))(m),
                waits=[("dv", p.c("dv"))], inc=("dv", 1))
            DVm = p.c("dv")
            p.ev("tensor", (lambda m: lambda e: e.matmul(
                pB[:, 0:1], oner_sb[:], scl_sb[:, 8 + m:9 + m], start=True, stop=True))(m),
                waits=[("dv", DVm), ("ac", p.c("ac"))], inc=("pe", 1))
            p.ev("scalar", (lambda m: lambda e: e.activation(
                thr_sb[:, 4 + m:5 + m], pB[:, 0:1], AF.Identity))(m),
                waits=[("pe", p.c("pe"))], inc=("ac", 1))
            p.ev("tensor", (lambda m: lambda e: e.matmul(
                pB[:, 0:1], oner_sb[:], scl_sb[:, 4 + m:5 + m], start=True, stop=True))(m),
                waits=[("ac", p.c("ac"))], inc=("pe", 1))
            p.ev("scalar", (lambda m: lambda e: e.activation(
                thr_sb[:, m:m + 1], pB[:, 0:1], AF.Identity))(m),
                waits=[("pe", p.c("pe"))], inc=("ac", 1))

        crep_v = BH[:, 0:16 * 512]
        xmix_v = BH[:, 16 * 512:48 * 512]
        for ch in range(2):
            tc0 = ch * 512
            for h in range(NH):
                for n in range(NS):
                    p.ev("sync", (lambda h, n, tc0: lambda e: e.dma_start(
                        out=cflat_sb, in_=cmat[h * NS + n: h * NS + n + 1, tc0:tc0 + 512]))(h, n, tc0),
                        waits=[("pe", p.c("pe"))], inc=("ld", 16))
                    p.ev("tensor", lambda e: e.matmul(pA[:], oner_sb[:], cflat_sb,
                                                      start=True, stop=True),
                         waits=[("ld", p.c("ld")), ("ac", p.c("ac")), ("dv", p.c("dv"))],
                         inc=("pe", 1))
                    p.ev("scalar", (lambda n: lambda e: e.activation(
                        crep_v[:, n * 512:(n + 1) * 512], pA[:], AF.Identity))(n),
                        waits=[("pe", p.c("pe"))], inc=("ac", 1))
                AC_CREP = p.c("ac")
                for db in range(16):
                    blk = h * 16 + db
                    p.ev("sync", (lambda blk, tc0: lambda e: e.dma_start(
                        out=ctile, in_=cdt[blk * 128:(blk + 1) * 128, tc0:tc0 + 512]))(blk, tc0),
                        waits=[("ac", p.c("ac")), ("dv", p.c("dv"))], inc=("ld", 16))
                    p.ev("sync", (lambda blk, tc0: lambda e: e.dma_start(
                        out=ytile, in_=y_loc[blk * 128:(blk + 1) * 128, tc0:tc0 + 512]))(blk, tc0),
                        inc=("ld", 16))
                    p.ev("sync", (lambda blk, tc0: lambda e: e.dma_start(
                        out=gtile, in_=gate[blk * 128:(blk + 1) * 128, tc0:tc0 + 512]))(blk, tc0),
                        inc=("ld", 16))
                    LD_T = p.c("ld")
                    for n in range(NS):
                        p.ev("scalar", (lambda blk, n: lambda e: e.activation(
                            ptile, ctile, AF.Exp,
                            scale=aneg_sbv[:, blk * NS + n: blk * NS + n + 1]))(blk, n),
                            waits=[("ld", LD_T), ("ac", AC_CREP), ("dv", p.c("dv"))],
                            inc=("ac", 1))
                        p.ev("vector", (lambda blk, n: lambda e: e.scalar_tensor_tensor(
                            g2t, ptile, h0_sbv[:, blk * NS + n: blk * NS + n + 1],
                            crep_v[:, n * 512:(n + 1) * 512], ALU.mult, ALU.mult))(blk, n),
                            waits=[("ac", p.c("ac")), ("dv", DV_H0), ("pe", p.c("pe"))],
                            inc=("dv", 1))
                        p.ev("tensor", (lambda n: lambda e: e.matmul(
                            pB[:], idnb_sb[:], g2t, start=(n == 0), stop=(n == NS - 1)))(n),
                            waits=[("dv", p.c("dv"))], inc=("pe", 1))
                    p.ev("vector", lambda e: e.tensor_add(sc1, ytile, pB[:]),
                         waits=[("pe", p.c("pe"))], inc=("dv", 1))
                    p.ev("vector", (lambda blk: lambda e: e.tensor_tensor(
                        xmix_v[:, blk * 512:(blk + 1) * 512], sc1, gtile, ALU.mult))(blk),
                        waits=[("dv", p.c("dv"))], inc=("dv", 1))
            DV_XMIX = p.c("dv")
            for ob in range(16):
                p.ev("sync", (lambda ob: lambda e: e.dma_start(
                    out=wgs[:].rearrange("p (kb o) -> p kb o", kb=16),
                    in_=opw.rearrange("(kb p) o -> p kb o", p=128)[:, 0:16, ob * 128:(ob + 1) * 128]))(ob),
                    waits=[("pe", p.c("pe"))], inc=("ld", 16))
                p.ev("sync", (lambda ob: lambda e: e.dma_start(
                    out=wus[:].rearrange("p (kb o) -> p kb o", kb=16),
                    in_=opw.rearrange("(kb p) o -> p kb o", p=128)[:, 16:32, ob * 128:(ob + 1) * 128]))(ob),
                    inc=("ld", 16))
                LD_OP = p.c("ld")
                p.ev("gpsimd", lambda e: e.tensor_scalar(
                    wgs[:].bitcast(I32), wgs[:].bitcast(I32), 0x8000, None, ALU.add),
                    waits=[("ld", LD_OP)], inc=("po", 1))
                p.ev("gpsimd", lambda e: e.tensor_scalar(
                    wus[:].bitcast(I32), wus[:].bitcast(I32), 0x8000, None, ALU.add),
                    inc=("po", 1))
                PO_OP = p.c("po")
                p.ev("sync", (lambda ob, tc0: lambda e: e.dma_start(
                    out=x0t_sb[:], in_=x0T[ob * 128:(ob + 1) * 128, tc0:tc0 + 512]))(ob, tc0),
                    inc=("ld", 16))
                LD_X0 = p.c("ld")
                for kb in range(32):
                    buf2 = wgs if kb < 16 else wus
                    kbl = kb % 16
                    p.ev("tensor", (lambda kb, kbl, buf2: lambda e: e.matmul(
                        pC[:, 0:512], bf16_view(buf2[:, kbl * 128:(kbl + 1) * 128]),
                        xmix_v[:, kb * 512:(kb + 1) * 512],
                        start=(kb == 0), stop=(kb == 31)))(kb, kbl, buf2),
                        waits=([("ld", LD_OP), ("dv", DV_XMIX), ("dv", p.c("dv")),
                                ("po", PO_OP)] if kb == 0 else []),
                        inc=(("pe", 1) if kb == 31 else None))
                p.ev("vector", lambda e: e.tensor_add(sc1, pC[:, 0:512], x0t_sb[:]),
                     waits=[("pe", p.c("pe")), ("ld", LD_X0), ("so", p.c("so"))], inc=("dv", 1))
                DV_X1 = p.c("dv")
                p.ev("gpsimd", (lambda ob, tc0: lambda e: e.dma_start(
                    out=x1_i[ob * 128:(ob + 1) * 128, tc0:tc0 + 512], in_=sc1))(ob, tc0),
                    waits=[("dv", DV_X1)], inc=("so", 16))
                if stage == 2:
                    p.ev("gpsimd", (lambda ob, tc0: lambda e: e.dma_start(
                        out=x1dbg[ob * 128:(ob + 1) * 128, tc0:tc0 + 512], in_=sc1))(ob, tc0),
                        inc=("so", 16))
        if stage == 2:
            p.ev("gpsimd", lambda e: None, waits=[("so", p.c("so"))], inc=None)
            with nc.Block() as block:
                p.emit(block, sems)
            return nc

        x1c_v = BHf[:, 0:1024]
        inv_h = 1.0 / HQ
        R_M, R_VAR, R_A, R_B = (i * 512 for i in range(4))
        p.ev("vector", lambda e: e.memset(sxc_sb[:], 0.0), waits=[("dv", p.c("dv"))], inc=("dv", 1))

        def ln_rows(w, pe_stats):
            def rowstep(fn, waits):
                p.ev("vector", fn, waits=waits, inc=("dv", 1))
            rowstep(lambda e: e.tensor_scalar_mul(row_sb[:, R_M:R_M + w], pA[0:1, 0:w], inv_h),
                    [("pe", pe_stats)])
            rowstep(lambda e: e.tensor_scalar_mul(row_sb[:, R_VAR:R_VAR + w], pB[0:1, 0:w], inv_h),
                    [("dv", p.c("dv"))])
            rowstep(lambda e: e.tensor_mul(row_sb[:, R_B:R_B + w], row_sb[:, R_M:R_M + w],
                                           row_sb[:, R_M:R_M + w]), [("dv", p.c("dv"))])
            rowstep(lambda e: e.tensor_sub(row_sb[:, R_VAR:R_VAR + w], row_sb[:, R_VAR:R_VAR + w],
                                           row_sb[:, R_B:R_B + w]), [("dv", p.c("dv"))])
            rowstep(lambda e: e.tensor_scalar_add(row_sb[:, R_VAR:R_VAR + w],
                                                  row_sb[:, R_VAR:R_VAR + w], 1e-5),
                    [("dv", p.c("dv"))])
            p.ev("scalar",
                 lambda e: e.activation(row_sb[:, R_VAR:R_VAR + w], row_sb[:, R_VAR:R_VAR + w],
                                        AF.Sqrt),
                 waits=[("dv", p.c("dv"))], inc=("ac", 1))
            rowstep(lambda e: e.reciprocal(row_sb[:, R_A:R_A + w], row_sb[:, R_VAR:R_VAR + w]),
                    [("ac", p.c("ac"))])
            rowstep(lambda e: e.tensor_mul(row_sb[:, R_B:R_B + w], row_sb[:, R_M:R_M + w],
                                           row_sb[:, R_A:R_A + w]), [("dv", p.c("dv"))])
            rowstep(lambda e: e.tensor_scalar_mul(row_sb[:, R_B:R_B + w], row_sb[:, R_B:R_B + w], -1.0),
                    [("dv", p.c("dv"))])

        for (c0, w) in [(0, 512), (512, 512)]:
            for j in range(16):
                p.ev("sync", (lambda j, c0, w: lambda e: e.dma_start(
                    out=x1c_v[:, 0:w], in_=x1_i[j * 128:(j + 1) * 128, c0:c0 + w]))(j, c0, w),
                    waits=[("so", p.c("so")), ("ac", p.c("ac")), ("pe", p.c("pe")),
                           ("dv", p.c("dv"))], inc=("ld", 16))
                LD_X1 = p.c("ld")
                p.ev("scalar", (lambda w: lambda e: e.activation(
                    sq2_sb[:, 0:w], x1c_v[:, 0:w], AF.Square))(w),
                    waits=[("ld", LD_X1), ("dv", p.c("dv"))], inc=("ac", 1))
                p.ev("tensor", (lambda j, w: lambda e: e.matmul(
                    pA[0:1, 0:w], onec_sb[:], x1c_v[:, 0:w],
                    start=(j == 0), stop=(j == 15)))(j, w),
                    waits=[("ld", LD_X1)], inc=None)
                p.ev("tensor", (lambda j, w: lambda e: e.matmul(
                    pB[0:1, 0:w], onec_sb[:], sq2_sb[:, 0:w],
                    start=(j == 0), stop=(j == 15)))(j, w),
                    waits=[("ac", p.c("ac"))], inc=("pe", 1))
            PE_STATS = p.c("pe")
            ln_rows(w, PE_STATS)
            DV_ROW = p.c("dv")
            p.ev("tensor",
                 lambda e, w=w: e.matmul(pC[:, 0:w], oner_sb[:], row_sb[:, R_A:R_A + w],
                                         start=True, stop=True),
                 waits=[("dv", DV_ROW)], inc=None)
            p.ev("tensor",
                 lambda e, w=w: e.matmul(pC[:, 512:512 + w], oner_sb[:], row_sb[:, R_B:R_B + w],
                                         start=True, stop=True),
                 waits=[], inc=("pe", 1))
            PE_BC = p.c("pe")
            for j in range(16):
                p.ev("sync", (lambda j, c0, w: lambda e: e.dma_start(
                    out=x1c_v[:, 0:w], in_=x1_i[j * 128:(j + 1) * 128, c0:c0 + w]))(j, c0, w),
                    waits=[("dv", p.c("dv")), ("pe", p.c("pe"))], inc=("ld", 16))
                LD_X1B = p.c("ld")
                p.ev("vector", (lambda w: lambda e: e.tensor_mul(
                    sq2_sb[:, 0:w], x1c_v[:, 0:w], pC[:, 0:w]))(w),
                    waits=[("ld", LD_X1B), ("pe", PE_BC), ("ac", p.c("ac"))], inc=("dv", 1))
                p.ev("vector", (lambda w: lambda e: e.tensor_tensor(
                    sc1[0:128, 0:w], sq2_sb[:, 0:w], pC[:, 512:512 + w], ALU.add))(w),
                    waits=[("dv", p.c("dv")), ("pe", p.c("pe"))], inc=("dv", 1))
                DV_XF = p.c("dv")
                p.ev("vector", (lambda j, c0, w: lambda e: e.tensor_copy(
                    qx_sb[:, j * TPC + c0: j * TPC + c0 + w], sc1[0:128, 0:w]))(j, c0, w),
                    waits=[("dv", DV_XF)], inc=("dv", 1))
                for tt in range(4):
                    gt = (c0 // 128) + tt
                    p.ev("tensor", (lambda tt: lambda e: e.transpose(
                        pE[:].bitcast(F32), sc1[:, tt * 128:(tt + 1) * 128], idn_sb[:]))(tt),
                        waits=[("dv", DV_XF), ("dv", p.c("dv"))], inc=("pe", 1))
                    p.ev("vector", lambda e: e.reduce_max(
                        sxc_sb[:, 32:33], pE[:], axis=AX.X, apply_absolute_value=True),
                        waits=[("pe", p.c("pe")), ("dv", p.c("dv"))], inc=("dv", 1))
                    p.ev("vector", (lambda gt: lambda e: e.tensor_max(
                        sxc_sb[:, gt:gt + 1], sxc_sb[:, gt:gt + 1], sxc_sb[:, 32:33]))(gt),
                        waits=[("dv", p.c("dv"))], inc=("dv", 1))
        def colstep(fn):
            p.ev("vector", fn, waits=[("dv", p.c("dv"))], inc=("dv", 1))
        colstep(lambda e: e.tensor_scalar_max(sxc_sb[:, 0:8], sxc_sb[:, 0:8], 1e-5))
        colstep(lambda e: e.tensor_scalar_mul(sxc_sb[:, 8:16], sxc_sb[:, 0:8], 1.0 / 127.0))
        colstep(lambda e: e.reciprocal(sxc_sb[:, 16:24], sxc_sb[:, 8:16]))
        colstep(lambda e: e.tensor_copy(pk_sb[:, 0:8], sxc_sb[:, 16:24]))
        colstep(lambda e: e.tensor_copy(pk_sb[:, 8:16], sxc_sb[:, 8:16]))
        DV_PK = p.c("dv")
        p.ev("tensor", lambda e: e.transpose(pE[0:16, :], pk_sb[:], idn_sb[:]),
             waits=[("dv", DV_PK)], inc=("pe", 1))
        p.ev("scalar", lambda e: e.activation(pr_sb[:], pE[0:16, :], AF.Identity),
             waits=[("pe", p.c("pe"))], inc=("ac", 1))
        AC_PR = p.c("ac")
        for g in range(8):
            p.ev("gpsimd", (lambda g: lambda e: e.dma_start(
                out=BHf[0:1, 2048 + g * 128:2048 + (g + 1) * 128],
                in_=pr_sb[g:g + 1, :]))(g),
                waits=([("ac", AC_PR)] if g == 0 else []), inc=("so", 16))
            p.ev("gpsimd", (lambda g: lambda e: e.dma_start(
                out=BHf[0:1, 3072 + g * 128:3072 + (g + 1) * 128],
                in_=pr_sb[8 + g:9 + g, :]))(g),
                inc=("so", 16))
        SO_SX = p.c("so")
        for ch in range(2):
            tc0 = ch * 512
            p.ev("tensor", (lambda tc0: lambda e: e.matmul(
                pC[:, 0:512], oner_sb[:], BHf[0:1, 2048 + tc0:2048 + tc0 + 512],
                start=True, stop=True))(tc0),
                waits=[("so", SO_SX), ("dv", p.c("dv")), ("ac", p.c("ac"))], inc=("pe", 1))
            PE_SX = p.c("pe")
            for j in range(16):
                p.ev("vector", (lambda j, tc0: lambda e: e.tensor_tensor(
                    sc1, qx_sb[:, j * TPC + tc0: j * TPC + tc0 + 512], pC[:, 0:512], ALU.mult))(j, tc0),
                    waits=[("pe", PE_SX), ("dv", p.c("dv"))], inc=("dv", 1))
                p.ev("vector", (lambda j, tc0: lambda e: e.tensor_scalar(
                    qx_sb[:, j * TPC + tc0: j * TPC + tc0 + 512], sc1, MAGIC, MAGIC,
                    ALU.add, ALU.subtract))(j, tc0),
                    waits=[("dv", p.c("dv"))], inc=("dv", 1))
        DV_QX = p.c("dv")

        p.ev("vector", lambda e: e.tensor_scalar(
            row_sb[:, 0:TPC], BHf[0:1, 3072:3072 + TPC], scl_sb[0:1, 4:5], None, ALU.mult),
            waits=[("dv", p.c("dv"))], inc=("dv", 1))
        p.ev("vector", lambda e: e.tensor_scalar(
            row_sb[:, TPC:2 * TPC], BHf[0:1, 3072:3072 + TPC], scl_sb[0:1, 5:6], None, ALU.mult),
            waits=[("dv", p.c("dv"))], inc=("dv", 1))
        DV_SC = p.c("dv")
        for i4, (r0) in enumerate([0, 512, TPC, TPC + 512]):
            p.ev("tensor", (lambda i4, r0: lambda e: e.matmul(
                pC[:, 0:512], oner_sb[:], row_sb[:, r0:r0 + 512], start=True, stop=True))(i4, r0),
                waits=[("dv", DV_SC), ("ac", p.c("ac"))], inc=("pe", 1))
            p.ev("scalar", (lambda i4: lambda e: e.activation(
                scbc_sb[:, i4 * 512:(i4 + 1) * 512], pC[:, 0:512], AF.Identity))(i4),
                waits=[("pe", p.c("pe"))], inc=("ac", 1))
        AC_SCBC = p.c("ac")

        h_v = BH[:]
        p.ev("vector", lambda e: e.memset(sxc_sb[:, 24:32], 0.0),
             waits=[("dv", p.c("dv"))], inc=("dv", 1))
        for iob in range(64):
            p.ev("sync", (lambda iob: lambda e: e.dma_start(
                out=wgs[:].rearrange("p (kb c) -> p kb c", kb=16),
                in_=wgT.rearrange("(kb p) c -> p kb c", p=128)[:, :, iob * 128:(iob + 1) * 128]))(iob),
                waits=[("pe", p.c("pe")), ("ac", p.c("ac"))], inc=("ld", 16))
            p.ev("sync", (lambda iob: lambda e: e.dma_start(
                out=wus[:].rearrange("p (kb c) -> p kb c", kb=16),
                in_=wuT.rearrange("(kb p) c -> p kb c", p=128)[:, :, iob * 128:(iob + 1) * 128]))(iob),
                inc=("ld", 16))
            LD_W = p.c("ld")
            p.ev("scalar", lambda e: e.activation(qw_sb[:, 0:2048], wgs[:], AF.Sign, bias=thr_sb[:, 4:5]),
                 waits=[("ld", LD_W), ("dv", p.c("dv")), ("pe", p.c("pe"))], inc=("ac", 1))
            p.ev("scalar", lambda e: e.activation(qs2, wgs[:], AF.Sign, bias=thr_sb[:, 0:1]),
                 inc=("ac", 1))
            p.ev("vector", lambda e: e.tensor_add(qw_sb[:, 0:2048], qw_sb[:, 0:2048], qs2),
                 waits=[("ac", p.c("ac"))], inc=("dv", 1))
            p.ev("scalar", lambda e: e.activation(qw_sb[:, 2048:4096], wus[:], AF.Sign, bias=thr_sb[:, 5:6]),
                 waits=[("dv", p.c("dv"))], inc=("ac", 1))
            p.ev("scalar", lambda e: e.activation(qs2, wus[:], AF.Sign, bias=thr_sb[:, 1:2]),
                 inc=("ac", 1))
            p.ev("vector", lambda e: e.tensor_add(qw_sb[:, 2048:4096], qw_sb[:, 2048:4096], qs2),
                 waits=[("ac", p.c("ac"))], inc=("dv", 1))
            DV_QW = p.c("dv")
            for kb in range(16):
                p.ev("tensor", (lambda kb: lambda e: e.matmul(
                    pA[:], qw_sb[:, kb * 128:(kb + 1) * 128],
                    qx_sb[:, kb * TPC:kb * TPC + 512],
                    start=(kb == 0), stop=(kb == 15)))(kb),
                    waits=([("dv", DV_QW), ("dv", DV_QX)] if kb == 0 else []), inc=None)
                p.ev("tensor", (lambda kb: lambda e: e.matmul(
                    pB[:], qw_sb[:, kb * 128:(kb + 1) * 128],
                    qx_sb[:, kb * TPC + 512:kb * TPC + 1024],
                    start=(kb == 0), stop=(kb == 15)))(kb), inc=None)
                p.ev("tensor", (lambda kb: lambda e: e.matmul(
                    pC[:, 0:512], qw_sb[:, 2048 + kb * 128:2048 + (kb + 1) * 128],
                    qx_sb[:, kb * TPC:kb * TPC + 512],
                    start=(kb == 0), stop=(kb == 15)))(kb), inc=None)
                p.ev("tensor", (lambda kb: lambda e: e.matmul(
                    pC[:, 512:1024], qw_sb[:, 2048 + kb * 128:2048 + (kb + 1) * 128],
                    qx_sb[:, kb * TPC + 512:kb * TPC + 1024],
                    start=(kb == 0), stop=(kb == 15)))(kb),
                    inc=(("pe", 1) if kb == 15 else None))
            PE_GU = p.c("pe")
            for c in range(2):
                gps = [pA, pB][c]
                ups_c0 = 512 * c
                p.ev("vector", (lambda gps, c: lambda e: e.tensor_tensor(
                    sc1, gps[:], scbc_sb[:, c * 512:(c + 1) * 512], ALU.mult))(gps, c),
                    waits=[("pe", PE_GU), ("dv", p.c("dv")), ("ac", AC_SCBC)], inc=("dv", 1))
                p.ev("scalar", lambda e: e.activation(sc3[:], sc1, AF.Silu),
                     waits=[("dv", p.c("dv"))], inc=("ac", 1))
                p.ev("vector", (lambda ups_c0, c: lambda e: e.tensor_tensor(
                    sc2, pC[:, ups_c0:ups_c0 + 512], scbc_sb[:, 1024 + c * 512:1024 + (c + 1) * 512],
                    ALU.mult))(ups_c0, c),
                    waits=[("dv", p.c("dv"))], inc=("dv", 1))
                p.ev("vector", (lambda iob, c: lambda e: e.tensor_tensor(
                    h_v[:, iob * TPC + c * 512: iob * TPC + (c + 1) * 512], sc3[:], sc2,
                    ALU.mult))(iob, c),
                    waits=[("ac", p.c("ac")), ("dv", p.c("dv"))], inc=("dv", 1))
                DV_H = p.c("dv")
                p.ev("scalar", (lambda iob, c: lambda e: e.activation(
                    sq2_sb[:], h_v[:, iob * TPC + c * 512: iob * TPC + (c + 1) * 512], AF.Square))(iob, c),
                    waits=[("dv", DV_H)], inc=("ac", 1))
                p.ev("tensor", (lambda iob, c: lambda e: e.matmul(
                    pD[0:1, c * 512:(c + 1) * 512], onec_sb[:], sq2_sb[:],
                    start=(iob == 0), stop=(iob == 63)))(iob, c),
                    waits=[("ac", p.c("ac"))], inc=("pe", 1))
                p.ev("vector", (lambda iob, c: lambda e: e.tensor_scalar(
                    sc1, h_v[:, iob * TPC + c * 512: iob * TPC + (c + 1) * 512],
                    wln_sb[:, iob:iob + 1], None, ALU.mult))(iob, c),
                    waits=[("dv", p.c("dv"))], inc=("dv", 1))
                DV_HW = p.c("dv")
                for tt in range(4):
                    gt = c * 4 + tt
                    p.ev("tensor", (lambda tt: lambda e: e.transpose(
                        pE[:].bitcast(F32), sc1[:, tt * 128:(tt + 1) * 128], idn_sb[:]))(tt),
                        waits=[("dv", DV_HW), ("dv", p.c("dv"))], inc=("pe", 1))
                    p.ev("vector", lambda e: e.reduce_max(
                        sxc_sb[:, 32:33], pE[:], axis=AX.X, apply_absolute_value=True),
                        waits=[("pe", p.c("pe")), ("dv", p.c("dv"))], inc=("dv", 1))
                    p.ev("vector", (lambda gt: lambda e: e.tensor_max(
                        sxc_sb[:, 24 + gt:25 + gt], sxc_sb[:, 24 + gt:25 + gt], sxc_sb[:, 32:33]))(gt),
                        waits=[("dv", p.c("dv"))], inc=("dv", 1))
        PE_SQ = p.c("pe")
        p.ev("vector", lambda e: e.tensor_scalar_mul(row_sb[:, 0:1024], pD[0:1, 0:1024], 1.0 / 8192.0),
             waits=[("pe", PE_SQ), ("dv", p.c("dv"))], inc=("dv", 1))
        p.ev("vector", lambda e: e.tensor_scalar_add(row_sb[:, 0:1024], row_sb[:, 0:1024], 1e-6),
             waits=[("dv", p.c("dv"))], inc=("dv", 1))
        p.ev("scalar", lambda e: e.activation(row_sb[:, 0:1024], row_sb[:, 0:1024], AF.Sqrt),
             waits=[("dv", p.c("dv"))], inc=("ac", 1))
        p.ev("vector", lambda e: e.reciprocal(row_sb[:, 1024:2048], row_sb[:, 0:1024]),
             waits=[("ac", p.c("ac"))], inc=("dv", 1))
        colstep(lambda e: e.tensor_scalar_max(sxc_sb[:, 24:32], sxc_sb[:, 24:32], 1e-5))
        colstep(lambda e: e.tensor_scalar_mul(sxc_sb[:, 40:48], sxc_sb[:, 24:32], 1.0 / 127.0))
        colstep(lambda e: e.reciprocal(sxc_sb[:, 32:40], sxc_sb[:, 40:48]))
        colstep(lambda e: e.tensor_copy(pk_sb[:, 0:8], sxc_sb[:, 32:40]))
        colstep(lambda e: e.tensor_copy(pk_sb[:, 8:16], sxc_sb[:, 40:48]))
        DV_PK2 = p.c("dv")
        p.ev("tensor", lambda e: e.transpose(pE[0:16, :], pk_sb[:], idn_sb[:]),
             waits=[("dv", DV_PK2)], inc=("pe", 1))
        p.ev("scalar", lambda e: e.activation(pr_sb[:], pE[0:16, :], AF.Identity),
             waits=[("pe", p.c("pe"))], inc=("ac", 1))
        AC_PR2 = p.c("ac")
        for g in range(8):
            p.ev("gpsimd", (lambda g: lambda e: e.dma_start(
                out=qxf[0:1, g * 128:(g + 1) * 128],
                in_=pr_sb[g:g + 1, :]))(g),
                waits=([("ac", AC_PR2), ("so", p.c("so")), ("pe", p.c("pe"))] if g == 0 else []),
                inc=("so", 16))
            p.ev("gpsimd", (lambda g: lambda e: e.dma_start(
                out=qxf[0:1, TPC + g * 128:TPC + (g + 1) * 128],
                in_=pr_sb[8 + g:9 + g, :]))(g),
                inc=("so", 16))
        SO_SH = p.c("so")
        p.ev("vector", lambda e: e.tensor_mul(row_sb[:, 0:TPC], row_sb[:, 1024:2048],
                                              qxf[0:1, 0:TPC]),
             waits=[("so", SO_SH), ("dv", p.c("dv"))], inc=("dv", 1))
        p.ev("vector", lambda e: e.tensor_scalar(
            row_sb[:, TPC:2 * TPC], qxf[0:1, TPC:2 * TPC], scl_sb[0:1, 6:7], None, ALU.mult),
            waits=[("dv", p.c("dv"))], inc=("dv", 1))
        DV_RS = p.c("dv")
        for i4, r0 in enumerate([0, 512, TPC, TPC + 512]):
            p.ev("tensor", (lambda r0: lambda e: e.matmul(
                pC[:, 0:512], oner_sb[:], row_sb[:, r0:r0 + 512], start=True, stop=True))(r0),
                waits=[("dv", DV_RS), ("ac", p.c("ac"))], inc=("pe", 1))
            p.ev("scalar", (lambda i4: lambda e: e.activation(
                scbc_sb[:, i4 * 512:(i4 + 1) * 512], pC[:, 0:512], AF.Identity))(i4),
                waits=[("pe", p.c("pe"))], inc=("ac", 1))
        AC_RSBC = p.c("ac")
        for iob in range(64):
            for c in range(2):
                p.ev("vector", (lambda iob, c: lambda e: e.scalar_tensor_tensor(
                    sc1, h_v[:, iob * TPC + c * 512: iob * TPC + (c + 1) * 512],
                    wln_sb[:, iob:iob + 1], scbc_sb[:, c * 512:(c + 1) * 512],
                    ALU.mult, ALU.mult))(iob, c),
                    waits=[("ac", AC_RSBC), ("dv", p.c("dv"))], inc=("dv", 1))
                p.ev("vector", (lambda iob, c: lambda e: e.tensor_scalar(
                    h_v[:, iob * TPC + c * 512: iob * TPC + (c + 1) * 512], sc1,
                    MAGIC, MAGIC, ALU.add, ALU.subtract))(iob, c),
                    waits=[("dv", p.c("dv"))], inc=("dv", 1))
        DV_QH = p.c("dv")

        for ob in range(16):
            for c in range(2):
                tc0 = c * 512
                for obq in range(4):
                    p.ev("sync", (lambda ob, obq: lambda e: e.dma_start(
                        out=wgs[:].rearrange("p (kb o) -> p kb o", kb=16),
                        in_=wdT.rearrange("(kb p) o -> p kb o", p=128)[:, obq * 16:(obq + 1) * 16,
                                                                      ob * 128:(ob + 1) * 128]))(ob, obq),
                        waits=[("pe", p.c("pe")), ("ac", p.c("ac"))], inc=("ld", 16))
                    LD_WD = p.c("ld")
                    p.ev("scalar", lambda e: e.activation(qw_sb[:, 0:2048], wgs[:], AF.Sign, bias=thr_sb[:, 6:7]),
                         waits=[("ld", LD_WD), ("dv", p.c("dv")), ("pe", p.c("pe"))], inc=("ac", 1))
                    p.ev("scalar", lambda e: e.activation(qs2, wgs[:], AF.Sign, bias=thr_sb[:, 2:3]),
                         inc=("ac", 1))
                    p.ev("vector", lambda e: e.tensor_add(qw_sb[:, 0:2048], qw_sb[:, 0:2048], qs2),
                         waits=[("ac", p.c("ac"))], inc=("dv", 1))
                    DV_QD = p.c("dv")
                    for kbl in range(16):
                        kb = obq * 16 + kbl
                        p.ev("tensor", (lambda kb, kbl, tc0: lambda e: e.matmul(
                            pA[:], qw_sb[:, kbl * 128:(kbl + 1) * 128],
                            h_v[:, kb * TPC + tc0: kb * TPC + tc0 + 512],
                            start=(kb == 0), stop=(kb == 63)))(kb, kbl, tc0),
                            waits=([("dv", DV_QD), ("dv", DV_QH)] if kbl == 0 else []),
                            inc=(("pe", 1) if kbl == 15 else None))
                PE_DN = p.c("pe")
                p.ev("sync", (lambda ob, tc0: lambda e: e.dma_start(
                    out=x0t_sb[:], in_=x1_i[ob * 128:(ob + 1) * 128, tc0:tc0 + 512]))(ob, tc0),
                    waits=[("dv", p.c("dv"))], inc=("ld", 16))
                LD_X1T = p.c("ld")
                p.ev("vector", (lambda c: lambda e: e.tensor_tensor(
                    sc1, pA[:], scbc_sb[:, 1024 + c * 512:1024 + (c + 1) * 512], ALU.mult))(c),
                    waits=[("pe", PE_DN), ("dv", p.c("dv")), ("ac", AC_RSBC)], inc=("dv", 1))
                p.ev("vector", lambda e: e.tensor_add(sc2, sc1, x0t_sb[:]),
                     waits=[("ld", LD_X1T), ("dv", p.c("dv")), ("so", p.c("so"))], inc=("dv", 1))
                DV_X2 = p.c("dv")
                p.ev("gpsimd", (lambda ob, tc0: lambda e: e.dma_start(
                    out=x2o[ob * 128:(ob + 1) * 128, tc0:tc0 + 512], in_=sc2))(ob, tc0),
                    waits=[("dv", DV_X2)], inc=("so", 16))

        x2c_v = BHf[:, 0:1024]
        cos_v = BHf[:, 1024:2048]
        sin_v = BHf[:, 2048:3072]
        rop_v = BHf[:, 3072:4096]
        qf_v = BHf[:, 4096:5120]
        vf_v = BHf[:, 5120:6144]
        a_v = BHf[:, 6144:7168]
        p.ev("sync", lambda e: e.dma_start(out=cos_v, in_=cosT[:]),
             waits=[("pe", p.c("pe")), ("dv", p.c("dv")), ("so", p.c("so"))], inc=("ld", 16))
        p.ev("sync", lambda e: e.dma_start(out=sin_v, in_=sinT[:]), inc=("ld", 16))
        LD_CS = p.c("ld")
        for (c0, w) in [(0, 512), (512, 512)]:
            for j in range(16):
                p.ev("sync", (lambda j, c0, w: lambda e: e.dma_start(
                    out=x2c_v[:, 0:w], in_=x2o[j * 128:(j + 1) * 128, c0:c0 + w]))(j, c0, w),
                    waits=[("so", p.c("so")), ("ac", p.c("ac")), ("pe", p.c("pe")),
                           ("dv", p.c("dv"))], inc=("ld", 16))
                LD_X2 = p.c("ld")
                p.ev("scalar", (lambda w: lambda e: e.activation(
                    sq2_sb[:, 0:w], x2c_v[:, 0:w], AF.Square))(w),
                    waits=[("ld", LD_X2), ("dv", p.c("dv"))], inc=("ac", 1))
                p.ev("tensor", (lambda j, w: lambda e: e.matmul(
                    pA[0:1, 0:w], onec_sb[:], x2c_v[:, 0:w],
                    start=(j == 0), stop=(j == 15)))(j, w),
                    waits=[("ld", LD_X2)], inc=None)
                p.ev("tensor", (lambda j, w: lambda e: e.matmul(
                    pB[0:1, 0:w], onec_sb[:], sq2_sb[:, 0:w],
                    start=(j == 0), stop=(j == 15)))(j, w),
                    waits=[("ac", p.c("ac"))], inc=("pe", 1))
            PE_ST2 = p.c("pe")
            ln_rows(w, PE_ST2)
            DV_ROW2 = p.c("dv")
            p.ev("tensor",
                 lambda e, w=w: e.matmul(pC[:, 0:w], oner_sb[:], row_sb[:, R_A:R_A + w],
                                         start=True, stop=True),
                 waits=[("dv", DV_ROW2)], inc=None)
            p.ev("tensor",
                 lambda e, w=w: e.matmul(pC[:, 512:512 + w], oner_sb[:], row_sb[:, R_B:R_B + w],
                                         start=True, stop=True),
                 waits=[], inc=("pe", 1))
            PE_BC2 = p.c("pe")
            for j in range(16):
                p.ev("sync", (lambda j, c0, w: lambda e: e.dma_start(
                    out=x2c_v[:, 0:w], in_=x2o[j * 128:(j + 1) * 128, c0:c0 + w]))(j, c0, w),
                    waits=[("dv", p.c("dv")), ("pe", p.c("pe"))], inc=("ld", 16))
                LD_X2B = p.c("ld")
                p.ev("vector", (lambda w: lambda e: e.tensor_mul(
                    sq2_sb[:, 0:w], x2c_v[:, 0:w], pC[:, 0:w]))(w),
                    waits=[("ld", LD_X2B), ("pe", PE_BC2)], inc=("dv", 1))
                p.ev("vector", (lambda j, c0, w: lambda e: e.tensor_tensor(
                    qx_sb[:, j * TPC + c0: j * TPC + c0 + w], sq2_sb[:, 0:w],
                    pC[:, 512:512 + w], ALU.add))(j, c0, w),
                    waits=[("dv", p.c("dv"))], inc=("dv", 1))
        DV_LN2 = p.c("dv")

        for wsrc, dst, mode in [(wq, qO, "rope"), (wk, kO, "rope"), (wv, vO, "v")]:
            for ob in range(16):
                p.ev("sync", (lambda wsrc, ob: lambda e: e.dma_start(
                    out=wgs[:].rearrange("p (kb o) -> p kb o", kb=16),
                    in_=wsrc.rearrange("(kb p) o -> p kb o", p=128)[:, :, ob * 128:(ob + 1) * 128]))(wsrc, ob),
                    waits=[("pe", p.c("pe"))], inc=("ld", 16))
                LD_QW = p.c("ld")
                p.ev("gpsimd", lambda e: e.tensor_scalar(
                    wgs[:].bitcast(I32), wgs[:].bitcast(I32), 0x8000, None, ALU.add),
                    waits=[("ld", LD_QW)], inc=("po", 1))
                PO_QW = p.c("po")
                for kb in range(16):
                    p.ev("tensor", (lambda kb: lambda e: e.matmul(
                        pC[:, 0:512], bf16_view(wgs[:, kb * 128:(kb + 1) * 128]),
                        qx_sb[:, kb * TPC:kb * TPC + 512],
                        start=(kb == 0), stop=(kb == 15)))(kb),
                        waits=([("ld", LD_QW), ("dv", DV_LN2), ("ac", p.c("ac")),
                                ("dv", p.c("dv")), ("po", PO_QW)] if kb == 0 else []), inc=None)
                    p.ev("tensor", (lambda kb: lambda e: e.matmul(
                        pC[:, 512:1024], bf16_view(wgs[:, kb * 128:(kb + 1) * 128]),
                        qx_sb[:, kb * TPC + 512:kb * TPC + 1024],
                        start=(kb == 0), stop=(kb == 15)))(kb),
                        inc=(("pe", 1) if kb == 15 else None))
                PE_QKV = p.c("pe")
                if mode == "rope":
                    p.ev("scalar", lambda e: e.activation(qf_v, pC[:, 0:1024], AF.Identity),
                         waits=[("pe", PE_QKV), ("ld", LD_CS), ("dv", p.c("dv"))], inc=("ac", 1))
                    AC_QF = p.c("ac")
                    p.ev("vector", lambda e: e.tensor_scalar_mul(rop_v[0:64, :], qf_v[64:128, :], -1.0),
                         waits=[("ac", AC_QF), ("dv", p.c("dv"))], inc=("dv", 1))
                    p.ev("vector", lambda e: e.tensor_copy(rop_v[64:128, :], qf_v[0:64, :]),
                         waits=[("ac", AC_QF)], inc=("dv", 1))
                    p.ev("vector", lambda e: e.tensor_mul(a_v, qf_v, cos_v),
                         waits=[("ac", AC_QF)], inc=("dv", 1))
                    p.ev("vector", lambda e: e.tensor_mul(rop_v, rop_v, sin_v),
                         waits=[("dv", p.c("dv"))], inc=("dv", 1))
                    p.ev("vector", lambda e: e.tensor_add(a_v, a_v, rop_v),
                        waits=[("dv", p.c("dv"))], inc=("dv", 1))
                    p.ev("vector", lambda e: e.tensor_copy(sc3[:, 0:512], a_v[:, 0:512]),
                         waits=[("dv", p.c("dv")), ("so", p.c("so"))], inc=("dv", 1))
                    p.ev("vector", lambda e: e.tensor_copy(gtile, a_v[:, 512:1024]),
                         inc=("dv", 1))
                    DV_RP = p.c("dv")
                    p.ev("gpsimd", (lambda dst, ob: lambda e: e.dma_start(
                        out=dst[ob * 128:(ob + 1) * 128, 0:512], in_=sc3[:, 0:512]))(dst, ob),
                        waits=[("dv", DV_RP)], inc=("so", 16))
                    p.ev("gpsimd", (lambda dst, ob: lambda e: e.dma_start(
                        out=dst[ob * 128:(ob + 1) * 128, 512:1024], in_=gtile))(dst, ob),
                        inc=("so", 16))
                else:
                    p.ev("scalar", lambda e: e.activation(vf_v, pC[:, 0:1024], AF.Identity),
                         waits=[("pe", PE_QKV), ("dv", p.c("dv"))], inc=("ac", 1))
                    AC_VF = p.c("ac")
                    for tt in range(8):
                        p.ev("tensor", (lambda tt: lambda e: e.transpose(
                            pE[:].bitcast(F32), vf_v[:, tt * 128:(tt + 1) * 128], idn_sb[:]))(tt),
                            waits=[("ac", AC_VF), ("so", p.c("so")), ("ac", p.c("ac"))], inc=("pe", 1))
                        p.ev("scalar", lambda e: e.activation(sc3[:, 0:128], pE[:], AF.Identity),
                             waits=[("pe", p.c("pe")), ("so", p.c("so"))], inc=("ac", 1))
                        AC_VT = p.c("ac")
                        p.ev("gpsimd", (lambda ob, tt: lambda e: e.dma_start(
                            out=vO[tt * 128:(tt + 1) * 128, ob * 128:(ob + 1) * 128],
                            in_=sc3[:, 0:128]))(ob, tt),
                            waits=[("ac", AC_VT)], inc=("so", 16))
        p.ev("gpsimd", lambda e: None, waits=[("so", p.c("so"))], inc=None)

        with nc.Block() as block:
            p.emit(block, sems)
    return nc


def prep_B(ins, ares):
    opwT = np.ascontiguousarray(ins["L0_attn_out_proj_w"].T)
    wgT = np.ascontiguousarray(ins["L0_ffn_w_gate"].T)
    wuT = np.ascontiguousarray(ins["L0_ffn_w_up"].T)
    wdT = np.ascontiguousarray(ins["L0_ffn_w_down"].T)
    wlnc = np.ascontiguousarray(ins["L0_ffn_ln_weight"].reshape(64, 128).T)
    wq = np.ascontiguousarray(ins["L1_attn_Wq"])
    wk = np.ascontiguousarray(ins["L1_attn_Wk"])
    wv = np.ascontiguousarray(ins["L1_attn_Wv"])
    alog = None
    x0 = ins["L0_attn_hidden_states"]

    def blkmaj(a, inner):
        return np.ascontiguousarray(
            a.reshape(32, 128, inner).transpose(1, 0, 2).reshape(128, 32 * inner))
    alog = blkmaj(np.ascontiguousarray(ins["L0_attn_A_log"].reshape(IQ, NS)), NS)

    hd = 128
    inv_freq = 1.0 / (10000.0 ** (np.arange(0, hd, 2, dtype=np.float64)[: hd // 2] / hd))
    pos = np.arange(SQ, dtype=np.float64)
    ang = pos[:, None] * inv_freq[None, :]
    ang = np.concatenate([ang, ang], axis=-1)
    cosf = np.cos(ang).astype(np.float32).T
    sinf = np.sin(ang).astype(np.float32).T
    onec = np.ones((128, 1), np.float32)
    oner = np.ones((1, 128), np.float32)
    idn = np.eye(128, dtype=np.float32)

    in_maps = []
    for c in range(8):
        b, q = c // 4, c % 4
        t0 = q * TPC
        hprev = np.zeros((3 * 128, 32 * NS), np.float32)
        cprev = np.zeros((3 * 128, 32), np.float32)
        for slot in range(3):
            j = q - 3 + slot
            if j >= 0:
                src = ares[4 * b + j]
                hprev[slot * 128:(slot + 1) * 128] = np.asarray(src["hfin"], np.float32)
                cprev[slot * 128:(slot + 1) * 128] = np.asarray(src["cdtfin"], np.float32)
        r = ares[c]
        in_maps.append({
            "y_loc": np.asarray(r["y_loc"], np.float32),
            "cdt": np.asarray(r["cdt_o"], np.float32),
            "cmat": np.asarray(r["cmat"], np.float32),
            "gate": np.asarray(r["gate_o"]),
            "hprev": hprev, "cprev": cprev, "alog": alog,
            "x0T": np.ascontiguousarray(x0[b, t0:t0 + TPC, :].T),
            "opw": opwT, "wgT": wgT, "wuT": wuT, "wdT": wdT, "wlnc": wlnc,
            "wq": wq, "wk": wk, "wv": wv,
            "cosT": np.ascontiguousarray(cosf[:, t0:t0 + TPC]),
            "sinT": np.ascontiguousarray(sinf[:, t0:t0 + TPC]),
            "onec": onec, "oner": oner, "idn": idn,
        })
    return in_maps



def build_C():
    nc = bass.Bass("TRN2", target_bir_lowering=False)
    IN = lambda n, s, dt=F32: nc.dram_tensor(n, s, dt, kind="ExternalInput").ap()
    OUT = lambda n, s, dt=F32: nc.dram_tensor(n, s, dt, kind="ExternalOutput").ap()

    x2T = IN("x2T", [HQ, TPC])
    qT = IN("qT", [HQ, TPC], BF16)
    kF = IN("kF", [HQ, SQ], BF16)
    vF = IN("vF", [SQ, HQ], BF16)
    wo = IN("wo", [HQ, HQ])
    wr = IN("wr", [HQ, 8])
    wgm = IN("wgm", [8 * HQ, 1408])
    wum = IN("wum", [8 * HQ, 1408])
    wdm = IN("wdm", [8 * 1408, HQ])
    maskS = IN("maskS", [128, 512])
    onec = IN("onec", [128, 1])
    oner = IN("oner", [1, 128])
    idn = IN("idn", [128, 128])
    outO = OUT("outO", [HQ, TPC])

    ISQ = 1.0 / float(np.sqrt(np.float32(128.0)))

    p = Prog()
    with ExitStack() as es:
        sb = lambda name, shape, dt=F32: es.enter_context(nc.sbuf_tensor(name, shape, dt))
        psalloc = lambda name, shape, dt=F32: es.enter_context(nc.psum_tensor(name, shape, dt))
        sems = {}
        for s in ["ld", "so", "pe", "ac", "dv", "po"]:
            sems[s] = es.enter_context(nc.semaphore(s))

        x3_sb = sb("x3_sb", [128, 16 * TPC])
        cxf_sb = sb("cxf_sb", [128, 16 * TPC], BF16)
        hf_v = cxf_sb[:, 0:11 * TPC]
        xln3_sb = sb("xln3_sb", [128, 16 * TPC], BF16)
        kv_sb = sb("kv_sb", [128, 2 * SQ + TPC], BF16)
        kv2_sb = sb("kv2_sb", [128, 2 * SQ + TPC], BF16)
        ctxt_sb = sb("ctxt_sb", [128, 2048])
        wst_sb = sb("wst_sb", [128, 2048])
        wst2_sb = sb("wst2_sb", [128, 2048])
        mask_sb = sb("mask_sb", [128, 512])
        ef_sb = sb("ef_sb", [128, 512])
        sf_sb = sb("sf_sb", [128, 512])
        et_sb = sb("et_sb", [128, 128], BF16)
        sc1 = sb("c_sc1", [128, 512])
        sc3 = sb("c_sc3", [128, 512], BF16)
        cbc_sb = sb("cbc_sb", [128, 1024], BF16)
        rsb_sb = sb("rsb_sb", [8, 1024])
        rT_sb = sb("rT_sb", [128, 64])
        cmb_sb = sb("cmb_sb", [128, 64])
        cmt_sb = sb("cmt_sb", [64, 128], BF16)
        crows_sb = kv_sb[0:1, 0:8 * TPC]
        onerb_sb = sb("onerb_sb", [1, 128], BF16)
        mcol_sb = sb("mcol_sb", [128, 16])
        row_sb = ctxt_sb[0:1, 0:2048]
        onec_sb = sb("c_onec", [128, 1])
        oner_sb = sb("c_oner", [1, 128])
        idn_sb = sb("c_idn", [128, 128])
        idnb_sb = sb("c_idnb", [128, 128], BF16)

        pA = psalloc("pA", [128, 512])
        pB = psalloc("pB", [128, 512])
        pC = psalloc("pC", [128, 1024])
        pD = psalloc("pD", [128, 1024])
        pE = psalloc("pE", [128, 128])

        M_RUN, M_PART, M_NEW, A_COL, L_RUN, L_PART, INV_C, BNEG, TMP, ISQC = range(10)
        mc = lambda i: mcol_sb[:, i:i + 1]

        for dst, src2 in [(onec_sb[:], onec[:]), (oner_sb[:], oner[:]), (idn_sb[:], idn[:]),
                          (mask_sb[:], maskS[:])]:
            p.ev("sync", (lambda d, s: lambda e: e.dma_start(out=d, in_=s))(dst, src2),
                 inc=("ld", 16))
        LD0 = p.c("ld")
        p.ev("vector", lambda e: e.tensor_copy(idnb_sb[:], idn_sb[:]),
             waits=[("ld", LD0)], inc=("dv", 1))
        p.ev("vector", lambda e: e.memset(mc(ISQC), ISQ), inc=("dv", 1))
        p.ev("vector", lambda e: e.tensor_copy(onerb_sb[:], oner_sb[:]),
             waits=[("ld", LD0)], inc=("dv", 1))

        for h in range(16):
            kvb = [kv_sb, kv2_sb][h % 2]
            kview = kvb[:, 0:SQ]
            vview = kvb[:, SQ:2 * SQ]
            qview = kvb[:, 2 * SQ:2 * SQ + TPC]
            p.ev("sync", (lambda h, kvb: lambda e: e.dma_start(
                out=kvb[:, 0:SQ], in_=kF[h * 128:(h + 1) * 128, :]))(h, kvb),
                waits=[("pe", p.c("pe"))], inc=("ld", 16))
            p.ev("sync", (lambda h, kvb: lambda e: e.dma_start(
                out=kvb[:, SQ:2 * SQ].rearrange("p (kc d) -> p kc d", d=128),
                in_=vF[:, h * 128:(h + 1) * 128].rearrange("(kc p) d -> p kc d", p=128)))(h, kvb),
                inc=("ld", 16))
            p.ev("sync", (lambda h, kvb: lambda e: e.dma_start(
                out=kvb[:, 2 * SQ:2 * SQ + TPC], in_=qT[h * 128:(h + 1) * 128, :]))(h, kvb),
                inc=("ld", 16))
            LD_KV = p.c("ld")
            for slot in range(8):
                p.ev("vector", lambda e: e.memset(mc(M_RUN), -1e30),
                     waits=[("dv", p.c("dv")), ("ac", p.c("ac"))], inc=("dv", 1))
                p.ev("vector", lambda e: e.memset(mc(L_RUN), 0.0),
                     waits=[("dv", p.c("dv"))], inc=("dv", 1))
                p.ev("vector", (lambda h: lambda e: e.memset(
                    ctxt_sb[:, h * 128:(h + 1) * 128], 0.0))(h),
                    waits=[("pe", p.c("pe")), ("dv", p.c("dv"))], inc=("dv", 1))
                for kb in range(slot + 1):
                    diag = (kb == slot)
                    p.ev("tensor", (lambda slot, kb, qview, kview: lambda e: e.matmul(
                        pA[:], qview[:, slot * 128:(slot + 1) * 128],
                        kview[:, kb * 512:(kb + 1) * 512], start=True, stop=True))(slot, kb, qview, kview),
                        waits=[("ld", LD_KV), ("ac", p.c("ac")), ("dv", p.c("dv"))],
                        inc=("pe", 1))
                    PE_S = p.c("pe")
                    if diag:
                        p.ev("vector", lambda e: e.tensor_add(sf_sb[:], pA[:], mask_sb[:]),
                             waits=[("pe", PE_S), ("dv", p.c("dv"))], inc=("dv", 1))
                        src = sf_sb[:]
                        SRC_RDY = [("dv", p.c("dv"))]
                    else:
                        src = pA[:]
                        SRC_RDY = [("pe", PE_S)]
                    p.ev("vector", (lambda src: lambda e: e.reduce_max(
                        mc(M_PART), src, axis=AX.X))(src),
                        waits=SRC_RDY + [("dv", p.c("dv"))], inc=("dv", 1))
                    p.ev("vector", lambda e: e.tensor_max(mc(M_NEW), mc(M_RUN), mc(M_PART)),
                         waits=[("dv", p.c("dv"))], inc=("dv", 1))
                    p.ev("vector", lambda e: e.tensor_sub(mc(TMP), mc(M_RUN), mc(M_NEW)),
                         waits=[("dv", p.c("dv"))], inc=("dv", 1))
                    p.ev("vector", lambda e: e.tensor_scalar_mul(mc(TMP), mc(TMP), ISQ),
                         waits=[("dv", p.c("dv"))], inc=("dv", 1))
                    p.ev("scalar", lambda e: e.activation(mc(A_COL), mc(TMP), AF.Exp),
                         waits=[("dv", p.c("dv"))], inc=("ac", 1))
                    p.ev("vector", lambda e: e.tensor_scalar_mul(mc(BNEG), mc(M_NEW), -ISQ),
                         waits=[("dv", p.c("dv"))], inc=("dv", 1))
                    DV_B = p.c("dv")
                    p.ev("scalar", (lambda src: lambda e: e.activation(
                        ef_sb[:], src, AF.Exp, bias=mc(BNEG), scale=mc(ISQC),
                        accum_out=mc(L_PART)))(src),
                        waits=[("dv", DV_B), ("ac", p.c("ac"))], inc=("ac", 1))
                    AC_E = p.c("ac")
                    p.ev("vector", lambda e: e.tensor_mul(mc(L_RUN), mc(L_RUN), mc(A_COL)),
                         waits=[("ac", AC_E), ("dv", p.c("dv"))], inc=("dv", 1))
                    p.ev("vector", lambda e: e.tensor_add(mc(L_RUN), mc(L_RUN), mc(L_PART)),
                         waits=[("dv", p.c("dv"))], inc=("dv", 1))
                    p.ev("vector", (lambda h: lambda e: e.tensor_scalar(
                        ctxt_sb[:, h * 128:(h + 1) * 128], ctxt_sb[:, h * 128:(h + 1) * 128],
                        mc(A_COL), None, ALU.mult))(h),
                        waits=[("dv", p.c("dv")), ("pe", p.c("pe"))], inc=("dv", 1))
                    DV_RS = p.c("dv")
                    for cc2 in range(4):
                        p.ev("tensor", (lambda cc2: lambda e: e.transpose(
                            pE[:], ef_sb[:, cc2 * 128:(cc2 + 1) * 128], idn_sb[:]))(cc2),
                            waits=[("ac", AC_E), ("ac", p.c("ac")), ("dv", p.c("dv"))],
                            inc=("pe", 1))
                        p.ev("scalar", lambda e: e.activation(et_sb[:], pE[:], AF.Identity),
                             waits=[("pe", p.c("pe"))], inc=("ac", 1))
                        AC_ET = p.c("ac")
                        p.ev("tensor", (lambda kb, cc2, vview: lambda e: e.matmul(
                            pB[:, 0:128], et_sb[:], vview[:, (kb * 4 + cc2) * 128:(kb * 4 + cc2 + 1) * 128],
                            start=(cc2 == 0), stop=(cc2 == 3)))(kb, cc2, vview),
                            waits=[("ac", AC_ET)], inc=("pe", 1))
                    p.ev("vector", (lambda h: lambda e: e.tensor_add(
                        ctxt_sb[:, h * 128:(h + 1) * 128], ctxt_sb[:, h * 128:(h + 1) * 128],
                        pB[:, 0:128]))(h),
                        waits=[("pe", p.c("pe")), ("dv", DV_RS)], inc=("dv", 1))
                p.ev("vector", lambda e: e.reciprocal(mc(INV_C), mc(L_RUN)),
                     waits=[("dv", p.c("dv"))], inc=("dv", 1))
                p.ev("vector", (lambda h: lambda e: e.tensor_scalar(
                    ctxt_sb[:, h * 128:(h + 1) * 128], ctxt_sb[:, h * 128:(h + 1) * 128],
                    mc(INV_C), None, ALU.mult))(h),
                    waits=[("dv", p.c("dv"))], inc=("dv", 1))
                DV_CTX = p.c("dv")
                p.ev("tensor", (lambda h: lambda e: e.transpose(
                    pE[:], ctxt_sb[:, h * 128:(h + 1) * 128], idn_sb[:]))(h),
                    waits=[("dv", DV_CTX), ("ac", p.c("ac"))], inc=("pe", 1))
                p.ev("scalar", (lambda h, slot: lambda e: e.activation(
                    cxf_sb[:, h * TPC + slot * 128: h * TPC + (slot + 1) * 128], pE[:],
                    AF.Identity))(h, slot),
                    waits=[("pe", p.c("pe"))], inc=("ac", 1))
        AC_ATT = p.c("ac")

        for ob in range(16):
            buf = [wst_sb, wst2_sb][ob % 2]
            p.ev("sync", (lambda ob, buf: lambda e: e.dma_start(
                out=buf[:].rearrange("p (kb o) -> p kb o", kb=16),
                in_=wo.rearrange("(kb p) o -> p kb o", p=128)[:, :, ob * 128:(ob + 1) * 128]))(ob, buf),
                waits=[("pe", p.c("pe"))], inc=("ld", 16))
            LD_WO = p.c("ld")
            p.ev("gpsimd", (lambda buf: lambda e: e.tensor_scalar(
                buf[:].bitcast(I32), buf[:].bitcast(I32), 0x8000, None, ALU.add))(buf),
                waits=[("ld", LD_WO)], inc=("po", 1))
            PO_WO = p.c("po")
            for c in range(2):
                for kb in range(16):
                    p.ev("tensor", (lambda buf, kb, c: lambda e: e.matmul(
                        pC[:, c * 512:(c + 1) * 512], bf16_view(buf[:, kb * 128:(kb + 1) * 128]),
                        cxf_sb[:, kb * TPC + c * 512: kb * TPC + c * 512 + 512],
                        start=(kb == 0), stop=(kb == 15)))(buf, kb, c),
                        waits=([("po", PO_WO), ("ac", AC_ATT), ("dv", p.c("dv"))] if kb == 0 else []),
                        inc=(("pe", 1) if kb == 15 else None))
            PE_WO = p.c("pe")
            p.ev("sync", (lambda ob: lambda e: e.dma_start(
                out=sc1[:], in_=x2T[ob * 128:(ob + 1) * 128, 0:512]))(ob),
                waits=[("dv", p.c("dv")), ("pe", p.c("pe"))], inc=("ld", 16))
            p.ev("sync", (lambda ob: lambda e: e.dma_start(
                out=ef_sb[:], in_=x2T[ob * 128:(ob + 1) * 128, 512:1024]))(ob),
                inc=("ld", 16))
            LD_X2 = p.c("ld")
            p.ev("vector", (lambda ob: lambda e: e.tensor_add(
                x3_sb[:, ob * TPC: ob * TPC + 512], pC[:, 0:512], sc1[:]))(ob),
                waits=[("pe", PE_WO), ("ld", LD_X2)], inc=("dv", 1))
            p.ev("vector", (lambda ob: lambda e: e.tensor_add(
                x3_sb[:, ob * TPC + 512: ob * TPC + 1024], pC[:, 512:1024], ef_sb[:]))(ob),
                waits=[("dv", p.c("dv"))], inc=("dv", 1))
        DV_X3 = p.c("dv")

        inv_h = 1.0 / HQ
        R_M, R_VAR, R_A, R_B = (i * 512 for i in range(4))
        for (c0, w) in [(0, 512), (512, 512)]:
            for j in range(16):
                p.ev("scalar", (lambda j, c0, w: lambda e: e.activation(
                    sc1[:, 0:w], x3_sb[:, j * TPC + c0: j * TPC + c0 + w], AF.Square))(j, c0, w),
                    waits=[("dv", DV_X3), ("dv", p.c("dv")), ("pe", max(0, p.c("pe") - 1))],
                    inc=("ac", 1))
                p.ev("tensor", (lambda j, c0, w: lambda e: e.matmul(
                    pA[0:1, 0:w], onec_sb[:], x3_sb[:, j * TPC + c0: j * TPC + c0 + w],
                    start=(j == 0), stop=(j == 15)))(j, c0, w),
                    waits=[("dv", DV_X3)], inc=None)
                p.ev("tensor", (lambda j, w: lambda e: e.matmul(
                    pB[0:1, 0:w], onec_sb[:], sc1[:, 0:w],
                    start=(j == 0), stop=(j == 15)))(j, w),
                    waits=[("ac", p.c("ac"))], inc=("pe", 1))
            PE_ST = p.c("pe")

            def rowstep(fn, waits):
                p.ev("vector", fn, waits=waits, inc=("dv", 1))
            rowstep(lambda e, w=w: e.tensor_scalar_mul(row_sb[:, R_M:R_M + w], pA[0:1, 0:w], inv_h),
                    [("pe", PE_ST)])
            rowstep(lambda e, w=w: e.tensor_scalar_mul(row_sb[:, R_VAR:R_VAR + w], pB[0:1, 0:w], inv_h),
                    [("dv", p.c("dv"))])
            rowstep(lambda e, w=w: e.tensor_mul(row_sb[:, R_B:R_B + w], row_sb[:, R_M:R_M + w],
                                                row_sb[:, R_M:R_M + w]), [("dv", p.c("dv"))])
            rowstep(lambda e, w=w: e.tensor_sub(row_sb[:, R_VAR:R_VAR + w], row_sb[:, R_VAR:R_VAR + w],
                                                row_sb[:, R_B:R_B + w]), [("dv", p.c("dv"))])
            rowstep(lambda e, w=w: e.tensor_scalar_add(row_sb[:, R_VAR:R_VAR + w],
                                                       row_sb[:, R_VAR:R_VAR + w], 1e-5),
                    [("dv", p.c("dv"))])
            p.ev("scalar",
                 lambda e, w=w: e.activation(row_sb[:, R_VAR:R_VAR + w], row_sb[:, R_VAR:R_VAR + w],
                                             AF.Sqrt),
                 waits=[("dv", p.c("dv"))], inc=("ac", 1))
            rowstep(lambda e, w=w: e.reciprocal(row_sb[:, R_A:R_A + w], row_sb[:, R_VAR:R_VAR + w]),
                    [("ac", p.c("ac"))])
            rowstep(lambda e, w=w: e.tensor_mul(row_sb[:, R_B:R_B + w], row_sb[:, R_M:R_M + w],
                                                row_sb[:, R_A:R_A + w]), [("dv", p.c("dv"))])
            rowstep(lambda e, w=w: e.tensor_scalar_mul(row_sb[:, R_B:R_B + w], row_sb[:, R_B:R_B + w],
                                                       -1.0), [("dv", p.c("dv"))])
            DV_ROW = p.c("dv")
            p.ev("tensor",
                 lambda e, w=w: e.matmul(pC[:, 0:w], oner_sb[:], row_sb[:, R_A:R_A + w],
                                         start=True, stop=True),
                 waits=[("dv", DV_ROW)], inc=None)
            p.ev("tensor",
                 lambda e, w=w: e.matmul(pC[:, 512:512 + w], oner_sb[:], row_sb[:, R_B:R_B + w],
                                         start=True, stop=True),
                 waits=[], inc=("pe", 1))
            PE_BC = p.c("pe")
            for j in range(16):
                p.ev("vector", (lambda j, c0, w: lambda e: e.tensor_mul(
                    sc1[:, 0:w], x3_sb[:, j * TPC + c0: j * TPC + c0 + w], pC[:, 0:w]))(j, c0, w),
                    waits=[("pe", PE_BC), ("ac", p.c("ac")), ("dv", p.c("dv"))], inc=("dv", 1))
                p.ev("vector", (lambda j, c0, w: lambda e: e.tensor_tensor(
                    xln3_sb[:, j * TPC + c0: j * TPC + c0 + w], sc1[:, 0:w],
                    pC[:, 512:512 + w], ALU.add))(j, c0, w),
                    waits=[("dv", p.c("dv"))], inc=("dv", 1))
        DV_LN3 = p.c("dv")

        p.ev("sync", lambda e: e.dma_start(out=wst_sb[:, 0:128].rearrange("p (kb r) -> p kb r", r=8),
                                           in_=wr.rearrange("(kb p) r -> p kb r", p=128)),
             waits=[("pe", p.c("pe"))], inc=("ld", 16))
        LD_WR = p.c("ld")
        p.ev("gpsimd", lambda e: e.tensor_scalar(
            wst_sb[:, 0:128].bitcast(I32), wst_sb[:, 0:128].bitcast(I32), 0x8000, None, ALU.add),
            waits=[("ld", LD_WR)], inc=("po", 1))
        PO_WR = p.c("po")
        for c in range(2):
            for kb in range(16):
                p.ev("tensor", (lambda kb, c: lambda e: e.matmul(
                    pD[0:8, c * 512:(c + 1) * 512], bf16_view(wst_sb[:, kb * 8:(kb + 1) * 8]),
                    xln3_sb[:, kb * TPC + c * 512: kb * TPC + c * 512 + 512],
                    start=(kb == 0), stop=(kb == 15)))(kb, c),
                    waits=([("po", PO_WR), ("dv", DV_LN3)] if kb == 0 else []),
                    inc=(("pe", 1) if kb == 15 else None))
        PE_R = p.c("pe")
        p.ev("scalar", lambda e: e.activation(rsb_sb[:], pD[0:8, :], AF.Identity),
             waits=[("pe", PE_R)], inc=("ac", 1))
        AC_R = p.c("ac")
        for s in range(8):
            p.ev("tensor", (lambda s: lambda e: e.transpose(
                pE[:, 0:8], rsb_sb[0:8, s * 128:(s + 1) * 128], idn_sb[0:8, 0:8]))(s),
                waits=[("ac", AC_R), ("ac", p.c("ac"))], inc=("pe", 1))
            p.ev("scalar", (lambda s: lambda e: e.activation(
                rT_sb[:, s * 8:(s + 1) * 8], pE[:, 0:8], AF.Identity))(s),
                waits=[("pe", p.c("pe"))], inc=("ac", 1))
        AC_RT = p.c("ac")
        for s in range(8):
            r8 = rT_sb[:, s * 8:(s + 1) * 8]

            def step(fn, waits=None):
                p.ev("vector", fn, waits=(waits or [("dv", p.c("dv"))]), inc=("dv", 1))
            step(lambda e, r8=r8: e.reduce_max(mc(0), r8, axis=AX.X),
                 [("ac", AC_RT), ("dv", p.c("dv"))])
            step(lambda e, r8=r8, s=s: e.tensor_scalar(cmb_sb[:, s * 8:(s + 1) * 8],
                                                       r8, mc(0), None, ALU.is_equal))
            step(lambda e, r8=r8, s=s: e.tensor_scalar(sc1[:, 0:8], cmb_sb[:, s * 8:(s + 1) * 8],
                                                       1e30, None, ALU.mult))
            step(lambda e, r8=r8: e.tensor_sub(sc1[:, 8:16], r8, sc1[:, 0:8]))
            step(lambda e: e.reduce_max(mc(1), sc1[:, 8:16], axis=AX.X))
            step(lambda e: e.tensor_sub(mc(2), mc(0), mc(1)))
            p.ev("scalar", lambda e: e.activation(mc(3), mc(2), AF.Sigmoid),
                 waits=[("dv", p.c("dv"))], inc=("ac", 1))
            step(lambda e: e.tensor_scalar(mc(4), mc(3), -1.0, 1.0, ALU.mult, ALU.add),
                 [("ac", p.c("ac"))])
            step(lambda e, r8=r8: e.tensor_scalar(sc1[:, 0:8], r8, mc(1), None, ALU.is_equal))
            step(lambda e, s=s: e.tensor_scalar(cmb_sb[:, s * 8:(s + 1) * 8],
                                                cmb_sb[:, s * 8:(s + 1) * 8], mc(3), None, ALU.mult))
            step(lambda e, s=s: e.scalar_tensor_tensor(cmb_sb[:, s * 8:(s + 1) * 8], sc1[:, 0:8],
                                                       mc(4), cmb_sb[:, s * 8:(s + 1) * 8],
                                                       ALU.mult, ALU.add))
        DV_CMB = p.c("dv")
        p.ev("tensor", lambda e: e.transpose(pE[0:64, :], cmb_sb[:], idn_sb[:]),
             waits=[("dv", DV_CMB), ("ac", p.c("ac"))], inc=("pe", 1))
        p.ev("scalar", lambda e: e.activation(cmt_sb[:], pE[0:64, :], AF.Identity),
             waits=[("pe", p.c("pe"))], inc=("ac", 1))
        AC_CMT = p.c("ac")
        for e8 in range(8):
            for s in range(8):
                p.ev("gpsimd", (lambda e8, s: lambda e: e.dma_start(
                    out=crows_sb[0:1, e8 * TPC + s * 128: e8 * TPC + (s + 1) * 128],
                    in_=cmt_sb[s * 8 + e8: s * 8 + e8 + 1, :]))(e8, s),
                    waits=([("ac", AC_CMT)] if (e8 == 0 and s == 0) else []), inc=("so", 16))
        SO_CROW = p.c("so")

        for e8 in range(8):
            for c in range(2):
                p.ev("tensor", (lambda e8, c: lambda e: e.matmul(
                    pC[:, 0:512], onerb_sb[:], crows_sb[0:1, e8 * TPC + c * 512: e8 * TPC + (c + 1) * 512],
                    start=True, stop=True))(e8, c),
                    waits=[("so", SO_CROW), ("dv", p.c("dv")), ("ac", p.c("ac")),
                           ("pe", p.c("pe"))], inc=("pe", 1))
                p.ev("scalar", (lambda c: lambda e: e.activation(
                    cbc_sb[:, c * 512:(c + 1) * 512], pC[:, 0:512], AF.Identity))(c),
                    waits=[("pe", p.c("pe"))], inc=("ac", 1))
            AC_CBC = p.c("ac")
            for iob in range(11):
                p.ev("sync", (lambda e8, iob: lambda e: e.dma_start(
                    out=wst_sb[:].rearrange("p (kb o) -> p kb o", kb=16),
                    in_=wgm.rearrange("(ee kb p) o -> p ee kb o", ee=8, p=128)[:, e8, :, iob * 128:(iob + 1) * 128]))(e8, iob),
                    waits=[("pe", p.c("pe"))], inc=("ld", 16))
                p.ev("sync", (lambda e8, iob: lambda e: e.dma_start(
                    out=wst2_sb[:].rearrange("p (kb o) -> p kb o", kb=16),
                    in_=wum.rearrange("(ee kb p) o -> p ee kb o", ee=8, p=128)[:, e8, :, iob * 128:(iob + 1) * 128]))(e8, iob),
                    inc=("ld", 16))
                LD_GU = p.c("ld")
                p.ev("gpsimd", lambda e: e.tensor_scalar(
                    wst_sb[:].bitcast(I32), wst_sb[:].bitcast(I32), 0x8000, None, ALU.add),
                    waits=[("ld", LD_GU)], inc=("po", 1))
                p.ev("gpsimd", lambda e: e.tensor_scalar(
                    wst2_sb[:].bitcast(I32), wst2_sb[:].bitcast(I32), 0x8000, None, ALU.add),
                    inc=("po", 1))
                PO_GU = p.c("po")
                for kb in range(16):
                    p.ev("tensor", (lambda kb: lambda e: e.matmul(
                        pA[:], bf16_view(wst_sb[:, kb * 128:(kb + 1) * 128]),
                        xln3_sb[:, kb * TPC:kb * TPC + 512],
                        start=(kb == 0), stop=(kb == 15)))(kb),
                        waits=([("po", PO_GU), ("dv", p.c("dv"))] if kb == 0 else []), inc=None)
                    p.ev("tensor", (lambda kb: lambda e: e.matmul(
                        pB[:], bf16_view(wst_sb[:, kb * 128:(kb + 1) * 128]),
                        xln3_sb[:, kb * TPC + 512:kb * TPC + 1024],
                        start=(kb == 0), stop=(kb == 15)))(kb), inc=None)
                    p.ev("tensor", (lambda kb: lambda e: e.matmul(
                        pC[:, 0:512], bf16_view(wst2_sb[:, kb * 128:(kb + 1) * 128]),
                        xln3_sb[:, kb * TPC:kb * TPC + 512],
                        start=(kb == 0), stop=(kb == 15)))(kb), inc=None)
                    p.ev("tensor", (lambda kb: lambda e: e.matmul(
                        pC[:, 512:1024], bf16_view(wst2_sb[:, kb * 128:(kb + 1) * 128]),
                        xln3_sb[:, kb * TPC + 512:kb * TPC + 1024],
                        start=(kb == 0), stop=(kb == 15)))(kb),
                        inc=(("pe", 1) if kb == 15 else None))
                PE_GU = p.c("pe")
                for c in range(2):
                    gps = [pA, pB][c]
                    p.ev("scalar", (lambda gps: lambda e: e.activation(sc3[:], gps[:], AF.Silu))(gps),
                         waits=[("pe", PE_GU), ("ac", p.c("ac")), ("dv", p.c("dv"))], inc=("ac", 1))
                    p.ev("vector", (lambda iob, c: lambda e: e.tensor_tensor(
                        hf_v[:, iob * TPC + c * 512: iob * TPC + (c + 1) * 512], sc3[:],
                        pC[:, c * 512:(c + 1) * 512], ALU.mult))(iob, c),
                        waits=[("ac", p.c("ac"))], inc=("dv", 1))
            DV_H = p.c("dv")
            for ob in range(16):
                p.ev("sync", (lambda e8, ob: lambda e: e.dma_start(
                    out=wst_sb[:, 0:11 * 128].rearrange("p (kb o) -> p kb o", kb=11),
                    in_=wdm.rearrange("(ee kb p) o -> p ee kb o", ee=8, p=128)[:, e8, :, ob * 128:(ob + 1) * 128]))(e8, ob),
                    waits=[("pe", p.c("pe"))], inc=("ld", 16))
                LD_D = p.c("ld")
                p.ev("gpsimd", lambda e: e.tensor_scalar(
                    wst_sb[:, 0:11 * 128].bitcast(I32), wst_sb[:, 0:11 * 128].bitcast(I32),
                    0x8000, None, ALU.add),
                    waits=[("ld", LD_D)], inc=("po", 1))
                PO_D = p.c("po")
                for c in range(2):
                    for kb in range(11):
                        p.ev("tensor", (lambda kb, c: lambda e: e.matmul(
                            pD[:, c * 512:(c + 1) * 512], bf16_view(wst_sb[:, kb * 128:(kb + 1) * 128]),
                            hf_v[:, kb * TPC + c * 512: kb * TPC + c * 512 + 512],
                            start=(kb == 0), stop=(kb == 10)))(kb, c),
                            waits=([("po", PO_D), ("dv", DV_H), ("dv", p.c("dv"))] if kb == 0 else []),
                            inc=(("pe", 1) if kb == 10 else None))
                PE_D = p.c("pe")
                for c in range(2):
                    p.ev("vector", (lambda c: lambda e: e.tensor_tensor(
                        sc1[:], pD[:, c * 512:(c + 1) * 512], cbc_sb[:, c * 512:(c + 1) * 512],
                        ALU.mult))(c),
                        waits=[("pe", PE_D), ("ac", AC_CBC), ("dv", p.c("dv"))], inc=("dv", 1))
                    p.ev("vector", (lambda ob, c: lambda e: e.tensor_add(
                        x3_sb[:, ob * TPC + c * 512: ob * TPC + (c + 1) * 512],
                        x3_sb[:, ob * TPC + c * 512: ob * TPC + (c + 1) * 512], sc1[:]))(ob, c),
                        waits=[("dv", p.c("dv"))], inc=("dv", 1))
        DV_OUT = p.c("dv")
        for ob in range(16):
            p.ev("gpsimd", (lambda ob: lambda e: e.dma_start(
                out=outO[ob * 128:(ob + 1) * 128, :], in_=x3_sb[:, ob * TPC:(ob + 1) * TPC]))(ob),
                waits=([("dv", DV_OUT)] if ob == 0 else []), inc=("so", 16))
        p.ev("gpsimd", lambda e: None, waits=[("so", p.c("so"))], inc=None)

        with nc.Block() as block:
            p.emit(block, sems)
    return nc


def prep_C(ins, bres):
    x2f, qf, kf, vf = {}, {}, {}, {}
    for b in range(2):
        x2f[b] = np.concatenate([np.asarray(bres[4 * b + q]["x2o"], np.float32) for q in range(4)], axis=1)
        qf[b] = np.concatenate([np.asarray(bres[4 * b + q]["qO"]) for q in range(4)], axis=1)
        kf[b] = np.ascontiguousarray(
            np.concatenate([np.asarray(bres[4 * b + q]["kO"]) for q in range(4)], axis=1))
        vf[b] = np.ascontiguousarray(
            np.concatenate([np.asarray(bres[4 * b + q]["vO"]) for q in range(4)], axis=0))
    wgm = np.ascontiguousarray(ins["L1_ffn_Wg"].reshape(8 * HQ, 1408))
    wum = np.ascontiguousarray(ins["L1_ffn_Wu"].reshape(8 * HQ, 1408))
    wdm = np.ascontiguousarray(ins["L1_ffn_Wd"].reshape(8 * 1408, HQ))
    wo = np.ascontiguousarray(ins["L1_attn_Wo"])
    wr = np.ascontiguousarray(ins["L1_ffn_Wr"])
    onec = np.ones((128, 1), np.float32)
    oner = np.ones((1, 128), np.float32)
    idn = np.eye(128, dtype=np.float32)

    in_maps = []
    for c in range(8):
        b, r = c // 4, c % 4
        idx = np.concatenate([np.arange(128 * (4 * i + r), 128 * (4 * i + r) + 128)
                              for i in range(8)])
        qi = np.arange(128)[:, None]
        kc = np.arange(512)[None, :]
        mask = np.where(kc <= 128 * r + qi, 0.0, -1e30).astype(np.float32)
        in_maps.append({
            "x2T": np.ascontiguousarray(x2f[b][:, idx]),
            "qT": np.ascontiguousarray(qf[b][:, idx]),
            "kF": kf[b], "vF": vf[b],
            "wo": wo, "wr": wr, "wgm": wgm, "wum": wum, "wdm": wdm,
            "maskS": mask, "onec": onec, "oner": oner, "idn": idn,
        })
    return in_maps


def kernel(**inputs):
    ra = run_bass_kernel_spmd(build_A(), prep_A(inputs), core_ids=list(range(8)))
    ares = [ra.results[c] for c in range(8)]
    rb = run_bass_kernel_spmd(build_B(), prep_B(inputs, ares), core_ids=list(range(8)))
    bres = [rb.results[c] for c in range(8)]
    rc = run_bass_kernel_spmd(build_C(), prep_C(inputs, bres), core_ids=list(range(8)))
    out = np.zeros((BQ, SQ, HQ), np.float32)
    for c in range(8):
        b, r = c // 4, c % 4
        idx = np.concatenate([np.arange(128 * (4 * i + r), 128 * (4 * i + r) + 128)
                              for i in range(8)])
        out[b, idx, :] = np.asarray(rc.results[c]["outO"], np.float32).T
    return out


# version 03
# speedup vs baseline: 1.3212x; 1.3212x over previous
import numpy as np
import ml_dtypes
from contextlib import ExitStack

import concourse.bass as bass
import concourse.mybir as mybir
from concourse.bass_utils import run_bass_kernel_spmd

F32 = mybir.dt.float32
BF16 = mybir.dt.bfloat16
I32 = mybir.dt.int32
AF = mybir.ActivationFunctionType
ALU = mybir.AluOpType
AX = mybir.AxisListType

BQ, SQ, HQ, IQ = 2, 4096, 2048, 4096
NH, HD, NS, RK = 2, 2048, 16, 128
TPC = 1024
TW = TPC + 3
MAGIC = float(np.float32(3 * 2.0 ** 22))
BF = ml_dtypes.bfloat16


def bf16_view(ap):
    return ap.bitcast(BF16).rearrange("p (f two) -> p f two", two=2)[:, :, 1:2].squeeze(2)


class Prog:

    ENGINES = ("sync", "tensor", "vector", "scalar", "gpsimd")

    def __init__(self):
        self.events = []
        self.cnt = {}

    def ev(self, engine, fn, waits=(), inc=None):
        assert engine in self.ENGINES
        self.events.append((engine, [(s, c) for (s, c) in waits if c > 0], fn, inc))
        if inc is not None:
            self.cnt[inc[0]] = self.cnt.get(inc[0], 0) + inc[1]

    def c(self, sem):
        return self.cnt.get(sem, 0)

    def emit(self, block, sems):
        for engine in self.ENGINES:
            evs = [(w, f, i) for (e, w, f, i) in self.events if e == engine]
            if not evs:
                continue
            dec = getattr(block, engine)

            def body(eng, evs=evs):
                for waits, fn, inc in evs:
                    for sem, cntv in waits:
                        eng.wait_ge(sems[sem], cntv)
                    instr = fn(eng)
                    if inc is not None:
                        assert instr is not None
                        instr.then_inc(sems[inc[0]], inc[1])
            dec(body)



def build_A():
    nc = bass.Bass("TRN2", target_bir_lowering=False)

    x0T = nc.dram_tensor("x0T", [HQ, TW], F32, kind="ExternalInput").ap()
    wx = nc.dram_tensor("wx", [HQ, IQ], F32, kind="ExternalInput").ap()
    wg = nc.dram_tensor("wg", [HQ, IQ], F32, kind="ExternalInput").ap()
    convw = nc.dram_tensor("convw", [128, 32 * 4], F32, kind="ExternalInput").ap()
    convb = nc.dram_tensor("convb", [128, 32], F32, kind="ExternalInput").ap()
    xpw_dt = nc.dram_tensor("xpw_dt", [128, 32 * RK], F32, kind="ExternalInput").ap()
    xpw_bc = nc.dram_tensor("xpw_bc", [128, 32 * 32], F32, kind="ExternalInput").ap()
    dtw = nc.dram_tensor("dtw", [128, NH * HD], F32, kind="ExternalInput").ap()
    dtb = nc.dram_tensor("dtb", [128, 32], F32, kind="ExternalInput").ap()
    alog = nc.dram_tensor("alog", [128, 32 * NS], F32, kind="ExternalInput").ap()
    dcol = nc.dram_tensor("dcol", [128, 32], F32, kind="ExternalInput").ap()
    onec = nc.dram_tensor("onec", [128, 1], F32, kind="ExternalInput").ap()
    oner = nc.dram_tensor("oner", [1, 128], F32, kind="ExternalInput").ap()
    idn = nc.dram_tensor("idn", [128, 128], F32, kind="ExternalInput").ap()

    y_loc = nc.dram_tensor("y_loc", [IQ, TPC], F32, kind="ExternalOutput").ap()
    cdt_o = nc.dram_tensor("cdt_o", [IQ, TPC], F32, kind="ExternalOutput").ap()
    bmat = nc.dram_tensor("bmat", [NH * NS, TPC], F32, kind="ExternalOutput").ap()
    cmat = nc.dram_tensor("cmat", [NH * NS, TPC], F32, kind="ExternalOutput").ap()
    gate_o = nc.dram_tensor("gate_o", [IQ, TPC], BF16, kind="ExternalOutput").ap()
    hfin = nc.dram_tensor("hfin", [128, 32 * NS], F32, kind="ExternalOutput").ap()
    cdtfin = nc.dram_tensor("cdtfin", [128, 32], F32, kind="ExternalOutput").ap()

    dt_i = nc.dram_tensor("dt_i", [IQ, TPC], F32, kind="Internal").ap()
    xc_i = nc.dram_tensor("xc_i", [IQ, TPC], BF16, kind="Internal").ap()

    p = Prog()
    with ExitStack() as es:
        sb = lambda name, shape, dt=F32: es.enter_context(nc.sbuf_tensor(name, shape, dt))
        psalloc = lambda name, shape, dt=F32: es.enter_context(nc.psum_tensor(name, shape, dt))
        sems = {}
        for s in ["ld", "so", "pe", "ac", "dv", "po"]:
            sems[s] = es.enter_context(nc.semaphore(s))

        x0_sb = sb("x0_sb", [128, 16 * TW])
        xln_sb = sb("xln_sb", [128, 16 * TW], BF16)
        wpair = sb("wpair", [128, 4096])
        dta_v = x0_sb[:, 0:16 * 512]
        h_v = x0_sb[:, 16 * 512:32 * 512]
        brep_v = xln_sb[:, 0:16 * 512]
        crep_v = xln_sb[:, 16 * 512:32 * 512]
        wbufs = [wpair[:, 0:2048], wpair[:, 2048:4096]]
        xca_v = wpair[:].bitcast(BF16)

        onec_sb = sb("onec_sb", [128, 1])
        oner_sb = sb("oner_sb", [1, 128])
        idnb_sb = sb("idnb_sb", [128, 128], BF16)
        idn_sb = sb("idn_sb", [128, 128])
        convw_sb = sb("convw_sb", [128, 32 * 4])
        convb_sb = sb("convb_sb", [128, 32])
        dtb_sb = sb("dtb_sb", [128, 32])
        dcol_sb = sb("dcol_sb", [128, 32])
        aneg_sb = sb("aneg_sb", [128, 32 * NS])
        xpdt_sb = sb("xpdt_sb", [128, 32 * RK])
        xpbc_sb = sb("xpbc_sb", [128, 32 * 32])
        dtw_sb = sb("dtw_sb", [128, NH * HD])
        negc_sb = sb("negc_sb", [128, 1])
        onecol_sb = sb("onecol_sb", [128, 1])
        hst_sb = sb("hst_sb", [128, 32 * NS])
        cdtc_sb = sb("cdtc_sb", [128, 32])

        row_sb = sb("row_sb", [1, 6 * 512])
        xin_sb = sb("xin_sb", [128, TW])
        xcv_sb = sb("xcv_sb", [128, TPC])
        xcb_sb = sb("xcb_sb", [128, TPC], BF16)
        sq_sb = [sb(f"sq_sb{i}", [128, 512]) for i in range(2)]
        dtin_sb = sb("dtin_sb", [128, TPC], BF16)
        bc_sb = sb("bc_sb", [32, TPC])
        sp_sb = [sb(f"sp_sb{i}", [128, 512]) for i in range(3)]
        dt_sb = sb("dt_sb", [128, TPC])
        zero_sb = sp_sb[0]

        bflat_sb = sb("bflat_sb", [1, 512])
        cflat_sb = sb("cflat_sb", [1, 512])
        dtx_sb = sb("dtx_sb", [128, 512], BF16)
        dbu_sb = sb("dbu_sb", [128, 512], BF16)
        g_sb = sb("g_sb", [128, 512], BF16)
        da_sb = xcv_sb[:, 0:512]
        cdt_sb = dt_sb[:, 0:512]
        ya_sb = dt_sb[:, 512:1024]

        pA = psalloc("pA", [128, 512])
        pB = psalloc("pB", [128, 512])
        pC = psalloc("pC", [128, 1024])
        pD = psalloc("pD", [128, 1024])

        def mkload(dst, src):
            return lambda e: e.dma_start(out=dst, in_=src)

        loads = [
            (onec_sb[:], onec[:]),
            (oner_sb[:], oner[:]),
            (idn_sb[:], idn[:]),
            (convw_sb[:], convw[:]),
            (convb_sb[:], convb[:]),
            (dtb_sb[:], dtb[:]),
            (dcol_sb[:], dcol[:]),
            (aneg_sb[:], alog[:]),
            (xpdt_sb[:], xpw_dt[:]),
            (xpbc_sb[:], xpw_bc[:]),
            (dtw_sb[:], dtw[:]),
            (x0_sb[:].rearrange("p (c t) -> p c t", c=16),
             x0T.rearrange("(c p) t -> p c t", p=128)),
        ]
        for dst, src in loads:
            p.ev("sync", mkload(dst, src), inc=("ld", 16))
        LD_CONST = p.c("ld")

        for buf, val in [(negc_sb, -1.0), (onecol_sb, 1.0),
                         (hst_sb, 0.0), (cdtc_sb, 0.0)]:
            p.ev("vector", (lambda b: (lambda e: e.memset(b[:], val)))(buf) if False else
                 (lambda e, b=buf, v=val: e.memset(b[:], v)), inc=("dv", 1))
        p.ev("vector", lambda e: e.tensor_copy(idnb_sb[:], idn_sb[:]),
             waits=[("ld", LD_CONST)], inc=("dv", 1))
        for wt in [xpdt_sb, xpbc_sb, dtw_sb]:
            p.ev("gpsimd", (lambda wt: lambda e: e.tensor_scalar(
                wt[:].bitcast(I32), wt[:].bitcast(I32), 0x8000, None, ALU.add))(wt),
                waits=[("ld", LD_CONST)], inc=("po", 1))
        PO_RTNC = p.c("po")
        p.ev("scalar", lambda e: e.activation(aneg_sb[:], aneg_sb[:], AF.Exp),
             waits=[("ld", LD_CONST)], inc=("ac", 1))
        p.ev("vector", lambda e: e.tensor_scalar_mul(aneg_sb[:], aneg_sb[:], -1.0),
             waits=[("ac", p.c("ac"))], inc=("dv", 1))

        inv_h = 1.0 / HQ
        R_M, R_MSQ, R_VAR, R_RSTD, R_A, R_B = (i * 512 for i in range(6))
        for (c0, w) in [(3, 512), (515, 512), (0, 3)]:
            for j in range(16):
                p.ev("scalar",
                     lambda e, j=j, c0=c0, w=w: e.activation(
                         sq_sb[j % 2][:, 0:w], x0_sb[:, j * TW + c0: j * TW + c0 + w], AF.Square),
                     waits=[("ld", LD_CONST), ("pe", max(0, p.c("pe") - 1)),
                            ("dv", p.c("dv"))],
                     inc=("ac", 1))
                p.ev("tensor",
                     lambda e, j=j, c0=c0, w=w: e.matmul(
                         pA[0:1, 0:w], onec_sb[:], x0_sb[:, j * TW + c0: j * TW + c0 + w],
                         start=(j == 0), stop=(j == 15)),
                     waits=([("ld", LD_CONST), ("dv", p.c("dv"))] if j == 0 else []), inc=None)
                p.ev("tensor",
                     lambda e, j=j, w=w: e.matmul(
                         pB[0:1, 0:w], onec_sb[:], sq_sb[j % 2][:, 0:w],
                         start=(j == 0), stop=(j == 15)),
                     waits=[("ac", p.c("ac"))], inc=("pe", 1))
            PE_STATS = p.c("pe")

            def rowstep(fn, waits):
                p.ev("vector", fn, waits=waits, inc=("dv", 1))
            rowstep(lambda e, w=w: e.tensor_scalar_mul(row_sb[:, R_M:R_M + w], pA[0:1, 0:w], inv_h),
                    [("pe", PE_STATS)])
            rowstep(lambda e, w=w: e.tensor_scalar_mul(row_sb[:, R_MSQ:R_MSQ + w], pB[0:1, 0:w], inv_h),
                    [("dv", p.c("dv"))])
            rowstep(lambda e, w=w: e.tensor_mul(row_sb[:, R_VAR:R_VAR + w], row_sb[:, R_M:R_M + w],
                                                row_sb[:, R_M:R_M + w]), [("dv", p.c("dv"))])
            rowstep(lambda e, w=w: e.tensor_sub(row_sb[:, R_VAR:R_VAR + w], row_sb[:, R_MSQ:R_MSQ + w],
                                                row_sb[:, R_VAR:R_VAR + w]), [("dv", p.c("dv"))])
            rowstep(lambda e, w=w: e.tensor_scalar_add(row_sb[:, R_VAR:R_VAR + w],
                                                       row_sb[:, R_VAR:R_VAR + w], 1e-5),
                    [("dv", p.c("dv"))])
            p.ev("scalar",
                 lambda e, w=w: e.activation(row_sb[:, R_RSTD:R_RSTD + w], row_sb[:, R_VAR:R_VAR + w], AF.Sqrt),
                 waits=[("dv", p.c("dv"))], inc=("ac", 1))
            rowstep(lambda e, w=w: e.reciprocal(row_sb[:, R_A:R_A + w], row_sb[:, R_RSTD:R_RSTD + w]),
                    [("ac", p.c("ac"))])
            rowstep(lambda e, w=w: e.tensor_mul(row_sb[:, R_B:R_B + w], row_sb[:, R_M:R_M + w],
                                                row_sb[:, R_A:R_A + w]), [("dv", p.c("dv"))])
            rowstep(lambda e, w=w: e.tensor_scalar_mul(row_sb[:, R_B:R_B + w], row_sb[:, R_B:R_B + w], -1.0),
                    [("dv", p.c("dv"))])
            DV_ROW = p.c("dv")
            p.ev("tensor",
                 lambda e, w=w: e.matmul(pC[:, 0:w], oner_sb[:], row_sb[:, R_A:R_A + w],
                                         start=True, stop=True),
                 waits=[("dv", DV_ROW)], inc=None)
            p.ev("tensor",
                 lambda e, w=w: e.matmul(pC[:, 512:512 + w], oner_sb[:], row_sb[:, R_B:R_B + w],
                                         start=True, stop=True),
                 waits=[], inc=("pe", 1))
            PE_BC = p.c("pe")
            for j in range(16):
                p.ev("vector",
                     lambda e, j=j, c0=c0, w=w: e.tensor_mul(
                         sq_sb[0][:, 0:w], x0_sb[:, j * TW + c0: j * TW + c0 + w], pC[:, 0:w]),
                     waits=[("pe", PE_BC), ("ac", p.c("ac"))], inc=("dv", 1))
                p.ev("vector",
                     lambda e, j=j, c0=c0, w=w: e.tensor_tensor(
                         xln_sb[:, j * TW + c0: j * TW + c0 + w],
                         sq_sb[0][:, 0:w], pC[:, 512:512 + w], ALU.add),
                     waits=[("dv", p.c("dv"))], inc=("dv", 1))
        DV_LN = p.c("dv")
        import os
        if os.environ.get("DEBUG_A"):
            dbg = nc.dram_tensor("dbg_xln", [128, 16 * TW], BF16, kind="ExternalOutput").ap()
            p.ev("gpsimd", lambda e: e.dma_start(out=dbg[:], in_=xln_sb[:]),
                 waits=[("dv", DV_LN)], inc=("so", 16))
            dbg2 = nc.dram_tensor("dbg_row", [1, 6 * 512], F32, kind="ExternalOutput").ap()
            p.ev("gpsimd", lambda e: e.dma_start(out=dbg2[:], in_=row_sb[:]),
                 waits=[("dv", DV_LN)], inc=("so", 16))

        wuse = {}
        widx = 0
        XGRP = [(0, 512), (512, 512), (1024, 3)]
        GGRP = [(3, 512, 0), (515, 512, 512)]

        for h in range(NH):
            for obl in range(16):
                ob = h * 16 + obl
                buf = wbufs[widx % 2]
                p.ev("sync",
                     lambda e, buf=buf, ob=ob: e.dma_start(
                         out=buf.rearrange("p (kb o) -> p kb o", kb=16),
                         in_=wx.rearrange("(kb p) o -> p kb o", p=128)[:, :, ob * 128:(ob + 1) * 128]),
                     waits=[("pe", wuse.get(widx - 2, 0))], inc=("ld", 16))
                LD_W = p.c("ld")
                p.ev("gpsimd", (lambda buf: lambda e: e.tensor_scalar(
                    buf.bitcast(I32), buf.bitcast(I32), 0x8000, None, ALU.add))(buf),
                    waits=[("ld", LD_W)], inc=("po", 1))
                PO_W = p.c("po")
                for gi, (xc0, wdt) in enumerate(XGRP):
                    pst = [pA, pB, pA][gi]
                    for kb in range(16):
                        p.ev("tensor",
                             lambda e, kb=kb, buf=buf, xc0=xc0, wdt=wdt, pst=pst: e.matmul(
                                 pst[:, 0:wdt],
                                 bf16_view(buf[:, kb * 128:(kb + 1) * 128]),
                                 xln_sb[:, kb * TW + xc0: kb * TW + xc0 + wdt],
                                 start=(kb == 0), stop=(kb == 15)),
                             waits=([("ld", LD_W), ("dv", p.c("dv")), ("po", PO_W)] if kb == 0 else []),
                             inc=(("pe", 1) if kb == 15 else None))
                    p.ev("vector",
                         lambda e, xc0=xc0, wdt=wdt, pst=pst: e.tensor_copy(
                             xin_sb[:, xc0:xc0 + wdt], pst[:, 0:wdt]),
                         waits=[("pe", p.c("pe")), ("po", p.c("po"))], inc=("dv", 1))
                wuse[widx] = p.c("pe")
                DV_XIN = p.c("dv")

                def conv_emit(e, ob=ob):
                    wc = lambda k: convw_sb[:, ob * 4 + k: ob * 4 + k + 1]
                    e.tensor_scalar(xcv_sb[:], xin_sb[:, 0:TPC], wc(0), convb_sb[:, ob:ob + 1],
                                    ALU.mult, ALU.add)
                    e.scalar_tensor_tensor(xcv_sb[:], xin_sb[:, 1:1 + TPC], wc(1), xcv_sb[:], ALU.mult, ALU.add)
                    e.scalar_tensor_tensor(xcv_sb[:], xin_sb[:, 2:2 + TPC], wc(2), xcv_sb[:], ALU.mult, ALU.add)
                    return e.scalar_tensor_tensor(xcv_sb[:], xin_sb[:, 3:3 + TPC], wc(3), xcv_sb[:], ALU.mult, ALU.add)
                p.ev("vector", conv_emit, waits=[("dv", DV_XIN), ("ac", p.c("ac"))], inc=("dv", 1))
                p.ev("scalar", lambda e: e.activation(xcb_sb[:], xcv_sb[:], AF.Silu),
                     waits=[("dv", p.c("dv")), ("pe", p.c("pe")), ("so", p.c("so"))], inc=("ac", 1))
                AC_XC = p.c("ac")
                p.ev("gpsimd",
                     lambda e, ob=ob: e.dma_start(out=xc_i[ob * 128:(ob + 1) * 128, :], in_=xcb_sb[:]),
                     waits=[("ac", AC_XC)], inc=("so", 16))
                for gi, (pc0, wdt) in enumerate([(0, 512), (512, 512)]):
                    p.ev("tensor",
                         lambda e, ob=ob, obl=obl, pc0=pc0, wdt=wdt: e.matmul(
                             pC[:, pc0:pc0 + wdt],
                             bf16_view(xpdt_sb[:, ob * RK:(ob + 1) * RK]),
                             xcb_sb[:, pc0:pc0 + wdt],
                             start=(obl == 0), stop=(obl == 15)),
                         waits=([("ac", AC_XC), ("po", PO_RTNC)] if gi == 0 else []), inc=None)
                    p.ev("tensor",
                         lambda e, ob=ob, obl=obl, pc0=pc0, wdt=wdt: e.matmul(
                             pD[0:32, pc0:pc0 + wdt],
                             bf16_view(xpbc_sb[:, ob * 32:(ob + 1) * 32]),
                             xcb_sb[:, pc0:pc0 + wdt],
                             start=(obl == 0), stop=(obl == 15)),
                         waits=[], inc=(("pe", 1) if gi == 1 else None))
                widx += 1
            PE_XPROJ = p.c("pe")

            p.ev("vector",
                 lambda e, h=h: e.tensor_copy(dtin_sb[:], pC[:]),
                 waits=[("pe", PE_XPROJ), ("so", p.c("so"))], inc=("dv", 1))
            p.ev("vector",
                 lambda e, h=h: e.tensor_copy(bc_sb[:], pD[0:32, :]),
                 waits=[("pe", PE_XPROJ), ("so", p.c("so"))], inc=("dv", 1))
            DV_EVBC = p.c("dv")
            p.ev("gpsimd",
                 lambda e, h=h: e.dma_start(out=bmat[h * NS:(h + 1) * NS, :],
                                            in_=bc_sb[0:16, :]),
                 waits=[("dv", DV_EVBC)], inc=("so", 16))
            p.ev("gpsimd",
                 lambda e, h=h: e.dma_start(out=cmat[h * NS:(h + 1) * NS, :],
                                            in_=bc_sb[16:32, :]),
                 waits=[("dv", DV_EVBC)], inc=("so", 16))

            for db in range(16):
                gdb = h * 16 + db
                for gi, (pc0, wdt) in enumerate([(0, 512), (512, 512)]):
                    pst = [pA, pB][gi]
                    p.ev("tensor",
                         lambda e, h=h, db=db, pc0=pc0, wdt=wdt, pst=pst: e.matmul(
                             pst[:, 0:wdt],
                             bf16_view(dtw_sb[:, h * HD + db * 128: h * HD + (db + 1) * 128]),
                             dtin_sb[:, pc0:pc0 + wdt],
                             start=True, stop=True),
                         waits=[("dv", p.c("dv")), ("ac", p.c("ac"))],
                         inc=("pe", 1))
                    PE_DT = p.c("pe")
                    p.ev("scalar",
                         lambda e, pst=pst, wdt=wdt, gdb=gdb: e.activation(
                             sp_sb[0][:, 0:wdt], pst[:, 0:wdt], AF.Abs, bias=dtb_sb[:, gdb:gdb + 1]),
                         waits=[("pe", PE_DT), ("dv", p.c("dv"))], inc=("ac", 1))
                    p.ev("scalar",
                         lambda e, wdt=wdt: e.activation(
                             sp_sb[1][:, 0:wdt], sp_sb[0][:, 0:wdt], AF.Exp, scale=negc_sb[:, 0:1]),
                         inc=("ac", 1))
                    p.ev("scalar",
                         lambda e, wdt=wdt: e.activation(
                             sp_sb[2][:, 0:wdt], sp_sb[1][:, 0:wdt], AF.Ln, bias=onecol_sb[:, 0:1]),
                         inc=("ac", 1))
                    p.ev("scalar",
                         lambda e, pst=pst, wdt=wdt, gdb=gdb: e.activation(
                             sp_sb[0][:, 0:wdt], pst[:, 0:wdt], AF.Relu, bias=dtb_sb[:, gdb:gdb + 1]),
                         inc=("ac", 1))
                    AC_SP = p.c("ac")
                    p.ev("vector",
                         lambda e, pc0=pc0, wdt=wdt: e.tensor_add(
                             dt_sb[:, pc0:pc0 + wdt], sp_sb[0][:, 0:wdt], sp_sb[2][:, 0:wdt]),
                         waits=[("ac", AC_SP), ("so", p.c("so"))], inc=("dv", 1))
                DV_DT = p.c("dv")
                p.ev("gpsimd",
                     lambda e, gdb=gdb: e.dma_start(out=dt_i[gdb * 128:(gdb + 1) * 128, :], in_=dt_sb[:]),
                     waits=[("dv", DV_DT)], inc=("so", 16))

            for obl in range(16):
                ob = h * 16 + obl
                buf = wbufs[widx % 2]
                p.ev("sync",
                     lambda e, buf=buf, ob=ob: e.dma_start(
                         out=buf.rearrange("p (kb o) -> p kb o", kb=16),
                         in_=wg.rearrange("(kb p) o -> p kb o", p=128)[:, :, ob * 128:(ob + 1) * 128]),
                     waits=[("pe", wuse.get(widx - 2, 0))], inc=("ld", 16))
                LD_W = p.c("ld")
                p.ev("gpsimd", (lambda buf: lambda e: e.tensor_scalar(
                    buf.bitcast(I32), buf.bitcast(I32), 0x8000, None, ALU.add))(buf),
                    waits=[("ld", LD_W)], inc=("po", 1))
                PO_W = p.c("po")
                for gi, (xc0, wdt, oc0) in enumerate(GGRP):
                    pst = [pA, pB][gi]
                    for kb in range(16):
                        p.ev("tensor",
                             lambda e, kb=kb, buf=buf, xc0=xc0, wdt=wdt, pst=pst: e.matmul(
                                 pst[:, 0:wdt],
                                 bf16_view(buf[:, kb * 128:(kb + 1) * 128]),
                                 xln_sb[:, kb * TW + xc0: kb * TW + xc0 + wdt],
                                 start=(kb == 0), stop=(kb == 15)),
                             waits=([("ld", LD_W), ("ac", p.c("ac")), ("po", PO_W)] if kb == 0 else []),
                             inc=(("pe", 1) if kb == 15 else None))
                    p.ev("scalar",
                         lambda e, wdt=wdt, oc0=oc0, pst=pst: e.activation(
                             xcb_sb[:, oc0:oc0 + wdt], pst[:, 0:wdt], AF.Silu),
                         waits=[("pe", p.c("pe")), ("so", p.c("so"))], inc=("ac", 1))
                wuse[widx] = p.c("pe")
                AC_G = p.c("ac")
                p.ev("gpsimd",
                     lambda e, ob=ob: e.dma_start(out=gate_o[ob * 128:(ob + 1) * 128, :],
                                                  in_=xcb_sb[:]),
                     waits=[("ac", AC_G)], inc=("so", 16))
                widx += 1

        p.ev("vector", lambda e: e.memset(zero_sb[:], 0.0),
             waits=[("ac", p.c("ac"))], inc=("dv", 1))
        for ch in range(2):
            tc0 = ch * 512
            for h in range(NH):
                p.ev("sync",
                     lambda e, h=h, tc0=tc0: e.dma_start(
                         out=dta_v.rearrange("p (blk t) -> p blk t", t=512),
                         in_=dt_i[h * HD:(h + 1) * HD, tc0:tc0 + 512].rearrange("(blk p) t -> p blk t", p=128)),
                     waits=[("so", p.c("so")), ("pe", p.c("pe"))], inc=("ld", 16))
                p.ev("sync",
                     lambda e, h=h, tc0=tc0: e.dma_start(
                         out=xca_v.rearrange("p (blk t) -> p blk t", t=512),
                         in_=xc_i[h * HD:(h + 1) * HD, tc0:tc0 + 512].rearrange("(blk p) t -> p blk t", p=128)),
                     inc=("ld", 16))
                LD_A2 = p.c("ld")

                for n in range(NS):
                    p.ev("sync",
                         lambda e, h=h, n=n, tc0=tc0: e.dma_start(
                             out=bflat_sb[:],
                             in_=bmat[h * NS + n: h * NS + n + 1, tc0:tc0 + 512]),
                         waits=[("pe", p.c("pe"))], inc=("ld", 16))
                    p.ev("tensor",
                         lambda e, n=n: e.matmul(pA[:], oner_sb[:], bflat_sb[:],
                                                 start=True, stop=True),
                         waits=[("ld", p.c("ld")), ("ac", p.c("ac")), ("dv", p.c("dv"))],
                         inc=("pe", 1))
                    p.ev("scalar",
                         lambda e, n=n: e.activation(brep_v[:, n * 512:(n + 1) * 512], pA[:], AF.Identity),
                         waits=[("pe", p.c("pe"))], inc=("ac", 1))
                    p.ev("sync",
                         lambda e, h=h, n=n, tc0=tc0: e.dma_start(
                             out=cflat_sb[:],
                             in_=cmat[h * NS + n: h * NS + n + 1, tc0:tc0 + 512]),
                         waits=[("pe", p.c("pe"))], inc=("ld", 16))
                    p.ev("tensor",
                         lambda e, n=n: e.matmul(pA[:], oner_sb[:], cflat_sb[:],
                                                 start=True, stop=True),
                         waits=[("ld", p.c("ld")), ("ac", p.c("ac"))], inc=("pe", 1))
                    p.ev("scalar",
                         lambda e, n=n: e.activation(crep_v[:, n * 512:(n + 1) * 512], pA[:], AF.Identity),
                         waits=[("pe", p.c("pe"))], inc=("ac", 1))
                AC_REP = p.c("ac")

                for db in range(16):
                    blk = h * 16 + db
                    dta = dta_v[:, db * 512:(db + 1) * 512]
                    xca = xca_v[:, db * 512:(db + 1) * 512]
                    p.ev("vector",
                         lambda e, dta=dta, blk=blk: e.tensor_tensor_scan(
                             cdt_sb[:], dta, zero_sb[:], cdtc_sb[:, blk:blk + 1], ALU.add, ALU.add),
                         waits=[("ld", LD_A2), ("so", p.c("so"))], inc=("dv", 1))
                    p.ev("vector",
                         lambda e, blk=blk: e.tensor_copy(cdtc_sb[:, blk:blk + 1], cdt_sb[:, 511:512]),
                         waits=[("dv", p.c("dv"))], inc=("dv", 1))
                    DV_CDT = p.c("dv")
                    p.ev("gpsimd",
                         lambda e, blk=blk, tc0=tc0: e.dma_start(
                             out=cdt_o[blk * 128:(blk + 1) * 128, tc0:tc0 + 512], in_=cdt_sb[:]),
                         waits=[("dv", DV_CDT)], inc=("so", 16))
                    p.ev("gpsimd",
                         lambda e, dta=dta, xca=xca: e.tensor_mul(dtx_sb[:], dta, xca),
                         waits=[("dv", DV_CDT)], inc=("po", 1))
                    PO_DTX = p.c("po")
                    for n in range(NS):
                        p.ev("scalar",
                             lambda e, dta=dta, blk=blk, n=n: e.activation(
                                 da_sb[:], dta, AF.Exp,
                                 scale=aneg_sb[:, blk * NS + n: blk * NS + n + 1]),
                             waits=[("ac", AC_REP), ("dv", p.c("dv")), ("ld", LD_A2)], inc=("ac", 1))
                        p.ev("gpsimd",
                             lambda e, n=n: e.tensor_mul(dbu_sb[:], dtx_sb[:],
                                                         brep_v[:, n * 512:(n + 1) * 512]),
                             waits=[("po", PO_DTX), ("dv", p.c("dv"))], inc=("po", 1))
                        p.ev("vector",
                             lambda e, n=n, blk=blk: e.tensor_tensor_scan(
                                 h_v[:, n * 512:(n + 1) * 512], da_sb[:], dbu_sb[:],
                                 hst_sb[:, blk * NS + n: blk * NS + n + 1], ALU.mult, ALU.add),
                             waits=[("ac", p.c("ac")), ("po", p.c("po")), ("pe", p.c("pe"))],
                             inc=("dv", 1))
                        p.ev("vector",
                             lambda e, n=n: e.tensor_mul(g_sb[:], crep_v[:, n * 512:(n + 1) * 512],
                                                         h_v[:, n * 512:(n + 1) * 512]),
                             waits=[("dv", p.c("dv"))], inc=("dv", 1))
                        p.ev("tensor",
                             lambda e, n=n: e.matmul(pB[:], idnb_sb[:], g_sb[:],
                                                     start=(n == 0), stop=(n == NS - 1)),
                             waits=[("dv", p.c("dv"))], inc=("pe", 1))
                    p.ev("vector",
                         lambda e, blk=blk: e.tensor_copy(
                             hst_sb[:, blk * NS:(blk + 1) * NS],
                             h_v.rearrange("p (n t) -> p n t", n=NS)[:, :, 511:512].squeeze(2)),
                         waits=[("pe", p.c("pe")), ("dv", p.c("dv"))], inc=("dv", 1))
                    p.ev("vector",
                         lambda e, xca=xca, blk=blk: e.scalar_tensor_tensor(
                             ya_sb[:], xca, dcol_sb[:, blk:blk + 1], pB[:], ALU.mult, ALU.add),
                         inc=("dv", 1))
                    DV_Y = p.c("dv")
                    p.ev("gpsimd",
                         lambda e, blk=blk, tc0=tc0: e.dma_start(
                             out=y_loc[blk * 128:(blk + 1) * 128, tc0:tc0 + 512], in_=ya_sb[:]),
                         waits=[("dv", DV_Y)], inc=("so", 16))

        p.ev("gpsimd",
             lambda e: e.dma_start(out=hfin[:], in_=hst_sb[:]),
             waits=[("dv", p.c("dv"))], inc=("so", 16))
        p.ev("gpsimd",
             lambda e: e.dma_start(out=cdtfin[:], in_=cdtc_sb[:]),
             waits=[("dv", p.c("dv"))], inc=("so", 16))
        p.ev("gpsimd", lambda e: None, waits=[("so", p.c("so"))], inc=None)

        with nc.Block() as block:
            p.emit(block, sems)
    return nc


def prep_A(ins):
    x0 = ins["L0_attn_hidden_states"]
    wxT = np.ascontiguousarray(ins["L0_attn_in_proj_w"][0::2, :].T)
    wgT = np.ascontiguousarray(ins["L0_attn_in_proj_w"][1::2, :].T)
    def blkmaj(a, inner):
        return np.ascontiguousarray(
            a.reshape(32, 128, inner).transpose(1, 0, 2).reshape(128, 32 * inner))
    convw = blkmaj(ins["L0_attn_conv_w"].reshape(IQ, 4), 4)
    convb = blkmaj(ins["L0_attn_conv_b"].reshape(IQ, 1), 1)
    xpw = ins["L0_attn_x_proj_weight"]
    xpw_dt = blkmaj(np.ascontiguousarray(xpw[:, :RK, :].transpose(0, 2, 1)).reshape(IQ, RK), RK)
    xpw_bc = blkmaj(np.ascontiguousarray(xpw[:, RK:, :].transpose(0, 2, 1)).reshape(IQ, 32), 32)
    dtwT = np.ascontiguousarray(
        ins["L0_attn_dt_proj_weight"].transpose(0, 2, 1).reshape(NH, RK, HD)
        .transpose(1, 0, 2).reshape(RK, NH * HD))
    dtb = blkmaj(ins["L0_attn_dt_proj_bias"].reshape(IQ, 1), 1)
    alog = blkmaj(np.ascontiguousarray(ins["L0_attn_A_log"].reshape(IQ, NS)), NS)
    dcol = blkmaj(ins["L0_attn_D"].reshape(IQ, 1), 1)
    onec = np.ones((128, 1), np.float32)
    oner = np.ones((1, 128), np.float32)
    idn = np.eye(128, dtype=np.float32)

    in_maps = []
    for c in range(8):
        b, q = c // 4, c % 4
        t0 = q * TPC
        xs = np.zeros((HQ, TW), np.float32)
        lo = max(0, t0 - 3)
        xs[:, 3 - (t0 - lo):] = x0[b, lo:t0 + TPC, :].T
        in_maps.append({
            "x0T": np.ascontiguousarray(xs), "wx": wxT, "wg": wgT,
            "convw": convw, "convb": convb, "xpw_dt": xpw_dt, "xpw_bc": xpw_bc,
            "dtw": dtwT, "dtb": dtb, "alog": alog, "dcol": dcol,
            "onec": onec, "oner": oner, "idn": idn,
        })
    return in_maps



def build_B(stage=6):
    nc = bass.Bass("TRN2", target_bir_lowering=False)
    IN = lambda n, s, dt=F32: nc.dram_tensor(n, s, dt, kind="ExternalInput").ap()
    OUT = lambda n, s, dt=F32: nc.dram_tensor(n, s, dt, kind="ExternalOutput").ap()

    y_loc = IN("y_loc", [IQ, TPC])
    cdt = IN("cdt", [IQ, TPC])
    cmat = IN("cmat", [NH * NS, TPC])
    gate = IN("gate", [IQ, TPC], BF16)
    hprev = IN("hprev", [3 * 128, 32 * NS])
    cprev = IN("cprev", [3 * 128, 32])
    alog = IN("alog", [128, 32 * NS])
    x0T = IN("x0T", [HQ, TPC])
    opw = IN("opw", [IQ, HQ])
    wgT = IN("wgT", [HQ, 8192])
    wuT = IN("wuT", [HQ, 8192])
    wdT = IN("wdT", [8192, HQ])
    wlnc = IN("wlnc", [128, 64])
    wq = IN("wq", [HQ, HQ])
    wk = IN("wk", [HQ, HQ])
    wv = IN("wv", [HQ, HQ])
    cosT = IN("cosT", [128, TPC])
    sinT = IN("sinT", [128, TPC])
    onec = IN("onec", [128, 1])
    oner = IN("oner", [1, 128])
    idn = IN("idn", [128, 128])

    x2o = OUT("x2o", [HQ, TPC])
    qO = OUT("qO", [HQ, TPC], BF16)
    kO = OUT("kO", [HQ, TPC], BF16)
    vO = OUT("vO", [TPC, HQ], BF16)
    x1dbg = OUT("x1dbg", [HQ, TPC]) if stage == 2 else None

    x1_i = nc.dram_tensor("x1_i", [HQ, TPC], F32, kind="Internal").ap()

    p = Prog()
    with ExitStack() as es:
        sb = lambda name, shape, dt=F32: es.enter_context(nc.sbuf_tensor(name, shape, dt))
        psalloc = lambda name, shape, dt=F32: es.enter_context(nc.psum_tensor(name, shape, dt))
        sems = {}
        for s in ["ld", "so", "pe", "ac", "dv", "po"]:
            sems[s] = es.enter_context(nc.semaphore(s))

        BH = sb("BH", [128, 64 * TPC], BF16)
        BHf = BH[:].bitcast(F32)
        qx_sb = sb("qx_sb", [128, 16 * TPC], BF16)
        wgs = sb("wgs", [128, 2048])
        wus = sb("wus", [128, 2048])
        qw_sb = sb("qw_sb", [128, 4096], BF16)
        ytile = BHf[:, 16384:16896]
        ctile = BHf[:, 16896:17408]
        ptile = BHf[:, 17408:17920]
        aneg_sbv = BHf[:, 17920:18432]
        h0_sbv = BHf[:, 18432:18944]
        g2t = BH[:, 37888:38400]
        gtile = BH[:, 38400:38912]
        scpair = sb("scpair", [128, 1024])
        sc1 = scpair[:, 0:512]
        sc2 = scpair[:, 512:1024]
        qs2 = scpair[:].bitcast(BF16)
        sc3 = sb("sc3", [128, 512], BF16)
        x0t_sb = sb("x0t_sb", [128, 512])
        sq2_sb = sb("sq2_sb", [128, 512])
        hp_sb = BHf[:, 0:512]
        pf_sb = BHf[:, 512:1024]
        cp_sb = sb("cp_sb", [128, 32])
        wln_sb = sb("wln_sb", [128, 64])
        onec_sb = sb("onec_sb", [128, 1])
        oner_sb = sb("oner_sb", [1, 128])
        idn_sb = sb("idn_sb", [128, 128])
        idnb_sb = sb("idnb_sb", [128, 128], BF16)
        cflat_sb = BHf[0:1, 19456:19968]
        row_sb = sb("row_sb", [1, 4 * 512])
        qxf = qx_sb[:].bitcast(F32)
        scl_sb = sb("scl_sb", [1, 16])
        thr_sb = sb("thr_sb", [128, 8])
        scbc_sb = sb("scbc_sb", [128, 4 * 512], BF16)
        sxc_sb = sb("sxc_sb", [128, 48])
        pk_sb = sb("pk_sb", [128, 16])
        pr_sb = sb("pr_sb", [16, 128])

        pA = psalloc("pA", [128, 512])
        pB = psalloc("pB", [128, 512])
        pC = psalloc("pC", [128, 1024])
        pD = psalloc("pD", [128, 1024])
        pE = psalloc("pE", [128, 128])

        for dst, src2 in [(onec_sb[:], onec[:]), (oner_sb[:], oner[:]), (idn_sb[:], idn[:]),
                          (aneg_sbv, alog[:]), (wln_sb[:], wlnc[:])]:
            p.ev("sync", (lambda d, s: lambda e: e.dma_start(out=d, in_=s))(dst, src2),
                 inc=("ld", 16))
        LD0 = p.c("ld")
        p.ev("scalar", lambda e: e.activation(aneg_sbv, aneg_sbv, AF.Exp),
             waits=[("ld", LD0)], inc=("ac", 1))
        p.ev("vector", lambda e: e.tensor_scalar_mul(aneg_sbv, aneg_sbv, -1.0),
             waits=[("ac", p.c("ac"))], inc=("dv", 1))
        p.ev("vector", lambda e: e.memset(h0_sbv, 0.0), inc=("dv", 1))
        p.ev("vector", lambda e: e.tensor_copy(idnb_sb[:], idn_sb[:]),
             waits=[("ld", LD0)], inc=("dv", 1))
        for j in range(3):
            p.ev("sync", (lambda j: lambda e: e.dma_start(
                out=hp_sb, in_=hprev[j * 128:(j + 1) * 128, :]))(j),
                waits=[("dv", p.c("dv"))], inc=("ld", 16))
            p.ev("sync", (lambda j: lambda e: e.dma_start(
                out=cp_sb[:], in_=cprev[j * 128:(j + 1) * 128, :]))(j), inc=("ld", 16))
            LDJ = p.c("ld")
            for blk in range(32):
                p.ev("scalar", (lambda blk: lambda e: e.activation(
                    pf_sb[:, blk * NS:(blk + 1) * NS], aneg_sbv[:, blk * NS:(blk + 1) * NS],
                    AF.Exp, scale=cp_sb[:, blk:blk + 1]))(blk),
                    waits=([("ld", LDJ), ("dv", p.c("dv"))] if blk == 0 else []),
                    inc=("ac", 1))
            p.ev("vector", lambda e: e.tensor_mul(h0_sbv, h0_sbv, pf_sb),
                 waits=[("ac", p.c("ac")), ("dv", p.c("dv"))], inc=("dv", 1))
            p.ev("vector", lambda e: e.tensor_add(h0_sbv, h0_sbv, hp_sb),
                 waits=[("dv", p.c("dv"))], inc=("dv", 1))
        DV_H0 = p.c("dv")

        strm_v = BHf[:, 0:8192]
        absv = BHf[:, 8192:16384]
        for m, wsrc in [(0, wgT), (1, wuT), (2, wdT)]:
            nrow, ncol = wsrc.shape
            rpt = (nrow // 128) // 16
            for t in range(16):
                p.ev("sync", (lambda wsrc, t, rpt, ncol: lambda e: e.dma_start(
                    out=strm_v.rearrange("p (r c) -> p r c", c=ncol),
                    in_=wsrc.rearrange("(r p) c -> p r c", p=128)[:, t * rpt:(t + 1) * rpt, :]))(wsrc, t, rpt, ncol),
                    waits=[("ac", p.c("ac"))], inc=("ld", 16))
                p.ev("scalar", lambda e: e.activation(absv, strm_v, AF.Abs),
                     waits=[("ld", p.c("ld"))], inc=("ac", 1))
                AC_ABS = p.c("ac")
                for cc in range(16):
                    p.ev("tensor", (lambda cc, t: lambda e: e.matmul(
                        pA[0:1, 0:512], onec_sb[:], absv[:, cc * 512:(cc + 1) * 512],
                        start=(t == 0 and cc == 0), stop=(t == 15 and cc == 15)))(cc, t),
                        waits=([("ac", AC_ABS)] if cc == 0 else []),
                        inc=(("pe", 1) if (t == 15 and cc == 15) else None))
            PE_SUM = p.c("pe")
            p.ev("vector", lambda e: e.reduce_sum(row_sb[:, 0:1], pA[0:1, 0:512], axis=AX.X),
                 waits=[("pe", PE_SUM), ("dv", p.c("dv"))], inc=("dv", 1))
            p.ev("vector", (lambda m: lambda e: e.tensor_scalar(
                scl_sb[:, m:m + 1], row_sb[:, 0:1], 1.0 / (HQ * 8192), None, ALU.mult))(m),
                waits=[("dv", p.c("dv"))], inc=("dv", 1))
            p.ev("vector", (lambda m: lambda e: e.tensor_scalar(
                scl_sb[:, 4 + m:5 + m], scl_sb[:, m:m + 1], 0.5, None, ALU.mult))(m),
                waits=[("dv", p.c("dv"))], inc=("dv", 1))
            p.ev("vector", (lambda m: lambda e: e.tensor_scalar(
                scl_sb[:, 8 + m:9 + m], scl_sb[:, 4 + m:5 + m], -1.0, None, ALU.mult))(m),
                waits=[("dv", p.c("dv"))], inc=("dv", 1))
            DVm = p.c("dv")
            p.ev("tensor", (lambda m: lambda e: e.matmul(
                pB[:, 0:1], oner_sb[:], scl_sb[:, 8 + m:9 + m], start=True, stop=True))(m),
                waits=[("dv", DVm), ("ac", p.c("ac"))], inc=("pe", 1))
            p.ev("scalar", (lambda m: lambda e: e.activation(
                thr_sb[:, 4 + m:5 + m], pB[:, 0:1], AF.Identity))(m),
                waits=[("pe", p.c("pe"))], inc=("ac", 1))
            p.ev("tensor", (lambda m: lambda e: e.matmul(
                pB[:, 0:1], oner_sb[:], scl_sb[:, 4 + m:5 + m], start=True, stop=True))(m),
                waits=[("ac", p.c("ac"))], inc=("pe", 1))
            p.ev("scalar", (lambda m: lambda e: e.activation(
                thr_sb[:, m:m + 1], pB[:, 0:1], AF.Identity))(m),
                waits=[("pe", p.c("pe"))], inc=("ac", 1))

        crep_v = BH[:, 0:16 * 512]
        xmix_v = BH[:, 16 * 512:48 * 512]
        for ch in range(2):
            tc0 = ch * 512
            for h in range(NH):
                for n in range(NS):
                    p.ev("sync", (lambda h, n, tc0: lambda e: e.dma_start(
                        out=cflat_sb, in_=cmat[h * NS + n: h * NS + n + 1, tc0:tc0 + 512]))(h, n, tc0),
                        waits=[("pe", p.c("pe"))], inc=("ld", 16))
                    p.ev("tensor", lambda e: e.matmul(pA[:], oner_sb[:], cflat_sb,
                                                      start=True, stop=True),
                         waits=[("ld", p.c("ld")), ("ac", p.c("ac")), ("dv", p.c("dv"))],
                         inc=("pe", 1))
                    p.ev("scalar", (lambda n: lambda e: e.activation(
                        crep_v[:, n * 512:(n + 1) * 512], pA[:], AF.Identity))(n),
                        waits=[("pe", p.c("pe"))], inc=("ac", 1))
                AC_CREP = p.c("ac")
                for db in range(16):
                    blk = h * 16 + db
                    p.ev("sync", (lambda blk, tc0: lambda e: e.dma_start(
                        out=ctile, in_=cdt[blk * 128:(blk + 1) * 128, tc0:tc0 + 512]))(blk, tc0),
                        waits=[("ac", p.c("ac")), ("dv", p.c("dv"))], inc=("ld", 16))
                    p.ev("sync", (lambda blk, tc0: lambda e: e.dma_start(
                        out=ytile, in_=y_loc[blk * 128:(blk + 1) * 128, tc0:tc0 + 512]))(blk, tc0),
                        inc=("ld", 16))
                    p.ev("sync", (lambda blk, tc0: lambda e: e.dma_start(
                        out=gtile, in_=gate[blk * 128:(blk + 1) * 128, tc0:tc0 + 512]))(blk, tc0),
                        inc=("ld", 16))
                    LD_T = p.c("ld")
                    for n in range(NS):
                        p.ev("scalar", (lambda blk, n: lambda e: e.activation(
                            ptile, ctile, AF.Exp,
                            scale=aneg_sbv[:, blk * NS + n: blk * NS + n + 1]))(blk, n),
                            waits=[("ld", LD_T), ("ac", AC_CREP), ("dv", p.c("dv"))],
                            inc=("ac", 1))
                        p.ev("vector", (lambda blk, n: lambda e: e.scalar_tensor_tensor(
                            g2t, ptile, h0_sbv[:, blk * NS + n: blk * NS + n + 1],
                            crep_v[:, n * 512:(n + 1) * 512], ALU.mult, ALU.mult))(blk, n),
                            waits=[("ac", p.c("ac")), ("dv", DV_H0), ("pe", p.c("pe"))],
                            inc=("dv", 1))
                        p.ev("tensor", (lambda n: lambda e: e.matmul(
                            pB[:], idnb_sb[:], g2t, start=(n == 0), stop=(n == NS - 1)))(n),
                            waits=[("dv", p.c("dv"))], inc=("pe", 1))
                    p.ev("vector", lambda e: e.tensor_add(sc1, ytile, pB[:]),
                         waits=[("pe", p.c("pe"))], inc=("dv", 1))
                    p.ev("vector", (lambda blk: lambda e: e.tensor_tensor(
                        xmix_v[:, blk * 512:(blk + 1) * 512], sc1, gtile, ALU.mult))(blk),
                        waits=[("dv", p.c("dv"))], inc=("dv", 1))
            DV_XMIX = p.c("dv")
            for ob in range(16):
                p.ev("sync", (lambda ob: lambda e: e.dma_start(
                    out=wgs[:].rearrange("p (kb o) -> p kb o", kb=16),
                    in_=opw.rearrange("(kb p) o -> p kb o", p=128)[:, 0:16, ob * 128:(ob + 1) * 128]))(ob),
                    waits=[("pe", p.c("pe"))], inc=("ld", 16))
                p.ev("sync", (lambda ob: lambda e: e.dma_start(
                    out=wus[:].rearrange("p (kb o) -> p kb o", kb=16),
                    in_=opw.rearrange("(kb p) o -> p kb o", p=128)[:, 16:32, ob * 128:(ob + 1) * 128]))(ob),
                    inc=("ld", 16))
                LD_OP = p.c("ld")
                p.ev("gpsimd", lambda e: e.tensor_scalar(
                    wgs[:].bitcast(I32), wgs[:].bitcast(I32), 0x8000, None, ALU.add),
                    waits=[("ld", LD_OP)], inc=("po", 1))
                p.ev("gpsimd", lambda e: e.tensor_scalar(
                    wus[:].bitcast(I32), wus[:].bitcast(I32), 0x8000, None, ALU.add),
                    inc=("po", 1))
                PO_OP = p.c("po")
                p.ev("sync", (lambda ob, tc0: lambda e: e.dma_start(
                    out=x0t_sb[:], in_=x0T[ob * 128:(ob + 1) * 128, tc0:tc0 + 512]))(ob, tc0),
                    inc=("ld", 16))
                LD_X0 = p.c("ld")
                for kb in range(32):
                    buf2 = wgs if kb < 16 else wus
                    kbl = kb % 16
                    p.ev("tensor", (lambda kb, kbl, buf2: lambda e: e.matmul(
                        pC[:, 0:512], bf16_view(buf2[:, kbl * 128:(kbl + 1) * 128]),
                        xmix_v[:, kb * 512:(kb + 1) * 512],
                        start=(kb == 0), stop=(kb == 31)))(kb, kbl, buf2),
                        waits=([("ld", LD_OP), ("dv", DV_XMIX), ("dv", p.c("dv")),
                                ("po", PO_OP)] if kb == 0 else []),
                        inc=(("pe", 1) if kb == 31 else None))
                p.ev("vector", lambda e: e.tensor_add(sc1, pC[:, 0:512], x0t_sb[:]),
                     waits=[("pe", p.c("pe")), ("ld", LD_X0), ("so", p.c("so"))], inc=("dv", 1))
                DV_X1 = p.c("dv")
                p.ev("gpsimd", (lambda ob, tc0: lambda e: e.dma_start(
                    out=x1_i[ob * 128:(ob + 1) * 128, tc0:tc0 + 512], in_=sc1))(ob, tc0),
                    waits=[("dv", DV_X1)], inc=("so", 16))
                if stage == 2:
                    p.ev("gpsimd", (lambda ob, tc0: lambda e: e.dma_start(
                        out=x1dbg[ob * 128:(ob + 1) * 128, tc0:tc0 + 512], in_=sc1))(ob, tc0),
                        inc=("so", 16))
        if stage == 2:
            p.ev("gpsimd", lambda e: None, waits=[("so", p.c("so"))], inc=None)
            with nc.Block() as block:
                p.emit(block, sems)
            return nc

        x1c_v = BHf[:, 0:1024]
        inv_h = 1.0 / HQ
        R_M, R_VAR, R_A, R_B = (i * 512 for i in range(4))
        p.ev("vector", lambda e: e.memset(sxc_sb[:], 0.0), waits=[("dv", p.c("dv"))], inc=("dv", 1))

        def ln_rows(w, pe_stats):
            def rowstep(fn, waits):
                p.ev("vector", fn, waits=waits, inc=("dv", 1))
            rowstep(lambda e: e.tensor_scalar_mul(row_sb[:, R_M:R_M + w], pA[0:1, 0:w], inv_h),
                    [("pe", pe_stats)])
            rowstep(lambda e: e.tensor_scalar_mul(row_sb[:, R_VAR:R_VAR + w], pB[0:1, 0:w], inv_h),
                    [("dv", p.c("dv"))])
            rowstep(lambda e: e.tensor_mul(row_sb[:, R_B:R_B + w], row_sb[:, R_M:R_M + w],
                                           row_sb[:, R_M:R_M + w]), [("dv", p.c("dv"))])
            rowstep(lambda e: e.tensor_sub(row_sb[:, R_VAR:R_VAR + w], row_sb[:, R_VAR:R_VAR + w],
                                           row_sb[:, R_B:R_B + w]), [("dv", p.c("dv"))])
            rowstep(lambda e: e.tensor_scalar_add(row_sb[:, R_VAR:R_VAR + w],
                                                  row_sb[:, R_VAR:R_VAR + w], 1e-5),
                    [("dv", p.c("dv"))])
            p.ev("scalar",
                 lambda e: e.activation(row_sb[:, R_VAR:R_VAR + w], row_sb[:, R_VAR:R_VAR + w],
                                        AF.Sqrt),
                 waits=[("dv", p.c("dv"))], inc=("ac", 1))
            rowstep(lambda e: e.reciprocal(row_sb[:, R_A:R_A + w], row_sb[:, R_VAR:R_VAR + w]),
                    [("ac", p.c("ac"))])
            rowstep(lambda e: e.tensor_mul(row_sb[:, R_B:R_B + w], row_sb[:, R_M:R_M + w],
                                           row_sb[:, R_A:R_A + w]), [("dv", p.c("dv"))])
            rowstep(lambda e: e.tensor_scalar_mul(row_sb[:, R_B:R_B + w], row_sb[:, R_B:R_B + w], -1.0),
                    [("dv", p.c("dv"))])

        for (c0, w) in [(0, 512), (512, 512)]:
            for j in range(16):
                p.ev("sync", (lambda j, c0, w: lambda e: e.dma_start(
                    out=x1c_v[:, 0:w], in_=x1_i[j * 128:(j + 1) * 128, c0:c0 + w]))(j, c0, w),
                    waits=[("so", p.c("so")), ("ac", p.c("ac")), ("pe", p.c("pe")),
                           ("dv", p.c("dv"))], inc=("ld", 16))
                LD_X1 = p.c("ld")
                p.ev("scalar", (lambda w: lambda e: e.activation(
                    sq2_sb[:, 0:w], x1c_v[:, 0:w], AF.Square))(w),
                    waits=[("ld", LD_X1), ("dv", p.c("dv"))], inc=("ac", 1))
                p.ev("tensor", (lambda j, w: lambda e: e.matmul(
                    pA[0:1, 0:w], onec_sb[:], x1c_v[:, 0:w],
                    start=(j == 0), stop=(j == 15)))(j, w),
                    waits=[("ld", LD_X1)], inc=None)
                p.ev("tensor", (lambda j, w: lambda e: e.matmul(
                    pB[0:1, 0:w], onec_sb[:], sq2_sb[:, 0:w],
                    start=(j == 0), stop=(j == 15)))(j, w),
                    waits=[("ac", p.c("ac"))], inc=("pe", 1))
            PE_STATS = p.c("pe")
            ln_rows(w, PE_STATS)
            DV_ROW = p.c("dv")
            p.ev("tensor",
                 lambda e, w=w: e.matmul(pC[:, 0:w], oner_sb[:], row_sb[:, R_A:R_A + w],
                                         start=True, stop=True),
                 waits=[("dv", DV_ROW)], inc=None)
            p.ev("tensor",
                 lambda e, w=w: e.matmul(pC[:, 512:512 + w], oner_sb[:], row_sb[:, R_B:R_B + w],
                                         start=True, stop=True),
                 waits=[], inc=("pe", 1))
            PE_BC = p.c("pe")
            for j in range(16):
                p.ev("sync", (lambda j, c0, w: lambda e: e.dma_start(
                    out=x1c_v[:, 0:w], in_=x1_i[j * 128:(j + 1) * 128, c0:c0 + w]))(j, c0, w),
                    waits=[("dv", p.c("dv")), ("pe", p.c("pe"))], inc=("ld", 16))
                LD_X1B = p.c("ld")
                p.ev("vector", (lambda w: lambda e: e.tensor_mul(
                    sq2_sb[:, 0:w], x1c_v[:, 0:w], pC[:, 0:w]))(w),
                    waits=[("ld", LD_X1B), ("pe", PE_BC), ("ac", p.c("ac"))], inc=("dv", 1))
                p.ev("vector", (lambda w: lambda e: e.tensor_tensor(
                    sc1[0:128, 0:w], sq2_sb[:, 0:w], pC[:, 512:512 + w], ALU.add))(w),
                    waits=[("dv", p.c("dv")), ("pe", p.c("pe"))], inc=("dv", 1))
                DV_XF = p.c("dv")
                p.ev("vector", (lambda j, c0, w: lambda e: e.tensor_copy(
                    qx_sb[:, j * TPC + c0: j * TPC + c0 + w], sc1[0:128, 0:w]))(j, c0, w),
                    waits=[("dv", DV_XF)], inc=("dv", 1))
                for tt in range(4):
                    gt = (c0 // 128) + tt
                    p.ev("tensor", (lambda tt: lambda e: e.transpose(
                        pE[:].bitcast(F32), sc1[:, tt * 128:(tt + 1) * 128], idn_sb[:]))(tt),
                        waits=[("dv", DV_XF), ("dv", p.c("dv"))], inc=("pe", 1))
                    p.ev("vector", lambda e: e.reduce_max(
                        sxc_sb[:, 32:33], pE[:], axis=AX.X, apply_absolute_value=True),
                        waits=[("pe", p.c("pe")), ("dv", p.c("dv"))], inc=("dv", 1))
                    p.ev("vector", (lambda gt: lambda e: e.tensor_max(
                        sxc_sb[:, gt:gt + 1], sxc_sb[:, gt:gt + 1], sxc_sb[:, 32:33]))(gt),
                        waits=[("dv", p.c("dv"))], inc=("dv", 1))
        def colstep(fn):
            p.ev("vector", fn, waits=[("dv", p.c("dv"))], inc=("dv", 1))
        colstep(lambda e: e.tensor_scalar_max(sxc_sb[:, 0:8], sxc_sb[:, 0:8], 1e-5))
        colstep(lambda e: e.tensor_scalar_mul(sxc_sb[:, 8:16], sxc_sb[:, 0:8], 1.0 / 127.0))
        colstep(lambda e: e.reciprocal(sxc_sb[:, 16:24], sxc_sb[:, 8:16]))
        colstep(lambda e: e.tensor_copy(pk_sb[:, 0:8], sxc_sb[:, 16:24]))
        colstep(lambda e: e.tensor_copy(pk_sb[:, 8:16], sxc_sb[:, 8:16]))
        DV_PK = p.c("dv")
        p.ev("tensor", lambda e: e.transpose(pE[0:16, :], pk_sb[:], idn_sb[:]),
             waits=[("dv", DV_PK)], inc=("pe", 1))
        p.ev("scalar", lambda e: e.activation(pr_sb[:], pE[0:16, :], AF.Identity),
             waits=[("pe", p.c("pe"))], inc=("ac", 1))
        AC_PR = p.c("ac")
        for g in range(8):
            p.ev("gpsimd", (lambda g: lambda e: e.dma_start(
                out=BHf[0:1, 2048 + g * 128:2048 + (g + 1) * 128],
                in_=pr_sb[g:g + 1, :]))(g),
                waits=([("ac", AC_PR)] if g == 0 else []), inc=("so", 16))
            p.ev("gpsimd", (lambda g: lambda e: e.dma_start(
                out=BHf[0:1, 3072 + g * 128:3072 + (g + 1) * 128],
                in_=pr_sb[8 + g:9 + g, :]))(g),
                inc=("so", 16))
        SO_SX = p.c("so")
        for ch in range(2):
            tc0 = ch * 512
            p.ev("tensor", (lambda tc0: lambda e: e.matmul(
                pC[:, 0:512], oner_sb[:], BHf[0:1, 2048 + tc0:2048 + tc0 + 512],
                start=True, stop=True))(tc0),
                waits=[("so", SO_SX), ("dv", p.c("dv")), ("ac", p.c("ac"))], inc=("pe", 1))
            PE_SX = p.c("pe")
            for j in range(16):
                p.ev("vector", (lambda j, tc0: lambda e: e.tensor_tensor(
                    sc1, qx_sb[:, j * TPC + tc0: j * TPC + tc0 + 512], pC[:, 0:512], ALU.mult))(j, tc0),
                    waits=[("pe", PE_SX), ("dv", p.c("dv"))], inc=("dv", 1))
                p.ev("vector", (lambda j, tc0: lambda e: e.tensor_scalar(
                    qx_sb[:, j * TPC + tc0: j * TPC + tc0 + 512], sc1, MAGIC, MAGIC,
                    ALU.add, ALU.subtract))(j, tc0),
                    waits=[("dv", p.c("dv"))], inc=("dv", 1))
        DV_QX = p.c("dv")

        p.ev("vector", lambda e: e.tensor_scalar(
            row_sb[:, 0:TPC], BHf[0:1, 3072:3072 + TPC], scl_sb[0:1, 4:5], None, ALU.mult),
            waits=[("dv", p.c("dv"))], inc=("dv", 1))
        p.ev("vector", lambda e: e.tensor_scalar(
            row_sb[:, TPC:2 * TPC], BHf[0:1, 3072:3072 + TPC], scl_sb[0:1, 5:6], None, ALU.mult),
            waits=[("dv", p.c("dv"))], inc=("dv", 1))
        DV_SC = p.c("dv")
        for i4, (r0) in enumerate([0, 512, TPC, TPC + 512]):
            p.ev("tensor", (lambda i4, r0: lambda e: e.matmul(
                pC[:, 0:512], oner_sb[:], row_sb[:, r0:r0 + 512], start=True, stop=True))(i4, r0),
                waits=[("dv", DV_SC), ("ac", p.c("ac"))], inc=("pe", 1))
            p.ev("scalar", (lambda i4: lambda e: e.activation(
                scbc_sb[:, i4 * 512:(i4 + 1) * 512], pC[:, 0:512], AF.Identity))(i4),
                waits=[("pe", p.c("pe"))], inc=("ac", 1))
        AC_SCBC = p.c("ac")

        h_v = BH[:]
        p.ev("vector", lambda e: e.memset(sxc_sb[:, 24:32], 0.0),
             waits=[("dv", p.c("dv"))], inc=("dv", 1))
        for iob in range(64):
            p.ev("sync", (lambda iob: lambda e: e.dma_start(
                out=wgs[:].rearrange("p (kb c) -> p kb c", kb=16),
                in_=wgT.rearrange("(kb p) c -> p kb c", p=128)[:, :, iob * 128:(iob + 1) * 128]))(iob),
                waits=[("pe", p.c("pe")), ("ac", p.c("ac"))], inc=("ld", 16))
            p.ev("sync", (lambda iob: lambda e: e.dma_start(
                out=wus[:].rearrange("p (kb c) -> p kb c", kb=16),
                in_=wuT.rearrange("(kb p) c -> p kb c", p=128)[:, :, iob * 128:(iob + 1) * 128]))(iob),
                inc=("ld", 16))
            LD_W = p.c("ld")
            p.ev("scalar", lambda e: e.activation(qw_sb[:, 0:2048], wgs[:], AF.Sign, bias=thr_sb[:, 4:5]),
                 waits=[("ld", LD_W), ("dv", p.c("dv")), ("pe", p.c("pe"))], inc=("ac", 1))
            p.ev("scalar", lambda e: e.activation(qs2, wgs[:], AF.Sign, bias=thr_sb[:, 0:1]),
                 inc=("ac", 1))
            p.ev("vector", lambda e: e.tensor_add(qw_sb[:, 0:2048], qw_sb[:, 0:2048], qs2),
                 waits=[("ac", p.c("ac"))], inc=("dv", 1))
            p.ev("scalar", lambda e: e.activation(qw_sb[:, 2048:4096], wus[:], AF.Sign, bias=thr_sb[:, 5:6]),
                 waits=[("dv", p.c("dv"))], inc=("ac", 1))
            p.ev("scalar", lambda e: e.activation(qs2, wus[:], AF.Sign, bias=thr_sb[:, 1:2]),
                 inc=("ac", 1))
            p.ev("vector", lambda e: e.tensor_add(qw_sb[:, 2048:4096], qw_sb[:, 2048:4096], qs2),
                 waits=[("ac", p.c("ac"))], inc=("dv", 1))
            DV_QW = p.c("dv")
            for kb in range(16):
                p.ev("tensor", (lambda kb: lambda e: e.matmul(
                    pA[:], qw_sb[:, kb * 128:(kb + 1) * 128],
                    qx_sb[:, kb * TPC:kb * TPC + 512],
                    start=(kb == 0), stop=(kb == 15)))(kb),
                    waits=([("dv", DV_QW), ("dv", DV_QX)] if kb == 0 else []), inc=None)
                p.ev("tensor", (lambda kb: lambda e: e.matmul(
                    pB[:], qw_sb[:, kb * 128:(kb + 1) * 128],
                    qx_sb[:, kb * TPC + 512:kb * TPC + 1024],
                    start=(kb == 0), stop=(kb == 15)))(kb), inc=None)
                p.ev("tensor", (lambda kb: lambda e: e.matmul(
                    pC[:, 0:512], qw_sb[:, 2048 + kb * 128:2048 + (kb + 1) * 128],
                    qx_sb[:, kb * TPC:kb * TPC + 512],
                    start=(kb == 0), stop=(kb == 15)))(kb), inc=None)
                p.ev("tensor", (lambda kb: lambda e: e.matmul(
                    pC[:, 512:1024], qw_sb[:, 2048 + kb * 128:2048 + (kb + 1) * 128],
                    qx_sb[:, kb * TPC + 512:kb * TPC + 1024],
                    start=(kb == 0), stop=(kb == 15)))(kb),
                    inc=(("pe", 1) if kb == 15 else None))
            PE_GU = p.c("pe")
            for c in range(2):
                gps = [pA, pB][c]
                ups_c0 = 512 * c
                p.ev("vector", (lambda gps, c: lambda e: e.tensor_tensor(
                    sc1, gps[:], scbc_sb[:, c * 512:(c + 1) * 512], ALU.mult))(gps, c),
                    waits=[("pe", PE_GU), ("dv", p.c("dv")), ("ac", AC_SCBC)], inc=("dv", 1))
                p.ev("scalar", lambda e: e.activation(sc3[:], sc1, AF.Silu),
                     waits=[("dv", p.c("dv"))], inc=("ac", 1))
                p.ev("vector", (lambda ups_c0, c: lambda e: e.tensor_tensor(
                    sc2, pC[:, ups_c0:ups_c0 + 512], scbc_sb[:, 1024 + c * 512:1024 + (c + 1) * 512],
                    ALU.mult))(ups_c0, c),
                    waits=[("dv", p.c("dv"))], inc=("dv", 1))
                p.ev("vector", (lambda iob, c: lambda e: e.tensor_tensor(
                    h_v[:, iob * TPC + c * 512: iob * TPC + (c + 1) * 512], sc3[:], sc2,
                    ALU.mult))(iob, c),
                    waits=[("ac", p.c("ac")), ("dv", p.c("dv"))], inc=("dv", 1))
                DV_H = p.c("dv")
                p.ev("scalar", (lambda iob, c: lambda e: e.activation(
                    sq2_sb[:], h_v[:, iob * TPC + c * 512: iob * TPC + (c + 1) * 512], AF.Square))(iob, c),
                    waits=[("dv", DV_H)], inc=("ac", 1))
                p.ev("tensor", (lambda iob, c: lambda e: e.matmul(
                    pD[0:1, c * 512:(c + 1) * 512], onec_sb[:], sq2_sb[:],
                    start=(iob == 0), stop=(iob == 63)))(iob, c),
                    waits=[("ac", p.c("ac"))], inc=("pe", 1))
                p.ev("vector", (lambda iob, c: lambda e: e.tensor_scalar(
                    sc1, h_v[:, iob * TPC + c * 512: iob * TPC + (c + 1) * 512],
                    wln_sb[:, iob:iob + 1], None, ALU.mult))(iob, c),
                    waits=[("dv", p.c("dv"))], inc=("dv", 1))
                DV_HW = p.c("dv")
                for tt in range(4):
                    gt = c * 4 + tt
                    p.ev("tensor", (lambda tt: lambda e: e.transpose(
                        pE[:].bitcast(F32), sc1[:, tt * 128:(tt + 1) * 128], idn_sb[:]))(tt),
                        waits=[("dv", DV_HW), ("dv", p.c("dv"))], inc=("pe", 1))
                    p.ev("vector", lambda e: e.reduce_max(
                        sxc_sb[:, 32:33], pE[:], axis=AX.X, apply_absolute_value=True),
                        waits=[("pe", p.c("pe")), ("dv", p.c("dv"))], inc=("dv", 1))
                    p.ev("vector", (lambda gt: lambda e: e.tensor_max(
                        sxc_sb[:, 24 + gt:25 + gt], sxc_sb[:, 24 + gt:25 + gt], sxc_sb[:, 32:33]))(gt),
                        waits=[("dv", p.c("dv"))], inc=("dv", 1))
        PE_SQ = p.c("pe")
        p.ev("vector", lambda e: e.tensor_scalar_mul(row_sb[:, 0:1024], pD[0:1, 0:1024], 1.0 / 8192.0),
             waits=[("pe", PE_SQ), ("dv", p.c("dv"))], inc=("dv", 1))
        p.ev("vector", lambda e: e.tensor_scalar_add(row_sb[:, 0:1024], row_sb[:, 0:1024], 1e-6),
             waits=[("dv", p.c("dv"))], inc=("dv", 1))
        p.ev("scalar", lambda e: e.activation(row_sb[:, 0:1024], row_sb[:, 0:1024], AF.Sqrt),
             waits=[("dv", p.c("dv"))], inc=("ac", 1))
        p.ev("vector", lambda e: e.reciprocal(row_sb[:, 1024:2048], row_sb[:, 0:1024]),
             waits=[("ac", p.c("ac"))], inc=("dv", 1))
        colstep(lambda e: e.tensor_scalar_max(sxc_sb[:, 24:32], sxc_sb[:, 24:32], 1e-5))
        colstep(lambda e: e.tensor_scalar_mul(sxc_sb[:, 40:48], sxc_sb[:, 24:32], 1.0 / 127.0))
        colstep(lambda e: e.reciprocal(sxc_sb[:, 32:40], sxc_sb[:, 40:48]))
        colstep(lambda e: e.tensor_copy(pk_sb[:, 0:8], sxc_sb[:, 32:40]))
        colstep(lambda e: e.tensor_copy(pk_sb[:, 8:16], sxc_sb[:, 40:48]))
        DV_PK2 = p.c("dv")
        p.ev("tensor", lambda e: e.transpose(pE[0:16, :], pk_sb[:], idn_sb[:]),
             waits=[("dv", DV_PK2)], inc=("pe", 1))
        p.ev("scalar", lambda e: e.activation(pr_sb[:], pE[0:16, :], AF.Identity),
             waits=[("pe", p.c("pe"))], inc=("ac", 1))
        AC_PR2 = p.c("ac")
        for g in range(8):
            p.ev("gpsimd", (lambda g: lambda e: e.dma_start(
                out=qxf[0:1, g * 128:(g + 1) * 128],
                in_=pr_sb[g:g + 1, :]))(g),
                waits=([("ac", AC_PR2), ("so", p.c("so")), ("pe", p.c("pe"))] if g == 0 else []),
                inc=("so", 16))
            p.ev("gpsimd", (lambda g: lambda e: e.dma_start(
                out=qxf[0:1, TPC + g * 128:TPC + (g + 1) * 128],
                in_=pr_sb[8 + g:9 + g, :]))(g),
                inc=("so", 16))
        SO_SH = p.c("so")
        p.ev("vector", lambda e: e.tensor_mul(row_sb[:, 0:TPC], row_sb[:, 1024:2048],
                                              qxf[0:1, 0:TPC]),
             waits=[("so", SO_SH), ("dv", p.c("dv"))], inc=("dv", 1))
        p.ev("vector", lambda e: e.tensor_scalar(
            row_sb[:, TPC:2 * TPC], qxf[0:1, TPC:2 * TPC], scl_sb[0:1, 6:7], None, ALU.mult),
            waits=[("dv", p.c("dv"))], inc=("dv", 1))
        DV_RS = p.c("dv")
        for i4, r0 in enumerate([0, 512, TPC, TPC + 512]):
            p.ev("tensor", (lambda r0: lambda e: e.matmul(
                pC[:, 0:512], oner_sb[:], row_sb[:, r0:r0 + 512], start=True, stop=True))(r0),
                waits=[("dv", DV_RS), ("ac", p.c("ac"))], inc=("pe", 1))
            p.ev("scalar", (lambda i4: lambda e: e.activation(
                scbc_sb[:, i4 * 512:(i4 + 1) * 512], pC[:, 0:512], AF.Identity))(i4),
                waits=[("pe", p.c("pe"))], inc=("ac", 1))
        AC_RSBC = p.c("ac")
        for iob in range(64):
            for c in range(2):
                p.ev("vector", (lambda iob, c: lambda e: e.scalar_tensor_tensor(
                    sc1, h_v[:, iob * TPC + c * 512: iob * TPC + (c + 1) * 512],
                    wln_sb[:, iob:iob + 1], scbc_sb[:, c * 512:(c + 1) * 512],
                    ALU.mult, ALU.mult))(iob, c),
                    waits=[("ac", AC_RSBC), ("dv", p.c("dv"))], inc=("dv", 1))
                p.ev("vector", (lambda iob, c: lambda e: e.tensor_scalar(
                    h_v[:, iob * TPC + c * 512: iob * TPC + (c + 1) * 512], sc1,
                    MAGIC, MAGIC, ALU.add, ALU.subtract))(iob, c),
                    waits=[("dv", p.c("dv"))], inc=("dv", 1))
        DV_QH = p.c("dv")

        for ob in range(16):
            for c in range(2):
                tc0 = c * 512
                for obq in range(4):
                    p.ev("sync", (lambda ob, obq: lambda e: e.dma_start(
                        out=wgs[:].rearrange("p (kb o) -> p kb o", kb=16),
                        in_=wdT.rearrange("(kb p) o -> p kb o", p=128)[:, obq * 16:(obq + 1) * 16,
                                                                      ob * 128:(ob + 1) * 128]))(ob, obq),
                        waits=[("pe", p.c("pe")), ("ac", p.c("ac"))], inc=("ld", 16))
                    LD_WD = p.c("ld")
                    p.ev("scalar", lambda e: e.activation(qw_sb[:, 0:2048], wgs[:], AF.Sign, bias=thr_sb[:, 6:7]),
                         waits=[("ld", LD_WD), ("dv", p.c("dv")), ("pe", p.c("pe"))], inc=("ac", 1))
                    p.ev("scalar", lambda e: e.activation(qs2, wgs[:], AF.Sign, bias=thr_sb[:, 2:3]),
                         inc=("ac", 1))
                    p.ev("vector", lambda e: e.tensor_add(qw_sb[:, 0:2048], qw_sb[:, 0:2048], qs2),
                         waits=[("ac", p.c("ac"))], inc=("dv", 1))
                    DV_QD = p.c("dv")
                    for kbl in range(16):
                        kb = obq * 16 + kbl
                        p.ev("tensor", (lambda kb, kbl, tc0: lambda e: e.matmul(
                            pA[:], qw_sb[:, kbl * 128:(kbl + 1) * 128],
                            h_v[:, kb * TPC + tc0: kb * TPC + tc0 + 512],
                            start=(kb == 0), stop=(kb == 63)))(kb, kbl, tc0),
                            waits=([("dv", DV_QD), ("dv", DV_QH)] if kbl == 0 else []),
                            inc=(("pe", 1) if kbl == 15 else None))
                PE_DN = p.c("pe")
                p.ev("sync", (lambda ob, tc0: lambda e: e.dma_start(
                    out=x0t_sb[:], in_=x1_i[ob * 128:(ob + 1) * 128, tc0:tc0 + 512]))(ob, tc0),
                    waits=[("dv", p.c("dv"))], inc=("ld", 16))
                LD_X1T = p.c("ld")
                p.ev("vector", (lambda c: lambda e: e.tensor_tensor(
                    sc1, pA[:], scbc_sb[:, 1024 + c * 512:1024 + (c + 1) * 512], ALU.mult))(c),
                    waits=[("pe", PE_DN), ("dv", p.c("dv")), ("ac", AC_RSBC)], inc=("dv", 1))
                p.ev("vector", lambda e: e.tensor_add(sc2, sc1, x0t_sb[:]),
                     waits=[("ld", LD_X1T), ("dv", p.c("dv")), ("so", p.c("so"))], inc=("dv", 1))
                DV_X2 = p.c("dv")
                p.ev("gpsimd", (lambda ob, tc0: lambda e: e.dma_start(
                    out=x2o[ob * 128:(ob + 1) * 128, tc0:tc0 + 512], in_=sc2))(ob, tc0),
                    waits=[("dv", DV_X2)], inc=("so", 16))

        x2c_v = BHf[:, 0:1024]
        cos_v = BHf[:, 1024:2048]
        sin_v = BHf[:, 2048:3072]
        rop_v = BHf[:, 3072:4096]
        qf_v = BHf[:, 4096:5120]
        vf_v = BHf[:, 5120:6144]
        a_v = BHf[:, 6144:7168]
        p.ev("sync", lambda e: e.dma_start(out=cos_v, in_=cosT[:]),
             waits=[("pe", p.c("pe")), ("dv", p.c("dv")), ("so", p.c("so"))], inc=("ld", 16))
        p.ev("sync", lambda e: e.dma_start(out=sin_v, in_=sinT[:]), inc=("ld", 16))
        LD_CS = p.c("ld")
        for (c0, w) in [(0, 512), (512, 512)]:
            for j in range(16):
                p.ev("sync", (lambda j, c0, w: lambda e: e.dma_start(
                    out=x2c_v[:, 0:w], in_=x2o[j * 128:(j + 1) * 128, c0:c0 + w]))(j, c0, w),
                    waits=[("so", p.c("so")), ("ac", p.c("ac")), ("pe", p.c("pe")),
                           ("dv", p.c("dv"))], inc=("ld", 16))
                LD_X2 = p.c("ld")
                p.ev("scalar", (lambda w: lambda e: e.activation(
                    sq2_sb[:, 0:w], x2c_v[:, 0:w], AF.Square))(w),
                    waits=[("ld", LD_X2), ("dv", p.c("dv"))], inc=("ac", 1))
                p.ev("tensor", (lambda j, w: lambda e: e.matmul(
                    pA[0:1, 0:w], onec_sb[:], x2c_v[:, 0:w],
                    start=(j == 0), stop=(j == 15)))(j, w),
                    waits=[("ld", LD_X2)], inc=None)
                p.ev("tensor", (lambda j, w: lambda e: e.matmul(
                    pB[0:1, 0:w], onec_sb[:], sq2_sb[:, 0:w],
                    start=(j == 0), stop=(j == 15)))(j, w),
                    waits=[("ac", p.c("ac"))], inc=("pe", 1))
            PE_ST2 = p.c("pe")
            ln_rows(w, PE_ST2)
            DV_ROW2 = p.c("dv")
            p.ev("tensor",
                 lambda e, w=w: e.matmul(pC[:, 0:w], oner_sb[:], row_sb[:, R_A:R_A + w],
                                         start=True, stop=True),
                 waits=[("dv", DV_ROW2)], inc=None)
            p.ev("tensor",
                 lambda e, w=w: e.matmul(pC[:, 512:512 + w], oner_sb[:], row_sb[:, R_B:R_B + w],
                                         start=True, stop=True),
                 waits=[], inc=("pe", 1))
            PE_BC2 = p.c("pe")
            for j in range(16):
                p.ev("sync", (lambda j, c0, w: lambda e: e.dma_start(
                    out=x2c_v[:, 0:w], in_=x2o[j * 128:(j + 1) * 128, c0:c0 + w]))(j, c0, w),
                    waits=[("dv", p.c("dv")), ("pe", p.c("pe"))], inc=("ld", 16))
                LD_X2B = p.c("ld")
                p.ev("vector", (lambda w: lambda e: e.tensor_mul(
                    sq2_sb[:, 0:w], x2c_v[:, 0:w], pC[:, 0:w]))(w),
                    waits=[("ld", LD_X2B), ("pe", PE_BC2)], inc=("dv", 1))
                p.ev("vector", (lambda j, c0, w: lambda e: e.tensor_tensor(
                    qx_sb[:, j * TPC + c0: j * TPC + c0 + w], sq2_sb[:, 0:w],
                    pC[:, 512:512 + w], ALU.add))(j, c0, w),
                    waits=[("dv", p.c("dv"))], inc=("dv", 1))
        DV_LN2 = p.c("dv")

        for wsrc, dst, mode in [(wq, qO, "rope"), (wk, kO, "rope"), (wv, vO, "v")]:
            for ob in range(16):
                p.ev("sync", (lambda wsrc, ob: lambda e: e.dma_start(
                    out=wgs[:].rearrange("p (kb o) -> p kb o", kb=16),
                    in_=wsrc.rearrange("(kb p) o -> p kb o", p=128)[:, :, ob * 128:(ob + 1) * 128]))(wsrc, ob),
                    waits=[("pe", p.c("pe"))], inc=("ld", 16))
                LD_QW = p.c("ld")
                p.ev("gpsimd", lambda e: e.tensor_scalar(
                    wgs[:].bitcast(I32), wgs[:].bitcast(I32), 0x8000, None, ALU.add),
                    waits=[("ld", LD_QW)], inc=("po", 1))
                PO_QW = p.c("po")
                for kb in range(16):
                    p.ev("tensor", (lambda kb: lambda e: e.matmul(
                        pC[:, 0:512], bf16_view(wgs[:, kb * 128:(kb + 1) * 128]),
                        qx_sb[:, kb * TPC:kb * TPC + 512],
                        start=(kb == 0), stop=(kb == 15)))(kb),
                        waits=([("ld", LD_QW), ("dv", DV_LN2), ("ac", p.c("ac")),
                                ("dv", p.c("dv")), ("po", PO_QW)] if kb == 0 else []), inc=None)
                    p.ev("tensor", (lambda kb: lambda e: e.matmul(
                        pC[:, 512:1024], bf16_view(wgs[:, kb * 128:(kb + 1) * 128]),
                        qx_sb[:, kb * TPC + 512:kb * TPC + 1024],
                        start=(kb == 0), stop=(kb == 15)))(kb),
                        inc=(("pe", 1) if kb == 15 else None))
                PE_QKV = p.c("pe")
                if mode == "rope":
                    p.ev("scalar", lambda e: e.activation(qf_v, pC[:, 0:1024], AF.Identity),
                         waits=[("pe", PE_QKV), ("ld", LD_CS), ("dv", p.c("dv"))], inc=("ac", 1))
                    AC_QF = p.c("ac")
                    p.ev("vector", lambda e: e.tensor_scalar_mul(rop_v[0:64, :], qf_v[64:128, :], -1.0),
                         waits=[("ac", AC_QF), ("dv", p.c("dv"))], inc=("dv", 1))
                    p.ev("vector", lambda e: e.tensor_copy(rop_v[64:128, :], qf_v[0:64, :]),
                         waits=[("ac", AC_QF)], inc=("dv", 1))
                    p.ev("vector", lambda e: e.tensor_mul(a_v, qf_v, cos_v),
                         waits=[("ac", AC_QF)], inc=("dv", 1))
                    p.ev("vector", lambda e: e.tensor_mul(rop_v, rop_v, sin_v),
                         waits=[("dv", p.c("dv"))], inc=("dv", 1))
                    p.ev("vector", lambda e: e.tensor_add(a_v, a_v, rop_v),
                        waits=[("dv", p.c("dv"))], inc=("dv", 1))
                    p.ev("vector", lambda e: e.tensor_copy(sc3[:, 0:512], a_v[:, 0:512]),
                         waits=[("dv", p.c("dv")), ("so", p.c("so"))], inc=("dv", 1))
                    p.ev("vector", lambda e: e.tensor_copy(gtile, a_v[:, 512:1024]),
                         inc=("dv", 1))
                    DV_RP = p.c("dv")
                    p.ev("gpsimd", (lambda dst, ob: lambda e: e.dma_start(
                        out=dst[ob * 128:(ob + 1) * 128, 0:512], in_=sc3[:, 0:512]))(dst, ob),
                        waits=[("dv", DV_RP)], inc=("so", 16))
                    p.ev("gpsimd", (lambda dst, ob: lambda e: e.dma_start(
                        out=dst[ob * 128:(ob + 1) * 128, 512:1024], in_=gtile))(dst, ob),
                        inc=("so", 16))
                else:
                    p.ev("scalar", lambda e: e.activation(vf_v, pC[:, 0:1024], AF.Identity),
                         waits=[("pe", PE_QKV), ("dv", p.c("dv"))], inc=("ac", 1))
                    AC_VF = p.c("ac")
                    for tt in range(8):
                        p.ev("tensor", (lambda tt: lambda e: e.transpose(
                            pE[:].bitcast(F32), vf_v[:, tt * 128:(tt + 1) * 128], idn_sb[:]))(tt),
                            waits=[("ac", AC_VF), ("so", p.c("so")), ("ac", p.c("ac"))], inc=("pe", 1))
                        p.ev("scalar", lambda e: e.activation(sc3[:, 0:128], pE[:], AF.Identity),
                             waits=[("pe", p.c("pe")), ("so", p.c("so"))], inc=("ac", 1))
                        AC_VT = p.c("ac")
                        p.ev("gpsimd", (lambda ob, tt: lambda e: e.dma_start(
                            out=vO[tt * 128:(tt + 1) * 128, ob * 128:(ob + 1) * 128],
                            in_=sc3[:, 0:128]))(ob, tt),
                            waits=[("ac", AC_VT)], inc=("so", 16))
        p.ev("gpsimd", lambda e: None, waits=[("so", p.c("so"))], inc=None)

        with nc.Block() as block:
            p.emit(block, sems)
    return nc


def prep_B(ins, ares):
    opwT = np.ascontiguousarray(ins["L0_attn_out_proj_w"].T)
    wgT = np.ascontiguousarray(ins["L0_ffn_w_gate"].T)
    wuT = np.ascontiguousarray(ins["L0_ffn_w_up"].T)
    wdT = np.ascontiguousarray(ins["L0_ffn_w_down"].T)
    wlnc = np.ascontiguousarray(ins["L0_ffn_ln_weight"].reshape(64, 128).T)
    wq = np.ascontiguousarray(ins["L1_attn_Wq"])
    wk = np.ascontiguousarray(ins["L1_attn_Wk"])
    wv = np.ascontiguousarray(ins["L1_attn_Wv"])
    alog = None
    x0 = ins["L0_attn_hidden_states"]

    def blkmaj(a, inner):
        return np.ascontiguousarray(
            a.reshape(32, 128, inner).transpose(1, 0, 2).reshape(128, 32 * inner))
    alog = blkmaj(np.ascontiguousarray(ins["L0_attn_A_log"].reshape(IQ, NS)), NS)

    hd = 128
    inv_freq = 1.0 / (10000.0 ** (np.arange(0, hd, 2, dtype=np.float64)[: hd // 2] / hd))
    pos = np.arange(SQ, dtype=np.float64)
    ang = pos[:, None] * inv_freq[None, :]
    ang = np.concatenate([ang, ang], axis=-1)
    cosf = np.cos(ang).astype(np.float32).T
    sinf = np.sin(ang).astype(np.float32).T
    onec = np.ones((128, 1), np.float32)
    oner = np.ones((1, 128), np.float32)
    idn = np.eye(128, dtype=np.float32)

    in_maps = []
    for c in range(8):
        b, q = c // 4, c % 4
        t0 = q * TPC
        hprev = np.zeros((3 * 128, 32 * NS), np.float32)
        cprev = np.zeros((3 * 128, 32), np.float32)
        for slot in range(3):
            j = q - 3 + slot
            if j >= 0:
                src = ares[4 * b + j]
                hprev[slot * 128:(slot + 1) * 128] = np.asarray(src["hfin"], np.float32)
                cprev[slot * 128:(slot + 1) * 128] = np.asarray(src["cdtfin"], np.float32)
        r = ares[c]
        in_maps.append({
            "y_loc": np.asarray(r["y_loc"], np.float32),
            "cdt": np.asarray(r["cdt_o"], np.float32),
            "cmat": np.asarray(r["cmat"], np.float32),
            "gate": np.asarray(r["gate_o"]),
            "hprev": hprev, "cprev": cprev, "alog": alog,
            "x0T": np.ascontiguousarray(x0[b, t0:t0 + TPC, :].T),
            "opw": opwT, "wgT": wgT, "wuT": wuT, "wdT": wdT, "wlnc": wlnc,
            "wq": wq, "wk": wk, "wv": wv,
            "cosT": np.ascontiguousarray(cosf[:, t0:t0 + TPC]),
            "sinT": np.ascontiguousarray(sinf[:, t0:t0 + TPC]),
            "onec": onec, "oner": oner, "idn": idn,
        })
    return in_maps





def _finish_C(nc, p, sems, outO, x3_sb, es):
    for ob in range(16):
        p.ev("gpsimd", (lambda ob: lambda e: e.dma_start(
            out=outO[ob * 128:(ob + 1) * 128, :], in_=x3_sb[:, ob * TPC:(ob + 1) * TPC]))(ob),
            waits=([("dv", p.c("dv")), ("ac", p.c("ac")), ("pe", p.c("pe"))] if ob == 0 else []),
            inc=("so", 16))
    p.ev("gpsimd", lambda e: None, waits=[("so", p.c("so"))], inc=None)
    with nc.Block() as block:
        p.emit(block, sems)
    es.close()
    return nc

def build_C(stop=None):
    nc = bass.Bass("TRN2", target_bir_lowering=False)
    IN = lambda n, s, dt=F32: nc.dram_tensor(n, s, dt, kind="ExternalInput").ap()
    OUT = lambda n, s, dt=F32: nc.dram_tensor(n, s, dt, kind="ExternalOutput").ap()

    x2T = IN("x2T", [HQ, TPC])
    qT = IN("qT", [HQ, TPC], BF16)
    kF = IN("kF", [HQ, SQ], BF16)
    vF = IN("vF", [SQ, HQ], BF16)
    wo = IN("wo", [HQ, HQ])
    wr = IN("wr", [HQ, 8])
    wgm = IN("wgm", [8 * HQ, 1408])
    wum = IN("wum", [8 * HQ, 1408])
    wdm = IN("wdm", [8 * 1408, HQ])
    maskS = IN("maskS", [128, 512])
    maskTT = IN("maskTT", [128, 512])
    onec = IN("onec", [128, 1])
    oner = IN("oner", [1, 128])
    idn = IN("idn", [128, 128])
    outO = OUT("outO", [HQ, TPC])

    ISQ = 1.0 / float(np.sqrt(np.float32(128.0)))

    p = Prog()
    with ExitStack() as es:
        sb = lambda name, shape, dt=F32: es.enter_context(nc.sbuf_tensor(name, shape, dt))
        psalloc = lambda name, shape, dt=F32: es.enter_context(nc.psum_tensor(name, shape, dt))
        sems = {}
        for s in ["ld", "so", "pe", "ac", "dv", "po"]:
            sems[s] = es.enter_context(nc.semaphore(s))

        x3_sb = sb("x3_sb", [128, 16 * TPC])
        cxf_sb = sb("cxf_sb", [128, 16 * TPC], BF16)
        hf_v = cxf_sb[:, 0:11 * TPC]
        xln3_sb = sb("xln3_sb", [128, 16 * TPC], BF16)
        kv_sb = sb("kv_sb", [128, 2 * SQ + TPC], BF16)
        kv2_sb = sb("kv2_sb", [128, 2 * SQ + TPC], BF16)
        crows_sb = kv_sb[0:1, 0:8 * TPC]
        wp_sb = sb("wp_sb", [128, 4096])
        etf_sb = sb("etf_sb", [128, 2 * TPC])
        etb_sb = sb("etb_sb", [128, 2 * TPC], BF16)
        fac_sb = sb("fac_sb", [128, TPC], BF16)
        mask_sb = sb("mask_sb", [128, 512])
        maskt_sb = sb("maskt_sb", [128, 512])
        sm_sb = sb("sm_sb", [128, 128])
        sc1 = sb("c_sc1", [128, 512])
        sc3 = etb_sb[:, 0:512]
        cbc_sb = sb("cbc_sb", [128, 1024], BF16)
        rsb_sb = sb("rsb_sb", [8, 1024])
        rT_sb = etf_sb[:, 0:64]
        cmb_sb = etf_sb[:, 64:128]
        cmt_sb = etf_sb[0:64, 128:192].bitcast(BF16)
        onerb_sb = etb_sb[0:1, 512:640]
        mcol_sb = sb("mcol_sb", [128, 32])
        prow_sb = x3_sb[0:1, 0:TPC]
        pk_sb = sb("pk_sb", [128, 8])
        row_sb = etf_sb[0:1, 0:2048]
        onec_sb = sb("c_onec", [128, 1])
        oner_sb = sb("c_oner", [1, 128])
        idn_sb = sb("c_idn", [128, 128])

        pA = psalloc("pA", [128, 512])
        pB = psalloc("pB", [128, 512])
        pC = psalloc("pC", [128, 1024])
        pD = psalloc("pD", [128, 1024])
        pE = psalloc("pE", [128, 1024])

        mrun = lambda s: mcol_sb[:, s:s + 1]
        lrun = lambda s: mcol_sb[:, 8 + s:9 + s]
        facc = lambda s: mcol_sb[:, 16 + s:17 + s]
        tmpc = mcol_sb[:, 24:25]
        isqc = mcol_sb[:, 26:27]

        for dst, src2 in [(onec_sb[:], onec[:]), (oner_sb[:], oner[:]), (idn_sb[:], idn[:]),
                          (mask_sb[:], maskS[:]), (maskt_sb[:], maskTT[:])]:
            p.ev("sync", (lambda d, s: lambda e: e.dma_start(out=d, in_=s))(dst, src2),
                 inc=("ld", 16))
        LD0 = p.c("ld")
        p.ev("vector", lambda e: e.memset(isqc, ISQ), waits=[("ld", LD0)], inc=("dv", 1))
        p.ev("vector", lambda e: e.tensor_copy(onerb_sb, oner_sb[:]), inc=("dv", 1))

        if stop == "ld":
            return _finish_C(nc, p, sems, outO, x3_sb, es)
        gAB = {0: {"dv": 0, "ac": 0}, 1: {"dv": 0, "ac": 0}}
        gE = {0: {"dv": 0, "ac": 0}, 1: {"dv": 0, "ac": 0}}
        gD = {"ac": 0}
        pe_ctx_done = {0: 0, 1: 0}
        for h in range(16):
            kvb = [kv_sb, kv2_sb][h % 2]
            kview = kvb[:, 0:SQ]
            vview = kvb[:, SQ:2 * SQ]
            qview = kvb[:, 2 * SQ:2 * SQ + TPC]
            p.ev("sync", (lambda h, kvb: lambda e: e.dma_start(
                out=kvb[:, 0:SQ], in_=kF[h * 128:(h + 1) * 128, :]))(h, kvb),
                waits=[("pe", p.c("pe")), ("dv", p.c("dv"))], inc=("ld", 16))
            p.ev("sync", (lambda h, kvb: lambda e: e.dma_start(
                out=kvb[:, SQ:2 * SQ].rearrange("p (kc d) -> p kc d", d=128),
                in_=vF[:, h * 128:(h + 1) * 128].rearrange("(kc p) d -> p kc d", p=128)))(h, kvb),
                inc=("ld", 16))
            p.ev("sync", (lambda h, kvb: lambda e: e.dma_start(
                out=kvb[:, 2 * SQ:2 * SQ + TPC], in_=qT[h * 128:(h + 1) * 128, :]))(h, kvb),
                inc=("ld", 16))
            LD_KV = p.c("ld")
            for slot in range(8):
                p.ev("vector", (lambda slot: lambda e: e.memset(mrun(slot), -1e30))(slot),
                     waits=[("dv", p.c("dv"))], inc=("dv", 1))
                for kb in range(slot + 1):
                    par = kb % 2
                    ps = [pA, pB][par]
                    p.ev("tensor", (lambda slot, kb, qview, kview, ps: lambda e: e.matmul(
                        ps[:], qview[:, slot * 128:(slot + 1) * 128],
                        kview[:, kb * 512:(kb + 1) * 512], start=True, stop=True))(slot, kb, qview, kview, ps),
                        waits=[("ld", LD_KV), ("dv", gAB[par]["dv"]), ("ac", gAB[par]["ac"])],
                        inc=("pe", 1))
                    PE_S = p.c("pe")
                    if kb == slot:
                        p.ev("vector", (lambda ps: lambda e: e.tensor_add(sc1[:], ps[:], mask_sb[:]))(ps),
                             waits=[("pe", PE_S), ("dv", p.c("dv"))], inc=("dv", 1))
                        p.ev("vector", lambda e: e.reduce_max(tmpc, sc1[:], axis=AX.X),
                             waits=[("dv", p.c("dv"))], inc=("dv", 1))
                    else:
                        p.ev("vector", (lambda ps: lambda e: e.reduce_max(tmpc, ps[:], axis=AX.X))(ps),
                             waits=[("pe", PE_S), ("dv", p.c("dv"))], inc=("dv", 1))
                    gAB[par]["dv"] = p.c("dv")
                    p.ev("vector", (lambda slot: lambda e: e.tensor_max(
                        mrun(slot), mrun(slot), tmpc))(slot),
                        waits=[("dv", p.c("dv"))], inc=("dv", 1))
                if stop == "p1":
                    continue
                p.ev("vector", (lambda slot: lambda e: e.tensor_scalar_mul(
                    tmpc, mrun(slot), -ISQ))(slot),
                    waits=[("dv", p.c("dv"))], inc=("dv", 1))
                p.ev("vector", (lambda slot: lambda e: e.memset(lrun(slot), 0.0))(slot), inc=("dv", 1))
                DV_BN = p.c("dv")
                for kb in range(slot + 1):
                    par = kb % 2
                    ps = [pA, pB][par]
                    p.ev("tensor", (lambda slot, kb, qview, kview, ps: lambda e: e.matmul(
                        ps[:], qview[:, slot * 128:(slot + 1) * 128],
                        kview[:, kb * 512:(kb + 1) * 512], start=True, stop=True))(slot, kb, qview, kview, ps),
                        waits=[("ac", gAB[par]["ac"]), ("dv", max(DV_BN, gAB[par]["dv"]))],
                        inc=("pe", 1))
                    PE_S2 = p.c("pe")
                    if kb == slot:
                        p.ev("vector", (lambda ps: lambda e: e.tensor_add(sc1[:], ps[:], mask_sb[:]))(ps),
                             waits=[("pe", PE_S2), ("dv", p.c("dv")), ("ac", p.c("ac"))], inc=("dv", 1))
                        gAB[par]["dv"] = p.c("dv")
                        p.ev("scalar", (lambda slot: lambda e: e.activation(
                            etf_sb[:, 0:512], sc1[:], AF.Exp, scale=isqc, bias=tmpc,
                            accum_out=mcol_sb[:, 25:26]))(slot),
                            waits=[("dv", p.c("dv"))], inc=("ac", 1))
                        gAB[par]["ac"] = p.c("ac")
                    else:
                        p.ev("scalar", (lambda ps, slot: lambda e: e.activation(
                            etf_sb[:, 0:512], ps[:], AF.Exp, scale=isqc, bias=tmpc,
                            accum_out=mcol_sb[:, 25:26]))(ps, slot),
                            waits=[("pe", PE_S2), ("dv", p.c("dv"))], inc=("ac", 1))
                        gAB[par]["ac"] = p.c("ac")
                    p.ev("vector", (lambda slot: lambda e: e.tensor_add(
                        lrun(slot), lrun(slot), mcol_sb[:, 25:26]))(slot),
                        waits=[("ac", p.c("ac"))], inc=("dv", 1))
                p.ev("scalar", (lambda slot: lambda e: e.activation(
                    facc(slot), tmpc, AF.Exp))(slot),
                    waits=[("dv", p.c("dv"))], inc=("ac", 1))
                p.ev("vector", (lambda slot: lambda e: e.reciprocal(tmpc, lrun(slot)))(slot),
                     waits=[("ac", p.c("ac"))], inc=("dv", 1))
                p.ev("vector", (lambda slot: lambda e: e.tensor_mul(
                    facc(slot), facc(slot), tmpc))(slot),
                    waits=[("dv", p.c("dv"))], inc=("dv", 1))
            DV_FAC = p.c("dv")
            if stop in ("p12", "p1"):
                continue
            p.ev("vector", lambda e: e.tensor_copy(pk_sb[:], mcol_sb[:, 16:24]),
                 waits=[("dv", DV_FAC)], inc=("dv", 1))
            p.ev("tensor", lambda e: e.transpose(pE[0:8, 0:128], pk_sb[:], idn_sb[:]),
                 waits=[("dv", p.c("dv")), ("ac", p.c("ac"))], inc=("pe", 1))
            p.ev("scalar", lambda e: e.activation(rsb_sb[0:8, 0:128], pE[0:8, 0:128], AF.Identity),
                 waits=[("pe", p.c("pe"))], inc=("ac", 1))
            AC_FR = p.c("ac")
            for g in range(8):
                p.ev("gpsimd", (lambda g: lambda e: e.dma_start(
                    out=prow_sb[0:1, g * 128:(g + 1) * 128], in_=rsb_sb[g:g + 1, 0:128]))(g),
                    waits=([("ac", AC_FR), ("so", p.c("so"))] if g == 0 else []), inc=("so", 16))
            SO_FR = p.c("so")
            p.ev("tensor", lambda e: e.matmul(pE[:, 0:512], oner_sb[:], prow_sb[0:1, 0:512],
                                              start=True, stop=True),
                 waits=[("so", SO_FR), ("ac", p.c("ac"))], inc=None)
            p.ev("tensor", lambda e: e.matmul(pE[:, 512:1024], oner_sb[:], prow_sb[0:1, 512:1024],
                                              start=True, stop=True),
                 waits=[], inc=("pe", 1))
            p.ev("scalar", lambda e: e.activation(fac_sb[:], pE[:, 0:1024], AF.Identity),
                 waits=[("pe", p.c("pe"))], inc=("ac", 1))
            AC_FB = p.c("ac")
            gE[0]["ac"] = max(gE[0]["ac"], AC_FB)
            gE[1]["ac"] = max(gE[1]["ac"], AC_FB)
            for kb in range(8):
                nq = TPC - kb * 128
                q0 = kb * 128
                for cc in range(4):
                    par = (kb * 4 + cc) % 2
                    pse = pE[:, par * 512:(par + 1) * 512]
                    pse2 = pC[:, par * 512:(par + 1) * 512]
                    n1 = min(nq, 512)
                    p.ev("tensor", (lambda kb, cc, n1, q0, kview, qview, pse: lambda e: e.matmul(
                        pse[:, 0:n1],
                        kview[:, (kb * 4 + cc) * 128:(kb * 4 + cc + 1) * 128],
                        qview[:, q0:q0 + n1],
                        start=True, stop=True))(kb, cc, n1, q0, kview, qview, pse),
                        waits=[("ac", gE[par]["ac"]), ("dv", gE[par]["dv"]),
                               ("ac", gD["ac"] if (kb == 0 and cc == 0) else 0)], inc=("pe", 1))
                    if nq > 512:
                        p.ev("tensor", (lambda kb, cc, nq, q0, kview, qview, pse2: lambda e: e.matmul(
                            pse2[:, 0:nq - 512],
                            kview[:, (kb * 4 + cc) * 128:(kb * 4 + cc + 1) * 128],
                            qview[:, q0 + 512:q0 + nq],
                            start=True, stop=True))(kb, cc, nq, q0, kview, qview, pse2),
                            waits=[], inc=("pe", 1))
                    PE_ST = p.c("pe")
                    eslot = [etf_sb[:, 0:TPC], etf_sb[:, TPC:2 * TPC]][par]
                    bslot = [etb_sb[:, 0:TPC], etb_sb[:, TPC:2 * TPC]][par]
                    p.ev("vector", (lambda pse, cc: lambda e: e.tensor_add(
                        sm_sb[:], pse[:, 0:128], maskt_sb[:, cc * 128:(cc + 1) * 128]))(pse, cc),
                        waits=[("pe", PE_ST), ("ac", gE[par]["ac"]), ("dv", p.c("dv"))],
                        inc=("dv", 1))
                    DV_SM = p.c("dv")
                    p.ev("scalar", (lambda eslot: lambda e: e.activation(
                        eslot[:, 0:128], sm_sb[:], AF.Exp, scale=isqc))(eslot),
                        waits=[("dv", DV_SM)], inc=("ac", 1))
                    if nq > 128:
                        p.ev("scalar", (lambda pse, eslot, n1: lambda e: e.activation(
                            eslot[:, 128:n1], pse[:, 128:n1], AF.Exp,
                            scale=isqc))(pse, eslot, n1),
                            inc=("ac", 1))
                    if nq > 512:
                        p.ev("scalar", (lambda eslot, nq, pse2: lambda e: e.activation(
                            eslot[:, 512:nq], pse2[:, 0:nq - 512], AF.Exp, scale=isqc))(eslot, nq, pse2),
                            inc=("ac", 1))
                    gE[par]["ac"] = p.c("ac")
                    p.ev("vector", (lambda eslot, bslot, nq, q0: lambda e: e.tensor_mul(
                        bslot[:, 0:nq], eslot[:, 0:nq], fac_sb[:, q0:q0 + nq]))(eslot, bslot, nq, q0),
                        waits=[("ac", gE[par]["ac"]), ("ac", AC_FB), ("pe", pe_ctx_done.get(par, 0)),
                               ("dv", p.c("dv"))], inc=("dv", 1))
                    gE[par]["dv"] = p.c("dv")
                    for s in range(kb, 8):
                        p.ev("tensor", (lambda kb, cc, s, q0, vview, bslot: lambda e: e.matmul(
                            pD[:, s * 128:(s + 1) * 128],
                            vview[:, (kb * 4 + cc) * 128:(kb * 4 + cc + 1) * 128],
                            bslot[:, s * 128 - q0: (s + 1) * 128 - q0],
                            start=(kb == 0 and cc == 0), stop=(s == kb and cc == 3)))(kb, cc, s, q0, vview, bslot),
                            waits=([("dv", gE[par]["dv"])] if s == kb else []),
                            inc=(("pe", 1) if s == 7 else None))
                    pe_ctx_done[par] = p.c("pe")
            PE_CTX = p.c("pe")
            for s in range(8):
                p.ev("scalar", (lambda h, s: lambda e: e.activation(
                    cxf_sb[:, h * TPC + s * 128: h * TPC + (s + 1) * 128],
                    pD[:, s * 128:(s + 1) * 128], AF.Identity))(h, s),
                    waits=([("pe", PE_CTX)] if s == 0 else []), inc=("ac", 1))
            gD["ac"] = p.c("ac")
        AC_ATT = p.c("ac")
        if stop in ("attn", "p12", "p1"):
            return _finish_C(nc, p, sems, outO, x3_sb, es)

        for ob in range(16):
            for hb in range(2):
                w0 = [wp_sb[:, 0:1024], wp_sb[:, 1024:2048]][hb]
                p.ev("sync", (lambda ob, hb, w0: lambda e: e.dma_start(
                    out=w0.rearrange("p (kb o) -> p kb o", kb=8),
                    in_=wo.rearrange("(kb p) o -> p kb o", p=128)[:, hb * 8:(hb + 1) * 8,
                                                                 ob * 128:(ob + 1) * 128]))(ob, hb, w0),
                    waits=[("pe", max(0, p.c("pe") - 8))], inc=("ld", 16))
                LD_WO = p.c("ld")
                p.ev("vector", (lambda w0: lambda e: e.tensor_scalar(
                    w0.bitcast(I32), w0.bitcast(I32), 0x8000, None, ALU.add))(w0),
                    waits=[("ld", LD_WO), ("pe", p.c("pe"))], inc=("dv", 1))
                DV_RTN = p.c("dv")
                for c in range(2):
                    for kbl in range(8):
                        kb = hb * 8 + kbl
                        p.ev("tensor", (lambda w0, kb, kbl, c: lambda e: e.matmul(
                            pC[:, c * 512:(c + 1) * 512], bf16_view(w0[:, kbl * 128:(kbl + 1) * 128]),
                            cxf_sb[:, kb * TPC + c * 512: kb * TPC + c * 512 + 512],
                            start=(kb == 0), stop=(kb == 15)))(w0, kb, kbl, c),
                            waits=([("dv", DV_RTN), ("ac", AC_ATT)] if kbl == 0 else []),
                            inc=(("pe", 1) if kbl == 7 else None))
            PE_WO = p.c("pe")
            p.ev("sync", (lambda ob: lambda e: e.dma_start(
                out=sc1[:], in_=x2T[ob * 128:(ob + 1) * 128, 0:512]))(ob),
                waits=[("dv", p.c("dv")), ("pe", max(0, p.c("pe") - 4))], inc=("ld", 16))
            p.ev("sync", (lambda ob: lambda e: e.dma_start(
                out=etf_sb[:, 0:512], in_=x2T[ob * 128:(ob + 1) * 128, 512:1024]))(ob),
                inc=("ld", 16))
            LD_X2 = p.c("ld")
            p.ev("vector", (lambda ob: lambda e: e.tensor_add(
                x3_sb[:, ob * TPC: ob * TPC + 512], pC[:, 0:512], sc1[:]))(ob),
                waits=[("pe", PE_WO), ("ld", LD_X2)], inc=("dv", 1))
            p.ev("vector", (lambda ob: lambda e: e.tensor_add(
                x3_sb[:, ob * TPC + 512: ob * TPC + 1024], pC[:, 512:1024], etf_sb[:, 0:512]))(ob),
                waits=[("dv", p.c("dv"))], inc=("dv", 1))
        DV_X3 = p.c("dv")
        if stop == "wo":
            return _finish_C(nc, p, sems, outO, x3_sb, es)

        inv_h = 1.0 / HQ
        R_M, R_VAR, R_A, R_B = (i * 512 for i in range(4))
        for (c0, w) in [(0, 512), (512, 512)]:
            for j in range(16):
                p.ev("scalar", (lambda j, c0, w: lambda e: e.activation(
                    sc1[:, 0:w], x3_sb[:, j * TPC + c0: j * TPC + c0 + w], AF.Square))(j, c0, w),
                    waits=[("dv", DV_X3), ("dv", p.c("dv")), ("pe", max(0, p.c("pe") - 1))],
                    inc=("ac", 1))
                p.ev("tensor", (lambda j, c0, w: lambda e: e.matmul(
                    pA[0:1, 0:w], onec_sb[:], x3_sb[:, j * TPC + c0: j * TPC + c0 + w],
                    start=(j == 0), stop=(j == 15)))(j, c0, w),
                    waits=[("dv", DV_X3)], inc=None)
                p.ev("tensor", (lambda j, w: lambda e: e.matmul(
                    pB[0:1, 0:w], onec_sb[:], sc1[:, 0:w],
                    start=(j == 0), stop=(j == 15)))(j, w),
                    waits=[("ac", p.c("ac"))], inc=("pe", 1))
            PE_ST = p.c("pe")

            def rowstep(fn, waits):
                p.ev("vector", fn, waits=waits, inc=("dv", 1))
            rowstep(lambda e, w=w: e.tensor_scalar_mul(row_sb[:, R_M:R_M + w], pA[0:1, 0:w], inv_h),
                    [("pe", PE_ST)])
            rowstep(lambda e, w=w: e.tensor_scalar_mul(row_sb[:, R_VAR:R_VAR + w], pB[0:1, 0:w], inv_h),
                    [("dv", p.c("dv"))])
            rowstep(lambda e, w=w: e.tensor_mul(row_sb[:, R_B:R_B + w], row_sb[:, R_M:R_M + w],
                                                row_sb[:, R_M:R_M + w]), [("dv", p.c("dv"))])
            rowstep(lambda e, w=w: e.tensor_sub(row_sb[:, R_VAR:R_VAR + w], row_sb[:, R_VAR:R_VAR + w],
                                                row_sb[:, R_B:R_B + w]), [("dv", p.c("dv"))])
            rowstep(lambda e, w=w: e.tensor_scalar_add(row_sb[:, R_VAR:R_VAR + w],
                                                       row_sb[:, R_VAR:R_VAR + w], 1e-5),
                    [("dv", p.c("dv"))])
            p.ev("scalar",
                 lambda e, w=w: e.activation(row_sb[:, R_VAR:R_VAR + w], row_sb[:, R_VAR:R_VAR + w],
                                             AF.Sqrt),
                 waits=[("dv", p.c("dv"))], inc=("ac", 1))
            rowstep(lambda e, w=w: e.reciprocal(row_sb[:, R_A:R_A + w], row_sb[:, R_VAR:R_VAR + w]),
                    [("ac", p.c("ac"))])
            rowstep(lambda e, w=w: e.tensor_mul(row_sb[:, R_B:R_B + w], row_sb[:, R_M:R_M + w],
                                                row_sb[:, R_A:R_A + w]), [("dv", p.c("dv"))])
            rowstep(lambda e, w=w: e.tensor_scalar_mul(row_sb[:, R_B:R_B + w], row_sb[:, R_B:R_B + w],
                                                       -1.0), [("dv", p.c("dv"))])
            DV_ROW = p.c("dv")
            p.ev("tensor",
                 lambda e, w=w: e.matmul(pC[:, 0:w], oner_sb[:], row_sb[:, R_A:R_A + w],
                                         start=True, stop=True),
                 waits=[("dv", DV_ROW)], inc=None)
            p.ev("tensor",
                 lambda e, w=w: e.matmul(pC[:, 512:512 + w], oner_sb[:], row_sb[:, R_B:R_B + w],
                                         start=True, stop=True),
                 waits=[], inc=("pe", 1))
            PE_BC = p.c("pe")
            for j in range(16):
                p.ev("vector", (lambda j, c0, w: lambda e: e.tensor_mul(
                    sc1[:, 0:w], x3_sb[:, j * TPC + c0: j * TPC + c0 + w], pC[:, 0:w]))(j, c0, w),
                    waits=[("pe", PE_BC), ("ac", p.c("ac")), ("dv", p.c("dv"))], inc=("dv", 1))
                p.ev("vector", (lambda j, c0, w: lambda e: e.tensor_tensor(
                    xln3_sb[:, j * TPC + c0: j * TPC + c0 + w], sc1[:, 0:w],
                    pC[:, 512:512 + w], ALU.add))(j, c0, w),
                    waits=[("dv", p.c("dv"))], inc=("dv", 1))
        DV_LN3 = p.c("dv")
        if stop == "ln":
            return _finish_C(nc, p, sems, outO, x3_sb, es)

        p.ev("sync", lambda e: e.dma_start(out=wp_sb[:, 0:128].rearrange("p (kb r) -> p kb r", r=8),
                                           in_=wr.rearrange("(kb p) r -> p kb r", p=128)),
             waits=[("pe", p.c("pe"))], inc=("ld", 16))
        LD_WR = p.c("ld")
        p.ev("vector", lambda e: e.tensor_scalar(
            wp_sb[:, 0:128].bitcast(I32), wp_sb[:, 0:128].bitcast(I32), 0x8000, None, ALU.add),
            waits=[("ld", LD_WR), ("dv", p.c("dv"))], inc=("dv", 1))
        DV_WR = p.c("dv")
        for c in range(2):
            for kb in range(16):
                p.ev("tensor", (lambda kb, c: lambda e: e.matmul(
                    pD[0:8, c * 512:(c + 1) * 512], bf16_view(wp_sb[:, kb * 8:(kb + 1) * 8]),
                    xln3_sb[:, kb * TPC + c * 512: kb * TPC + c * 512 + 512],
                    start=(kb == 0), stop=(kb == 15)))(kb, c),
                    waits=([("dv", DV_WR), ("dv", DV_LN3)] if kb == 0 else []),
                    inc=(("pe", 1) if kb == 15 else None))
        PE_R = p.c("pe")
        p.ev("scalar", lambda e: e.activation(rsb_sb[:], pD[0:8, :], AF.Identity),
             waits=[("pe", PE_R)], inc=("ac", 1))
        AC_R = p.c("ac")
        for s in range(8):
            p.ev("tensor", (lambda s: lambda e: e.transpose(
                pE[:, 0:8], rsb_sb[0:8, s * 128:(s + 1) * 128], idn_sb[0:8, 0:8]))(s),
                waits=[("ac", AC_R), ("ac", p.c("ac"))], inc=("pe", 1))
            p.ev("scalar", (lambda s: lambda e: e.activation(
                rT_sb[:, s * 8:(s + 1) * 8], pE[:, 0:8], AF.Identity))(s),
                waits=[("pe", p.c("pe"))], inc=("ac", 1))
        AC_RT = p.c("ac")
        for s in range(8):
            r8 = rT_sb[:, s * 8:(s + 1) * 8]

            def step(fn, waits=None):
                p.ev("vector", fn, waits=(waits or [("dv", p.c("dv"))]), inc=("dv", 1))
            step(lambda e, r8=r8: e.reduce_max(mcol_sb[:, 0:1], r8, axis=AX.X),
                 [("ac", AC_RT), ("dv", p.c("dv"))])
            step(lambda e, r8=r8, s=s: e.tensor_scalar(cmb_sb[:, s * 8:(s + 1) * 8],
                                                       r8, mcol_sb[:, 0:1], None, ALU.is_equal))
            step(lambda e, s=s: e.tensor_scalar(sc1[:, 0:8], cmb_sb[:, s * 8:(s + 1) * 8],
                                                1e30, None, ALU.mult))
            step(lambda e, r8=r8: e.tensor_sub(sc1[:, 8:16], r8, sc1[:, 0:8]))
            step(lambda e: e.reduce_max(mcol_sb[:, 1:2], sc1[:, 8:16], axis=AX.X))
            step(lambda e: e.tensor_sub(mcol_sb[:, 2:3], mcol_sb[:, 0:1], mcol_sb[:, 1:2]))
            p.ev("scalar", lambda e: e.activation(mcol_sb[:, 3:4], mcol_sb[:, 2:3], AF.Sigmoid),
                 waits=[("dv", p.c("dv"))], inc=("ac", 1))
            step(lambda e: e.tensor_scalar(mcol_sb[:, 4:5], mcol_sb[:, 3:4], -1.0, 1.0,
                                           ALU.mult, ALU.add), [("ac", p.c("ac"))])
            step(lambda e, r8=r8: e.tensor_scalar(sc1[:, 0:8], r8, mcol_sb[:, 1:2], None, ALU.is_equal))
            step(lambda e, s=s: e.tensor_scalar(cmb_sb[:, s * 8:(s + 1) * 8],
                                                cmb_sb[:, s * 8:(s + 1) * 8], mcol_sb[:, 3:4],
                                                None, ALU.mult))
            step(lambda e, s=s: e.scalar_tensor_tensor(cmb_sb[:, s * 8:(s + 1) * 8], sc1[:, 0:8],
                                                       mcol_sb[:, 4:5], cmb_sb[:, s * 8:(s + 1) * 8],
                                                       ALU.mult, ALU.add))
        DV_CMB = p.c("dv")
        p.ev("tensor", lambda e: e.transpose(pE[0:64, 0:128], cmb_sb[:], idn_sb[:]),
             waits=[("dv", DV_CMB), ("ac", p.c("ac"))], inc=("pe", 1))
        p.ev("scalar", lambda e: e.activation(cmt_sb[:], pE[0:64, 0:128], AF.Identity),
             waits=[("pe", p.c("pe"))], inc=("ac", 1))
        AC_CMT = p.c("ac")
        for e8 in range(8):
            for s in range(8):
                p.ev("gpsimd", (lambda e8, s: lambda e: e.dma_start(
                    out=crows_sb[0:1, e8 * TPC + s * 128: e8 * TPC + (s + 1) * 128],
                    in_=cmt_sb[s * 8 + e8: s * 8 + e8 + 1, :]))(e8, s),
                    waits=([("ac", AC_CMT), ("pe", p.c("pe"))] if (e8 == 0 and s == 0) else []),
                    inc=("so", 16))
        SO_CROW = p.c("so")
        if stop == "router":
            return _finish_C(nc, p, sems, outO, x3_sb, es)

        for e8 in range(8):
            for c in range(2):
                p.ev("tensor", (lambda e8, c: lambda e: e.matmul(
                    pC[:, 0:512], onerb_sb, crows_sb[0:1, e8 * TPC + c * 512: e8 * TPC + (c + 1) * 512],
                    start=True, stop=True))(e8, c),
                    waits=[("so", SO_CROW), ("dv", p.c("dv")), ("ac", p.c("ac")),
                           ("pe", p.c("pe"))], inc=("pe", 1))
                p.ev("scalar", (lambda c: lambda e: e.activation(
                    cbc_sb[:, c * 512:(c + 1) * 512], pC[:, 0:512], AF.Identity))(c),
                    waits=[("pe", p.c("pe"))], inc=("ac", 1))
            AC_CBC = p.c("ac")
            for iob in range(11):
                for mat, wsrc, off in [(0, wgm, 0), (1, wum, 2048)]:
                    for hh in range(2):
                        wv = wp_sb[:, off + hh * 1024: off + (hh + 1) * 1024]
                        p.ev("sync", (lambda e8, iob, wsrc, hh, wv: lambda e: e.dma_start(
                            out=wv.rearrange("p (kb o) -> p kb o", kb=8),
                            in_=wsrc.rearrange("(ee kb p) o -> p ee kb o", ee=8, p=128)[
                                :, e8, hh * 8:(hh + 1) * 8, iob * 128:(iob + 1) * 128]))(e8, iob, wsrc, hh, wv),
                            waits=[("pe", max(0, p.c("pe") - 24))], inc=("ld", 16))
                        p.ev("vector", (lambda wv: lambda e: e.tensor_scalar(
                            wv.bitcast(I32), wv.bitcast(I32), 0x8000, None, ALU.add))(wv),
                            waits=[("ld", p.c("ld")), ("pe", p.c("pe"))], inc=("dv", 1))
                DV_W = p.c("dv")
                for kb in range(16):
                    woff = (kb // 8) * 1024 + (kb % 8) * 128
                    p.ev("tensor", (lambda kb, woff: lambda e: e.matmul(
                        pA[:], bf16_view(wp_sb[:, woff:woff + 128]),
                        xln3_sb[:, kb * TPC:kb * TPC + 512],
                        start=(kb == 0), stop=(kb == 15)))(kb, woff),
                        waits=([("dv", DV_W)] if kb == 0 else []), inc=None)
                    p.ev("tensor", (lambda kb, woff: lambda e: e.matmul(
                        pB[:], bf16_view(wp_sb[:, woff:woff + 128]),
                        xln3_sb[:, kb * TPC + 512:kb * TPC + 1024],
                        start=(kb == 0), stop=(kb == 15)))(kb, woff), inc=None)
                    p.ev("tensor", (lambda kb, woff: lambda e: e.matmul(
                        pC[:, 0:512], bf16_view(wp_sb[:, 2048 + woff:2048 + woff + 128]),
                        xln3_sb[:, kb * TPC:kb * TPC + 512],
                        start=(kb == 0), stop=(kb == 15)))(kb, woff), inc=None)
                    p.ev("tensor", (lambda kb, woff: lambda e: e.matmul(
                        pC[:, 512:1024], bf16_view(wp_sb[:, 2048 + woff:2048 + woff + 128]),
                        xln3_sb[:, kb * TPC + 512:kb * TPC + 1024],
                        start=(kb == 0), stop=(kb == 15)))(kb, woff),
                        inc=(("pe", 1) if kb == 15 else None))
                PE_GU = p.c("pe")
                for c in range(2):
                    gps = [pA, pB][c]
                    p.ev("scalar", (lambda gps: lambda e: e.activation(sc3[:], gps[:], AF.Silu))(gps),
                         waits=[("pe", PE_GU), ("ac", p.c("ac")), ("dv", p.c("dv"))], inc=("ac", 1))
                    p.ev("vector", (lambda iob, c: lambda e: e.tensor_tensor(
                        hf_v[:, iob * TPC + c * 512: iob * TPC + (c + 1) * 512], sc3[:],
                        pC[:, c * 512:(c + 1) * 512], ALU.mult))(iob, c),
                        waits=[("ac", p.c("ac"))], inc=("dv", 1))
            DV_H = p.c("dv")
            for ob in range(16):
                for hh in range(2):
                    nkb = 6 if hh == 0 else 5
                    wv = wp_sb[:, hh * 1024: hh * 1024 + nkb * 128]
                    p.ev("sync", (lambda e8, ob, hh, nkb, wv: lambda e: e.dma_start(
                        out=wv.rearrange("p (kb o) -> p kb o", kb=nkb),
                        in_=wdm.rearrange("(ee kb p) o -> p ee kb o", ee=8, p=128)[
                            :, e8, hh * 6:hh * 6 + nkb, ob * 128:(ob + 1) * 128]))(e8, ob, hh, nkb, wv),
                        waits=[("pe", max(0, p.c("pe") - 4)), ("dv", p.c("dv") - 2 if p.c("dv") >= 2 else 0)],
                        inc=("ld", 16))
                    p.ev("vector", (lambda wv: lambda e: e.tensor_scalar(
                        wv.bitcast(I32), wv.bitcast(I32), 0x8000, None, ALU.add))(wv),
                        waits=[("ld", p.c("ld")), ("pe", p.c("pe"))], inc=("dv", 1))
                    DV_WD = p.c("dv")
                    for kbl in range(nkb):
                        kb = hh * 6 + kbl
                        p.ev("tensor", (lambda kb, kbl, hh, wv: lambda e: e.matmul(
                            pD[:, 0:512], bf16_view(wv[:, kbl * 128:(kbl + 1) * 128]),
                            hf_v[:, kb * TPC: kb * TPC + 512],
                            start=(kb == 0), stop=(kb == 10)))(kb, kbl, hh, wv),
                            waits=([("dv", DV_WD), ("dv", DV_H)] if kbl == 0 else []), inc=None)
                        p.ev("tensor", (lambda kb, kbl, hh, wv: lambda e: e.matmul(
                            pD[:, 512:1024], bf16_view(wv[:, kbl * 128:(kbl + 1) * 128]),
                            hf_v[:, kb * TPC + 512: kb * TPC + 1024],
                            start=(kb == 0), stop=(kb == 10)))(kb, kbl, hh, wv),
                            inc=(("pe", 1) if kbl == nkb - 1 else None))
                PE_D = p.c("pe")
                for c in range(2):
                    p.ev("vector", (lambda c: lambda e: e.tensor_tensor(
                        sc1[:], pD[:, c * 512:(c + 1) * 512], cbc_sb[:, c * 512:(c + 1) * 512],
                        ALU.mult))(c),
                        waits=[("pe", PE_D), ("ac", AC_CBC), ("dv", p.c("dv"))], inc=("dv", 1))
                    p.ev("vector", (lambda ob, c: lambda e: e.tensor_add(
                        x3_sb[:, ob * TPC + c * 512: ob * TPC + (c + 1) * 512],
                        x3_sb[:, ob * TPC + c * 512: ob * TPC + (c + 1) * 512], sc1[:]))(ob, c),
                        waits=[("dv", p.c("dv"))], inc=("dv", 1))
        DV_OUT = p.c("dv")
        for ob in range(16):
            p.ev("gpsimd", (lambda ob: lambda e: e.dma_start(
                out=outO[ob * 128:(ob + 1) * 128, :], in_=x3_sb[:, ob * TPC:(ob + 1) * TPC]))(ob),
                waits=([("dv", DV_OUT)] if ob == 0 else []), inc=("so", 16))
        p.ev("gpsimd", lambda e: None, waits=[("so", p.c("so"))], inc=None)

        with nc.Block() as block:
            p.emit(block, sems)
    return nc


def prep_C(ins, bres):
    x2f, qf, kf, vf = {}, {}, {}, {}
    for b in range(2):
        x2f[b] = np.concatenate([np.asarray(bres[4 * b + q]["x2o"], np.float32) for q in range(4)], axis=1)
        qf[b] = np.concatenate([np.asarray(bres[4 * b + q]["qO"]) for q in range(4)], axis=1)
        kf[b] = np.ascontiguousarray(
            np.concatenate([np.asarray(bres[4 * b + q]["kO"]) for q in range(4)], axis=1))
        vf[b] = np.ascontiguousarray(
            np.concatenate([np.asarray(bres[4 * b + q]["vO"]) for q in range(4)], axis=0))
    wgm = np.ascontiguousarray(ins["L1_ffn_Wg"].reshape(8 * HQ, 1408))
    wum = np.ascontiguousarray(ins["L1_ffn_Wu"].reshape(8 * HQ, 1408))
    wdm = np.ascontiguousarray(ins["L1_ffn_Wd"].reshape(8 * 1408, HQ))
    wo = np.ascontiguousarray(ins["L1_attn_Wo"])
    wr = np.ascontiguousarray(ins["L1_ffn_Wr"])
    onec = np.ones((128, 1), np.float32)
    oner = np.ones((1, 128), np.float32)
    idn = np.eye(128, dtype=np.float32)

    in_maps = []
    for c in range(8):
        b, r = c // 4, c % 4
        idx = np.concatenate([np.arange(128 * (4 * i + r), 128 * (4 * i + r) + 128)
                              for i in range(8)])
        qi = np.arange(128)[:, None]
        kc = np.arange(512)[None, :]
        mask = np.where(kc <= 128 * r + qi, 0.0, -1e30).astype(np.float32)
        mt = np.zeros((128, 512), np.float32)
        for cc in range(4):
            kl = np.arange(128)[:, None] + cc * 128
            qq = np.arange(128)[None, :]
            mt[:, cc * 128:(cc + 1) * 128] = np.where(kl <= 128 * r + qq, 0.0, -1e30)
        in_maps.append({
            "x2T": np.ascontiguousarray(x2f[b][:, idx]),
            "qT": np.ascontiguousarray(qf[b][:, idx]),
            "kF": kf[b], "vF": vf[b],
            "wo": wo, "wr": wr, "wgm": wgm, "wum": wum, "wdm": wdm,
            "maskS": mask, "maskTT": mt, "onec": onec, "oner": oner, "idn": idn,
        })
    return in_maps


def kernel(**inputs):
    ra = run_bass_kernel_spmd(build_A(), prep_A(inputs), core_ids=list(range(8)))
    ares = [ra.results[c] for c in range(8)]
    rb = run_bass_kernel_spmd(build_B(), prep_B(inputs, ares), core_ids=list(range(8)))
    bres = [rb.results[c] for c in range(8)]
    rc = run_bass_kernel_spmd(build_C(), prep_C(inputs, bres), core_ids=list(range(8)))
    out = np.zeros((BQ, SQ, HQ), np.float32)
    for c in range(8):
        b, r = c // 4, c % 4
        idx = np.concatenate([np.arange(128 * (4 * i + r), 128 * (4 * i + r) + 128)
                              for i in range(8)])
        out[b, idx, :] = np.asarray(rc.results[c]["outO"], np.float32).T
    return out


# version 04
# speedup vs baseline: 1.4094x; 1.0668x over previous
import numpy as np
import ml_dtypes
from contextlib import ExitStack

import concourse.bass as bass
import concourse.mybir as mybir
from concourse.bass_utils import run_bass_kernel_spmd

F32 = mybir.dt.float32
BF16 = mybir.dt.bfloat16
I32 = mybir.dt.int32
AF = mybir.ActivationFunctionType
ALU = mybir.AluOpType
AX = mybir.AxisListType

BQ, SQ, HQ, IQ = 2, 4096, 2048, 4096
NH, HD, NS, RK = 2, 2048, 16, 128
TPC = 1024
TW = TPC + 3
MAGIC = float(np.float32(3 * 2.0 ** 22))
BF = ml_dtypes.bfloat16


def bf16_view(ap):
    return ap.bitcast(BF16).rearrange("p (f two) -> p f two", two=2)[:, :, 1:2].squeeze(2)


class Prog:

    ENGINES = ("sync", "tensor", "vector", "scalar", "gpsimd")

    def __init__(self):
        self.events = []
        self.cnt = {}

    def ev(self, engine, fn, waits=(), inc=None):
        assert engine in self.ENGINES
        self.events.append((engine, [(s, c) for (s, c) in waits if c > 0], fn, inc))
        if inc is not None:
            self.cnt[inc[0]] = self.cnt.get(inc[0], 0) + inc[1]

    def c(self, sem):
        return self.cnt.get(sem, 0)

    def emit(self, block, sems):
        for engine in self.ENGINES:
            evs = [(w, f, i) for (e, w, f, i) in self.events if e == engine]
            if not evs:
                continue
            dec = getattr(block, engine)

            def body(eng, evs=evs):
                for waits, fn, inc in evs:
                    for sem, cntv in waits:
                        eng.wait_ge(sems[sem], cntv)
                    instr = fn(eng)
                    if inc is not None:
                        assert instr is not None
                        instr.then_inc(sems[inc[0]], inc[1])
            dec(body)



def build_A():
    nc = bass.Bass("TRN2", target_bir_lowering=False)

    x0T = nc.dram_tensor("x0T", [HQ, TW], F32, kind="ExternalInput").ap()
    wx = nc.dram_tensor("wx", [HQ, IQ], F32, kind="ExternalInput").ap()
    wg = nc.dram_tensor("wg", [HQ, IQ], F32, kind="ExternalInput").ap()
    convw = nc.dram_tensor("convw", [128, 32 * 4], F32, kind="ExternalInput").ap()
    convb = nc.dram_tensor("convb", [128, 32], F32, kind="ExternalInput").ap()
    xpw_dt = nc.dram_tensor("xpw_dt", [128, 32 * RK], F32, kind="ExternalInput").ap()
    xpw_bc = nc.dram_tensor("xpw_bc", [128, 32 * 32], F32, kind="ExternalInput").ap()
    dtw = nc.dram_tensor("dtw", [128, NH * HD], F32, kind="ExternalInput").ap()
    dtb = nc.dram_tensor("dtb", [128, 32], F32, kind="ExternalInput").ap()
    alog = nc.dram_tensor("alog", [128, 32 * NS], F32, kind="ExternalInput").ap()
    dcol = nc.dram_tensor("dcol", [128, 32], F32, kind="ExternalInput").ap()
    onec = nc.dram_tensor("onec", [128, 1], F32, kind="ExternalInput").ap()
    oner = nc.dram_tensor("oner", [1, 128], F32, kind="ExternalInput").ap()
    idn = nc.dram_tensor("idn", [128, 128], F32, kind="ExternalInput").ap()

    y_loc = nc.dram_tensor("y_loc", [IQ, TPC], F32, kind="ExternalOutput").ap()
    cdt_o = nc.dram_tensor("cdt_o", [IQ, TPC], F32, kind="ExternalOutput").ap()
    bmat = nc.dram_tensor("bmat", [NH * NS, TPC], F32, kind="ExternalOutput").ap()
    cmat = nc.dram_tensor("cmat", [NH * NS, TPC], F32, kind="ExternalOutput").ap()
    gate_o = nc.dram_tensor("gate_o", [IQ, TPC], BF16, kind="ExternalOutput").ap()
    hfin = nc.dram_tensor("hfin", [128, 32 * NS], F32, kind="ExternalOutput").ap()
    cdtfin = nc.dram_tensor("cdtfin", [128, 32], F32, kind="ExternalOutput").ap()

    dt_i = nc.dram_tensor("dt_i", [IQ, TPC], F32, kind="Internal").ap()
    xc_i = nc.dram_tensor("xc_i", [IQ, TPC], BF16, kind="Internal").ap()

    p = Prog()
    with ExitStack() as es:
        sb = lambda name, shape, dt=F32: es.enter_context(nc.sbuf_tensor(name, shape, dt))
        psalloc = lambda name, shape, dt=F32: es.enter_context(nc.psum_tensor(name, shape, dt))
        sems = {}
        for s in ["ld", "so", "pe", "ac", "dv", "po"]:
            sems[s] = es.enter_context(nc.semaphore(s))

        x0_sb = sb("x0_sb", [128, 16 * TW])
        xln_sb = sb("xln_sb", [128, 16 * TW], BF16)
        wpair = sb("wpair", [128, 4096])
        dta_v = x0_sb[:, 0:16 * 512]
        h_v = x0_sb[:, 16 * 512:32 * 512]
        brep_v = xln_sb[:, 0:16 * 512]
        crep_v = xln_sb[:, 16 * 512:32 * 512]
        wbufs = [wpair[:, 0:2048], wpair[:, 2048:4096]]
        xca_v = wpair[:].bitcast(BF16)

        onec_sb = sb("onec_sb", [128, 1])
        oner_sb = sb("oner_sb", [1, 128])
        idnb_sb = sb("idnb_sb", [128, 128], BF16)
        idn_sb = sb("idn_sb", [128, 128])
        convw_sb = sb("convw_sb", [128, 32 * 4])
        convb_sb = sb("convb_sb", [128, 32])
        dtb_sb = sb("dtb_sb", [128, 32])
        dcol_sb = sb("dcol_sb", [128, 32])
        aneg_sb = sb("aneg_sb", [128, 32 * NS])
        xpdt_sb = sb("xpdt_sb", [128, 32 * RK])
        xpbc_sb = sb("xpbc_sb", [128, 32 * 32])
        dtw_sb = sb("dtw_sb", [128, NH * HD])
        negc_sb = sb("negc_sb", [128, 1])
        onecol_sb = sb("onecol_sb", [128, 1])
        hst_sb = sb("hst_sb", [128, 32 * NS])
        cdtc_sb = sb("cdtc_sb", [128, 32])

        row_sb = sb("row_sb", [1, 6 * 512])
        xin_sb = sb("xin_sb", [128, TW])
        xcv_sb = sb("xcv_sb", [128, TPC])
        xcb_sb = sb("xcb_sb", [128, TPC], BF16)
        sq_sb = [sb(f"sq_sb{i}", [128, 512]) for i in range(2)]
        dtin_sb = sb("dtin_sb", [128, TPC], BF16)
        bc_sb = sb("bc_sb", [32, TPC])
        sp_sb = [sb(f"sp_sb{i}", [128, 512]) for i in range(3)]
        dt_sb = sb("dt_sb", [128, TPC])
        zero_sb = sp_sb[0]

        bflat_sb = sb("bflat_sb", [1, 512])
        cflat_sb = sb("cflat_sb", [1, 512])
        dtx_sb = sb("dtx_sb", [128, 512], BF16)
        dbu_sb = sb("dbu_sb", [128, 512], BF16)
        g_sb = sb("g_sb", [128, 512], BF16)
        da_sb = xcv_sb[:, 0:512]
        cdt_sb = dt_sb[:, 0:512]
        ya_sb = dt_sb[:, 512:1024]

        pA = psalloc("pA", [128, 512])
        pB = psalloc("pB", [128, 512])
        pC = psalloc("pC", [128, 1024])
        pD = psalloc("pD", [128, 1024])

        def mkload(dst, src):
            return lambda e: e.dma_start(out=dst, in_=src)

        loads = [
            (onec_sb[:], onec[:]),
            (oner_sb[:], oner[:]),
            (idn_sb[:], idn[:]),
            (convw_sb[:], convw[:]),
            (convb_sb[:], convb[:]),
            (dtb_sb[:], dtb[:]),
            (dcol_sb[:], dcol[:]),
            (aneg_sb[:], alog[:]),
            (xpdt_sb[:], xpw_dt[:]),
            (xpbc_sb[:], xpw_bc[:]),
            (dtw_sb[:], dtw[:]),
            (x0_sb[:].rearrange("p (c t) -> p c t", c=16),
             x0T.rearrange("(c p) t -> p c t", p=128)),
        ]
        for dst, src in loads:
            p.ev("sync", mkload(dst, src), inc=("ld", 16))
        LD_CONST = p.c("ld")

        for buf, val in [(negc_sb, -1.0), (onecol_sb, 1.0),
                         (hst_sb, 0.0), (cdtc_sb, 0.0)]:
            p.ev("vector", (lambda b: (lambda e: e.memset(b[:], val)))(buf) if False else
                 (lambda e, b=buf, v=val: e.memset(b[:], v)), inc=("dv", 1))
        p.ev("vector", lambda e: e.tensor_copy(idnb_sb[:], idn_sb[:]),
             waits=[("ld", LD_CONST)], inc=("dv", 1))
        for wt in [xpdt_sb, xpbc_sb, dtw_sb]:
            p.ev("gpsimd", (lambda wt: lambda e: e.tensor_scalar(
                wt[:].bitcast(I32), wt[:].bitcast(I32), 0x8000, None, ALU.add))(wt),
                waits=[("ld", LD_CONST)], inc=("po", 1))
        PO_RTNC = p.c("po")
        p.ev("scalar", lambda e: e.activation(aneg_sb[:], aneg_sb[:], AF.Exp),
             waits=[("ld", LD_CONST)], inc=("ac", 1))
        p.ev("vector", lambda e: e.tensor_scalar_mul(aneg_sb[:], aneg_sb[:], -1.0),
             waits=[("ac", p.c("ac"))], inc=("dv", 1))

        inv_h = 1.0 / HQ
        R_M, R_MSQ, R_VAR, R_RSTD, R_A, R_B = (i * 512 for i in range(6))
        for (c0, w) in [(3, 512), (515, 512), (0, 3)]:
            for j in range(16):
                p.ev("scalar",
                     lambda e, j=j, c0=c0, w=w: e.activation(
                         sq_sb[j % 2][:, 0:w], x0_sb[:, j * TW + c0: j * TW + c0 + w], AF.Square),
                     waits=[("ld", LD_CONST), ("pe", max(0, p.c("pe") - 1)),
                            ("dv", p.c("dv"))],
                     inc=("ac", 1))
                p.ev("tensor",
                     lambda e, j=j, c0=c0, w=w: e.matmul(
                         pA[0:1, 0:w], onec_sb[:], x0_sb[:, j * TW + c0: j * TW + c0 + w],
                         start=(j == 0), stop=(j == 15)),
                     waits=([("ld", LD_CONST), ("dv", p.c("dv"))] if j == 0 else []), inc=None)
                p.ev("tensor",
                     lambda e, j=j, w=w: e.matmul(
                         pB[0:1, 0:w], onec_sb[:], sq_sb[j % 2][:, 0:w],
                         start=(j == 0), stop=(j == 15)),
                     waits=[("ac", p.c("ac"))], inc=("pe", 1))
            PE_STATS = p.c("pe")

            def rowstep(fn, waits):
                p.ev("vector", fn, waits=waits, inc=("dv", 1))
            rowstep(lambda e, w=w: e.tensor_scalar_mul(row_sb[:, R_M:R_M + w], pA[0:1, 0:w], inv_h),
                    [("pe", PE_STATS)])
            rowstep(lambda e, w=w: e.tensor_scalar_mul(row_sb[:, R_MSQ:R_MSQ + w], pB[0:1, 0:w], inv_h),
                    [("dv", p.c("dv"))])
            rowstep(lambda e, w=w: e.tensor_mul(row_sb[:, R_VAR:R_VAR + w], row_sb[:, R_M:R_M + w],
                                                row_sb[:, R_M:R_M + w]), [("dv", p.c("dv"))])
            rowstep(lambda e, w=w: e.tensor_sub(row_sb[:, R_VAR:R_VAR + w], row_sb[:, R_MSQ:R_MSQ + w],
                                                row_sb[:, R_VAR:R_VAR + w]), [("dv", p.c("dv"))])
            rowstep(lambda e, w=w: e.tensor_scalar_add(row_sb[:, R_VAR:R_VAR + w],
                                                       row_sb[:, R_VAR:R_VAR + w], 1e-5),
                    [("dv", p.c("dv"))])
            p.ev("scalar",
                 lambda e, w=w: e.activation(row_sb[:, R_RSTD:R_RSTD + w], row_sb[:, R_VAR:R_VAR + w], AF.Sqrt),
                 waits=[("dv", p.c("dv"))], inc=("ac", 1))
            rowstep(lambda e, w=w: e.reciprocal(row_sb[:, R_A:R_A + w], row_sb[:, R_RSTD:R_RSTD + w]),
                    [("ac", p.c("ac"))])
            rowstep(lambda e, w=w: e.tensor_mul(row_sb[:, R_B:R_B + w], row_sb[:, R_M:R_M + w],
                                                row_sb[:, R_A:R_A + w]), [("dv", p.c("dv"))])
            rowstep(lambda e, w=w: e.tensor_scalar_mul(row_sb[:, R_B:R_B + w], row_sb[:, R_B:R_B + w], -1.0),
                    [("dv", p.c("dv"))])
            DV_ROW = p.c("dv")
            p.ev("tensor",
                 lambda e, w=w: e.matmul(pC[:, 0:w], oner_sb[:], row_sb[:, R_A:R_A + w],
                                         start=True, stop=True),
                 waits=[("dv", DV_ROW)], inc=None)
            p.ev("tensor",
                 lambda e, w=w: e.matmul(pC[:, 512:512 + w], oner_sb[:], row_sb[:, R_B:R_B + w],
                                         start=True, stop=True),
                 waits=[], inc=("pe", 1))
            PE_BC = p.c("pe")
            for j in range(16):
                p.ev("vector",
                     lambda e, j=j, c0=c0, w=w: e.tensor_mul(
                         sq_sb[0][:, 0:w], x0_sb[:, j * TW + c0: j * TW + c0 + w], pC[:, 0:w]),
                     waits=[("pe", PE_BC), ("ac", p.c("ac"))], inc=("dv", 1))
                p.ev("vector",
                     lambda e, j=j, c0=c0, w=w: e.tensor_tensor(
                         xln_sb[:, j * TW + c0: j * TW + c0 + w],
                         sq_sb[0][:, 0:w], pC[:, 512:512 + w], ALU.add),
                     waits=[("dv", p.c("dv"))], inc=("dv", 1))
        DV_LN = p.c("dv")
        import os
        if os.environ.get("DEBUG_A"):
            dbg = nc.dram_tensor("dbg_xln", [128, 16 * TW], BF16, kind="ExternalOutput").ap()
            p.ev("gpsimd", lambda e: e.dma_start(out=dbg[:], in_=xln_sb[:]),
                 waits=[("dv", DV_LN)], inc=("so", 16))
            dbg2 = nc.dram_tensor("dbg_row", [1, 6 * 512], F32, kind="ExternalOutput").ap()
            p.ev("gpsimd", lambda e: e.dma_start(out=dbg2[:], in_=row_sb[:]),
                 waits=[("dv", DV_LN)], inc=("so", 16))

        wuse = {}
        widx = 0
        XGRP = [(0, 512), (512, 512), (1024, 3)]
        GGRP = [(3, 512, 0), (515, 512, 512)]

        for h in range(NH):
            for obl in range(16):
                ob = h * 16 + obl
                buf = wbufs[widx % 2]
                p.ev("sync",
                     lambda e, buf=buf, ob=ob: e.dma_start(
                         out=buf.rearrange("p (kb o) -> p kb o", kb=16),
                         in_=wx.rearrange("(kb p) o -> p kb o", p=128)[:, :, ob * 128:(ob + 1) * 128]),
                     waits=[("pe", wuse.get(widx - 2, 0))], inc=("ld", 16))
                LD_W = p.c("ld")
                p.ev("gpsimd", (lambda buf: lambda e: e.tensor_scalar(
                    buf.bitcast(I32), buf.bitcast(I32), 0x8000, None, ALU.add))(buf),
                    waits=[("ld", LD_W)], inc=("po", 1))
                PO_W = p.c("po")
                for gi, (xc0, wdt) in enumerate(XGRP):
                    pst = [pA, pB, pA][gi]
                    for kb in range(16):
                        p.ev("tensor",
                             lambda e, kb=kb, buf=buf, xc0=xc0, wdt=wdt, pst=pst: e.matmul(
                                 pst[:, 0:wdt],
                                 bf16_view(buf[:, kb * 128:(kb + 1) * 128]),
                                 xln_sb[:, kb * TW + xc0: kb * TW + xc0 + wdt],
                                 start=(kb == 0), stop=(kb == 15)),
                             waits=([("ld", LD_W), ("dv", p.c("dv")), ("po", PO_W)] if kb == 0 else []),
                             inc=(("pe", 1) if kb == 15 else None))
                    p.ev("vector",
                         lambda e, xc0=xc0, wdt=wdt, pst=pst: e.tensor_copy(
                             xin_sb[:, xc0:xc0 + wdt], pst[:, 0:wdt]),
                         waits=[("pe", p.c("pe")), ("po", p.c("po"))], inc=("dv", 1))
                wuse[widx] = p.c("pe")
                DV_XIN = p.c("dv")

                def conv_emit(e, ob=ob):
                    wc = lambda k: convw_sb[:, ob * 4 + k: ob * 4 + k + 1]
                    e.tensor_scalar(xcv_sb[:], xin_sb[:, 0:TPC], wc(0), convb_sb[:, ob:ob + 1],
                                    ALU.mult, ALU.add)
                    e.scalar_tensor_tensor(xcv_sb[:], xin_sb[:, 1:1 + TPC], wc(1), xcv_sb[:], ALU.mult, ALU.add)
                    e.scalar_tensor_tensor(xcv_sb[:], xin_sb[:, 2:2 + TPC], wc(2), xcv_sb[:], ALU.mult, ALU.add)
                    return e.scalar_tensor_tensor(xcv_sb[:], xin_sb[:, 3:3 + TPC], wc(3), xcv_sb[:], ALU.mult, ALU.add)
                p.ev("vector", conv_emit, waits=[("dv", DV_XIN), ("ac", p.c("ac"))], inc=("dv", 1))
                p.ev("scalar", lambda e: e.activation(xcb_sb[:], xcv_sb[:], AF.Silu),
                     waits=[("dv", p.c("dv")), ("pe", p.c("pe")), ("so", p.c("so"))], inc=("ac", 1))
                AC_XC = p.c("ac")
                p.ev("gpsimd",
                     lambda e, ob=ob: e.dma_start(out=xc_i[ob * 128:(ob + 1) * 128, :], in_=xcb_sb[:]),
                     waits=[("ac", AC_XC)], inc=("so", 16))
                for gi, (pc0, wdt) in enumerate([(0, 512), (512, 512)]):
                    p.ev("tensor",
                         lambda e, ob=ob, obl=obl, pc0=pc0, wdt=wdt: e.matmul(
                             pC[:, pc0:pc0 + wdt],
                             bf16_view(xpdt_sb[:, ob * RK:(ob + 1) * RK]),
                             xcb_sb[:, pc0:pc0 + wdt],
                             start=(obl == 0), stop=(obl == 15)),
                         waits=([("ac", AC_XC), ("po", PO_RTNC)] if gi == 0 else []), inc=None)
                    p.ev("tensor",
                         lambda e, ob=ob, obl=obl, pc0=pc0, wdt=wdt: e.matmul(
                             pD[0:32, pc0:pc0 + wdt],
                             bf16_view(xpbc_sb[:, ob * 32:(ob + 1) * 32]),
                             xcb_sb[:, pc0:pc0 + wdt],
                             start=(obl == 0), stop=(obl == 15)),
                         waits=[], inc=(("pe", 1) if gi == 1 else None))
                widx += 1
            PE_XPROJ = p.c("pe")

            p.ev("vector",
                 lambda e, h=h: e.tensor_copy(dtin_sb[:], pC[:]),
                 waits=[("pe", PE_XPROJ), ("so", p.c("so"))], inc=("dv", 1))
            p.ev("vector",
                 lambda e, h=h: e.tensor_copy(bc_sb[:], pD[0:32, :]),
                 waits=[("pe", PE_XPROJ), ("so", p.c("so"))], inc=("dv", 1))
            DV_EVBC = p.c("dv")
            p.ev("gpsimd",
                 lambda e, h=h: e.dma_start(out=bmat[h * NS:(h + 1) * NS, :],
                                            in_=bc_sb[0:16, :]),
                 waits=[("dv", DV_EVBC)], inc=("so", 16))
            p.ev("gpsimd",
                 lambda e, h=h: e.dma_start(out=cmat[h * NS:(h + 1) * NS, :],
                                            in_=bc_sb[16:32, :]),
                 waits=[("dv", DV_EVBC)], inc=("so", 16))

            for db in range(16):
                gdb = h * 16 + db
                for gi, (pc0, wdt) in enumerate([(0, 512), (512, 512)]):
                    pst = [pA, pB][gi]
                    p.ev("tensor",
                         lambda e, h=h, db=db, pc0=pc0, wdt=wdt, pst=pst: e.matmul(
                             pst[:, 0:wdt],
                             bf16_view(dtw_sb[:, h * HD + db * 128: h * HD + (db + 1) * 128]),
                             dtin_sb[:, pc0:pc0 + wdt],
                             start=True, stop=True),
                         waits=[("dv", p.c("dv")), ("ac", p.c("ac"))],
                         inc=("pe", 1))
                    PE_DT = p.c("pe")
                    p.ev("scalar",
                         lambda e, pst=pst, wdt=wdt, gdb=gdb: e.activation(
                             sp_sb[0][:, 0:wdt], pst[:, 0:wdt], AF.Abs, bias=dtb_sb[:, gdb:gdb + 1]),
                         waits=[("pe", PE_DT), ("dv", p.c("dv"))], inc=("ac", 1))
                    p.ev("scalar",
                         lambda e, wdt=wdt: e.activation(
                             sp_sb[1][:, 0:wdt], sp_sb[0][:, 0:wdt], AF.Exp, scale=negc_sb[:, 0:1]),
                         inc=("ac", 1))
                    p.ev("scalar",
                         lambda e, wdt=wdt: e.activation(
                             sp_sb[2][:, 0:wdt], sp_sb[1][:, 0:wdt], AF.Ln, bias=onecol_sb[:, 0:1]),
                         inc=("ac", 1))
                    p.ev("scalar",
                         lambda e, pst=pst, wdt=wdt, gdb=gdb: e.activation(
                             sp_sb[0][:, 0:wdt], pst[:, 0:wdt], AF.Relu, bias=dtb_sb[:, gdb:gdb + 1]),
                         inc=("ac", 1))
                    AC_SP = p.c("ac")
                    p.ev("vector",
                         lambda e, pc0=pc0, wdt=wdt: e.tensor_add(
                             dt_sb[:, pc0:pc0 + wdt], sp_sb[0][:, 0:wdt], sp_sb[2][:, 0:wdt]),
                         waits=[("ac", AC_SP), ("so", p.c("so"))], inc=("dv", 1))
                DV_DT = p.c("dv")
                p.ev("gpsimd",
                     lambda e, gdb=gdb: e.dma_start(out=dt_i[gdb * 128:(gdb + 1) * 128, :], in_=dt_sb[:]),
                     waits=[("dv", DV_DT)], inc=("so", 16))

            for obl in range(16):
                ob = h * 16 + obl
                buf = wbufs[widx % 2]
                p.ev("sync",
                     lambda e, buf=buf, ob=ob: e.dma_start(
                         out=buf.rearrange("p (kb o) -> p kb o", kb=16),
                         in_=wg.rearrange("(kb p) o -> p kb o", p=128)[:, :, ob * 128:(ob + 1) * 128]),
                     waits=[("pe", wuse.get(widx - 2, 0))], inc=("ld", 16))
                LD_W = p.c("ld")
                p.ev("gpsimd", (lambda buf: lambda e: e.tensor_scalar(
                    buf.bitcast(I32), buf.bitcast(I32), 0x8000, None, ALU.add))(buf),
                    waits=[("ld", LD_W)], inc=("po", 1))
                PO_W = p.c("po")
                for gi, (xc0, wdt, oc0) in enumerate(GGRP):
                    pst = [pA, pB][gi]
                    for kb in range(16):
                        p.ev("tensor",
                             lambda e, kb=kb, buf=buf, xc0=xc0, wdt=wdt, pst=pst: e.matmul(
                                 pst[:, 0:wdt],
                                 bf16_view(buf[:, kb * 128:(kb + 1) * 128]),
                                 xln_sb[:, kb * TW + xc0: kb * TW + xc0 + wdt],
                                 start=(kb == 0), stop=(kb == 15)),
                             waits=([("ld", LD_W), ("ac", p.c("ac")), ("po", PO_W)] if kb == 0 else []),
                             inc=(("pe", 1) if kb == 15 else None))
                    p.ev("scalar",
                         lambda e, wdt=wdt, oc0=oc0, pst=pst: e.activation(
                             xcb_sb[:, oc0:oc0 + wdt], pst[:, 0:wdt], AF.Silu),
                         waits=[("pe", p.c("pe")), ("so", p.c("so"))], inc=("ac", 1))
                wuse[widx] = p.c("pe")
                AC_G = p.c("ac")
                p.ev("gpsimd",
                     lambda e, ob=ob: e.dma_start(out=gate_o[ob * 128:(ob + 1) * 128, :],
                                                  in_=xcb_sb[:]),
                     waits=[("ac", AC_G)], inc=("so", 16))
                widx += 1

        p.ev("vector", lambda e: e.memset(zero_sb[:], 0.0),
             waits=[("ac", p.c("ac"))], inc=("dv", 1))
        for ch in range(2):
            tc0 = ch * 512
            for h in range(NH):
                p.ev("sync",
                     lambda e, h=h, tc0=tc0: e.dma_start(
                         out=dta_v.rearrange("p (blk t) -> p blk t", t=512),
                         in_=dt_i[h * HD:(h + 1) * HD, tc0:tc0 + 512].rearrange("(blk p) t -> p blk t", p=128)),
                     waits=[("so", p.c("so")), ("pe", p.c("pe"))], inc=("ld", 16))
                p.ev("sync",
                     lambda e, h=h, tc0=tc0: e.dma_start(
                         out=xca_v.rearrange("p (blk t) -> p blk t", t=512),
                         in_=xc_i[h * HD:(h + 1) * HD, tc0:tc0 + 512].rearrange("(blk p) t -> p blk t", p=128)),
                     inc=("ld", 16))
                LD_A2 = p.c("ld")

                for n in range(NS):
                    p.ev("sync",
                         lambda e, h=h, n=n, tc0=tc0: e.dma_start(
                             out=bflat_sb[:],
                             in_=bmat[h * NS + n: h * NS + n + 1, tc0:tc0 + 512]),
                         waits=[("pe", p.c("pe"))], inc=("ld", 16))
                    p.ev("tensor",
                         lambda e, n=n: e.matmul(pA[:], oner_sb[:], bflat_sb[:],
                                                 start=True, stop=True),
                         waits=[("ld", p.c("ld")), ("ac", p.c("ac")), ("dv", p.c("dv"))],
                         inc=("pe", 1))
                    p.ev("scalar",
                         lambda e, n=n: e.activation(brep_v[:, n * 512:(n + 1) * 512], pA[:], AF.Identity),
                         waits=[("pe", p.c("pe"))], inc=("ac", 1))
                    p.ev("sync",
                         lambda e, h=h, n=n, tc0=tc0: e.dma_start(
                             out=cflat_sb[:],
                             in_=cmat[h * NS + n: h * NS + n + 1, tc0:tc0 + 512]),
                         waits=[("pe", p.c("pe"))], inc=("ld", 16))
                    p.ev("tensor",
                         lambda e, n=n: e.matmul(pA[:], oner_sb[:], cflat_sb[:],
                                                 start=True, stop=True),
                         waits=[("ld", p.c("ld")), ("ac", p.c("ac"))], inc=("pe", 1))
                    p.ev("scalar",
                         lambda e, n=n: e.activation(crep_v[:, n * 512:(n + 1) * 512], pA[:], AF.Identity),
                         waits=[("pe", p.c("pe"))], inc=("ac", 1))
                AC_REP = p.c("ac")

                for db in range(16):
                    blk = h * 16 + db
                    dta = dta_v[:, db * 512:(db + 1) * 512]
                    xca = xca_v[:, db * 512:(db + 1) * 512]
                    p.ev("vector",
                         lambda e, dta=dta, blk=blk: e.tensor_tensor_scan(
                             cdt_sb[:], dta, zero_sb[:], cdtc_sb[:, blk:blk + 1], ALU.add, ALU.add),
                         waits=[("ld", LD_A2), ("so", p.c("so"))], inc=("dv", 1))
                    p.ev("vector",
                         lambda e, blk=blk: e.tensor_copy(cdtc_sb[:, blk:blk + 1], cdt_sb[:, 511:512]),
                         waits=[("dv", p.c("dv"))], inc=("dv", 1))
                    DV_CDT = p.c("dv")
                    p.ev("gpsimd",
                         lambda e, blk=blk, tc0=tc0: e.dma_start(
                             out=cdt_o[blk * 128:(blk + 1) * 128, tc0:tc0 + 512], in_=cdt_sb[:]),
                         waits=[("dv", DV_CDT)], inc=("so", 16))
                    p.ev("gpsimd",
                         lambda e, dta=dta, xca=xca: e.tensor_mul(dtx_sb[:], dta, xca),
                         waits=[("dv", DV_CDT)], inc=("po", 1))
                    PO_DTX = p.c("po")
                    for n in range(NS):
                        p.ev("scalar",
                             lambda e, dta=dta, blk=blk, n=n: e.activation(
                                 da_sb[:], dta, AF.Exp,
                                 scale=aneg_sb[:, blk * NS + n: blk * NS + n + 1]),
                             waits=[("ac", AC_REP), ("dv", p.c("dv")), ("ld", LD_A2)], inc=("ac", 1))
                        p.ev("gpsimd",
                             lambda e, n=n: e.tensor_mul(dbu_sb[:], dtx_sb[:],
                                                         brep_v[:, n * 512:(n + 1) * 512]),
                             waits=[("po", PO_DTX), ("dv", p.c("dv"))], inc=("po", 1))
                        p.ev("vector",
                             lambda e, n=n, blk=blk: e.tensor_tensor_scan(
                                 h_v[:, n * 512:(n + 1) * 512], da_sb[:], dbu_sb[:],
                                 hst_sb[:, blk * NS + n: blk * NS + n + 1], ALU.mult, ALU.add),
                             waits=[("ac", p.c("ac")), ("po", p.c("po")), ("pe", p.c("pe"))],
                             inc=("dv", 1))
                        p.ev("vector",
                             lambda e, n=n: e.tensor_mul(g_sb[:], crep_v[:, n * 512:(n + 1) * 512],
                                                         h_v[:, n * 512:(n + 1) * 512]),
                             waits=[("dv", p.c("dv"))], inc=("dv", 1))
                        p.ev("tensor",
                             lambda e, n=n: e.matmul(pB[:], idnb_sb[:], g_sb[:],
                                                     start=(n == 0), stop=(n == NS - 1)),
                             waits=[("dv", p.c("dv"))], inc=("pe", 1))
                    p.ev("vector",
                         lambda e, blk=blk: e.tensor_copy(
                             hst_sb[:, blk * NS:(blk + 1) * NS],
                             h_v.rearrange("p (n t) -> p n t", n=NS)[:, :, 511:512].squeeze(2)),
                         waits=[("pe", p.c("pe")), ("dv", p.c("dv"))], inc=("dv", 1))
                    p.ev("vector",
                         lambda e, xca=xca, blk=blk: e.scalar_tensor_tensor(
                             ya_sb[:], xca, dcol_sb[:, blk:blk + 1], pB[:], ALU.mult, ALU.add),
                         inc=("dv", 1))
                    DV_Y = p.c("dv")
                    p.ev("gpsimd",
                         lambda e, blk=blk, tc0=tc0: e.dma_start(
                             out=y_loc[blk * 128:(blk + 1) * 128, tc0:tc0 + 512], in_=ya_sb[:]),
                         waits=[("dv", DV_Y)], inc=("so", 16))

        p.ev("gpsimd",
             lambda e: e.dma_start(out=hfin[:], in_=hst_sb[:]),
             waits=[("dv", p.c("dv"))], inc=("so", 16))
        p.ev("gpsimd",
             lambda e: e.dma_start(out=cdtfin[:], in_=cdtc_sb[:]),
             waits=[("dv", p.c("dv"))], inc=("so", 16))
        p.ev("gpsimd", lambda e: None, waits=[("so", p.c("so"))], inc=None)

        with nc.Block() as block:
            p.emit(block, sems)
    return nc


def prep_A(ins):
    x0 = ins["L0_attn_hidden_states"]
    wxT = np.ascontiguousarray(ins["L0_attn_in_proj_w"][0::2, :].T)
    wgT = np.ascontiguousarray(ins["L0_attn_in_proj_w"][1::2, :].T)
    def blkmaj(a, inner):
        return np.ascontiguousarray(
            a.reshape(32, 128, inner).transpose(1, 0, 2).reshape(128, 32 * inner))
    convw = blkmaj(ins["L0_attn_conv_w"].reshape(IQ, 4), 4)
    convb = blkmaj(ins["L0_attn_conv_b"].reshape(IQ, 1), 1)
    xpw = ins["L0_attn_x_proj_weight"]
    xpw_dt = blkmaj(np.ascontiguousarray(xpw[:, :RK, :].transpose(0, 2, 1)).reshape(IQ, RK), RK)
    xpw_bc = blkmaj(np.ascontiguousarray(xpw[:, RK:, :].transpose(0, 2, 1)).reshape(IQ, 32), 32)
    dtwT = np.ascontiguousarray(
        ins["L0_attn_dt_proj_weight"].transpose(0, 2, 1).reshape(NH, RK, HD)
        .transpose(1, 0, 2).reshape(RK, NH * HD))
    dtb = blkmaj(ins["L0_attn_dt_proj_bias"].reshape(IQ, 1), 1)
    alog = blkmaj(np.ascontiguousarray(ins["L0_attn_A_log"].reshape(IQ, NS)), NS)
    dcol = blkmaj(ins["L0_attn_D"].reshape(IQ, 1), 1)
    onec = np.ones((128, 1), np.float32)
    oner = np.ones((1, 128), np.float32)
    idn = np.eye(128, dtype=np.float32)

    in_maps = []
    for c in range(8):
        b, q = c // 4, c % 4
        t0 = q * TPC
        xs = np.zeros((HQ, TW), np.float32)
        lo = max(0, t0 - 3)
        xs[:, 3 - (t0 - lo):] = x0[b, lo:t0 + TPC, :].T
        in_maps.append({
            "x0T": np.ascontiguousarray(xs), "wx": wxT, "wg": wgT,
            "convw": convw, "convb": convb, "xpw_dt": xpw_dt, "xpw_bc": xpw_bc,
            "dtw": dtwT, "dtb": dtb, "alog": alog, "dcol": dcol,
            "onec": onec, "oner": oner, "idn": idn,
        })
    return in_maps



def build_B(stage=6):
    nc = bass.Bass("TRN2", target_bir_lowering=False)
    IN = lambda n, s, dt=F32: nc.dram_tensor(n, s, dt, kind="ExternalInput").ap()
    OUT = lambda n, s, dt=F32: nc.dram_tensor(n, s, dt, kind="ExternalOutput").ap()

    y_loc = IN("y_loc", [IQ, TPC])
    cdt = IN("cdt", [IQ, TPC])
    cmat = IN("cmat", [NH * NS, TPC])
    gate = IN("gate", [IQ, TPC], BF16)
    hprev = IN("hprev", [3 * 128, 32 * NS])
    cprev = IN("cprev", [3 * 128, 32])
    alog = IN("alog", [128, 32 * NS])
    x0T = IN("x0T", [HQ, TPC])
    opw = IN("opw", [IQ, HQ])
    wgT = IN("wgT", [HQ, 8192])
    wuT = IN("wuT", [HQ, 8192])
    wdT = IN("wdT", [8192, HQ])
    wlnc = IN("wlnc", [128, 64])
    wq = IN("wq", [HQ, HQ])
    wk = IN("wk", [HQ, HQ])
    wv = IN("wv", [HQ, HQ])
    cosT = IN("cosT", [128, TPC])
    sinT = IN("sinT", [128, TPC])
    onec = IN("onec", [128, 1])
    oner = IN("oner", [1, 128])
    idn = IN("idn", [128, 128])

    x2o = OUT("x2o", [HQ, TPC])
    qO = OUT("qO", [HQ, TPC], BF16)
    kO = OUT("kO", [HQ, TPC], BF16)
    vO = OUT("vO", [TPC, HQ], BF16)
    x1dbg = OUT("x1dbg", [HQ, TPC]) if stage == 2 else None

    x1_i = nc.dram_tensor("x1_i", [HQ, TPC], F32, kind="Internal").ap()

    p = Prog()
    with ExitStack() as es:
        sb = lambda name, shape, dt=F32: es.enter_context(nc.sbuf_tensor(name, shape, dt))
        psalloc = lambda name, shape, dt=F32: es.enter_context(nc.psum_tensor(name, shape, dt))
        sems = {}
        for s in ["ld", "so", "pe", "ac", "dv", "po"]:
            sems[s] = es.enter_context(nc.semaphore(s))

        BH = sb("BH", [128, 64 * TPC], BF16)
        BHf = BH[:].bitcast(F32)
        qx_sb = sb("qx_sb", [128, 16 * TPC], BF16)
        wgs = sb("wgs", [128, 2048])
        wus = sb("wus", [128, 2048])
        qw_sb = sb("qw_sb", [128, 4096], BF16)
        ytile2 = [BHf[:, 16384:16896], BHf[:, 19968:20480]]
        ctile2 = [BHf[:, 16896:17408], BHf[:, 20480:20992]]
        ptile2 = [BHf[:, 17408:17920], BHf[:, 20992:21504]]
        aneg_sbv = BHf[:, 17920:18432]
        h0_sbv = BHf[:, 18432:18944]
        g2t2 = [BH[:, 37888:38400], BH[:, 43008:43520]]
        gtile2 = [BH[:, 38400:38912], BH[:, 43520:44032]]
        scpair = sb("scpair", [128, 1024])
        sc1 = scpair[:, 0:512]
        sc2 = scpair[:, 512:1024]
        qs2 = scpair[:].bitcast(BF16)
        sc3 = sb("sc3", [128, 512], BF16)
        x0t_sb = sb("x0t_sb", [128, 512])
        sq2_sb = sb("sq2_sb", [128, 512])
        hp_sb = BHf[:, 0:512]
        pf_sb = BHf[:, 512:1024]
        cp_sb = sb("cp_sb", [128, 32])
        wln_sb = sb("wln_sb", [128, 64])
        onec_sb = sb("onec_sb", [128, 1])
        oner_sb = sb("oner_sb", [1, 128])
        idn_sb = sb("idn_sb", [128, 128])
        idnb_sb = sb("idnb_sb", [128, 128], BF16)
        cflat_sb = BHf[0:1, 19456:19968]
        row_sb = sb("row_sb", [1, 4 * 512])
        qxf = qx_sb[:].bitcast(F32)
        scl_sb = sb("scl_sb", [1, 16])
        thr_sb = sb("thr_sb", [128, 8])
        scbc_sb = sb("scbc_sb", [128, 4 * 512], BF16)
        sxc_sb = sb("sxc_sb", [128, 48])
        pk_sb = sb("pk_sb", [128, 16])
        pr_sb = sb("pr_sb", [16, 128])

        pA = psalloc("pA", [128, 512])
        pB = psalloc("pB", [128, 512])
        pC = psalloc("pC", [128, 1024])
        pD = psalloc("pD", [128, 1024])
        pE = psalloc("pE", [128, 128])

        for dst, src2 in [(onec_sb[:], onec[:]), (oner_sb[:], oner[:]), (idn_sb[:], idn[:]),
                          (aneg_sbv, alog[:]), (wln_sb[:], wlnc[:])]:
            p.ev("sync", (lambda d, s: lambda e: e.dma_start(out=d, in_=s))(dst, src2),
                 inc=("ld", 16))
        LD0 = p.c("ld")
        p.ev("scalar", lambda e: e.activation(aneg_sbv, aneg_sbv, AF.Exp),
             waits=[("ld", LD0)], inc=("ac", 1))
        p.ev("vector", lambda e: e.tensor_scalar_mul(aneg_sbv, aneg_sbv, -1.0),
             waits=[("ac", p.c("ac"))], inc=("dv", 1))
        p.ev("vector", lambda e: e.memset(h0_sbv, 0.0), inc=("dv", 1))
        p.ev("vector", lambda e: e.tensor_copy(idnb_sb[:], idn_sb[:]),
             waits=[("ld", LD0)], inc=("dv", 1))
        for j in range(3):
            p.ev("sync", (lambda j: lambda e: e.dma_start(
                out=hp_sb, in_=hprev[j * 128:(j + 1) * 128, :]))(j),
                waits=[("dv", p.c("dv"))], inc=("ld", 16))
            p.ev("sync", (lambda j: lambda e: e.dma_start(
                out=cp_sb[:], in_=cprev[j * 128:(j + 1) * 128, :]))(j), inc=("ld", 16))
            LDJ = p.c("ld")
            for blk in range(32):
                p.ev("scalar", (lambda blk: lambda e: e.activation(
                    pf_sb[:, blk * NS:(blk + 1) * NS], aneg_sbv[:, blk * NS:(blk + 1) * NS],
                    AF.Exp, scale=cp_sb[:, blk:blk + 1]))(blk),
                    waits=([("ld", LDJ), ("dv", p.c("dv"))] if blk == 0 else []),
                    inc=("ac", 1))
            p.ev("vector", lambda e: e.tensor_mul(h0_sbv, h0_sbv, pf_sb),
                 waits=[("ac", p.c("ac")), ("dv", p.c("dv"))], inc=("dv", 1))
            p.ev("vector", lambda e: e.tensor_add(h0_sbv, h0_sbv, hp_sb),
                 waits=[("dv", p.c("dv"))], inc=("dv", 1))
        DV_H0 = p.c("dv")

        strm_v = BHf[:, 0:8192]
        absv = BHf[:, 8192:16384]
        for m, wsrc in [(0, wgT), (1, wuT), (2, wdT)]:
            nrow, ncol = wsrc.shape
            rpt = (nrow // 128) // 16
            for t in range(16):
                p.ev("sync", (lambda wsrc, t, rpt, ncol: lambda e: e.dma_start(
                    out=strm_v.rearrange("p (r c) -> p r c", c=ncol),
                    in_=wsrc.rearrange("(r p) c -> p r c", p=128)[:, t * rpt:(t + 1) * rpt, :]))(wsrc, t, rpt, ncol),
                    waits=[("ac", p.c("ac"))], inc=("ld", 16))
                p.ev("scalar", lambda e: e.activation(absv, strm_v, AF.Abs),
                     waits=[("ld", p.c("ld"))], inc=("ac", 1))
                AC_ABS = p.c("ac")
                for cc in range(16):
                    p.ev("tensor", (lambda cc, t: lambda e: e.matmul(
                        pA[0:1, 0:512], onec_sb[:], absv[:, cc * 512:(cc + 1) * 512],
                        start=(t == 0 and cc == 0), stop=(t == 15 and cc == 15)))(cc, t),
                        waits=([("ac", AC_ABS)] if cc == 0 else []),
                        inc=(("pe", 1) if (t == 15 and cc == 15) else None))
            PE_SUM = p.c("pe")
            p.ev("vector", lambda e: e.reduce_sum(row_sb[:, 0:1], pA[0:1, 0:512], axis=AX.X),
                 waits=[("pe", PE_SUM), ("dv", p.c("dv"))], inc=("dv", 1))
            p.ev("vector", (lambda m: lambda e: e.tensor_scalar(
                scl_sb[:, m:m + 1], row_sb[:, 0:1], 1.0 / (HQ * 8192), None, ALU.mult))(m),
                waits=[("dv", p.c("dv"))], inc=("dv", 1))
            p.ev("vector", (lambda m: lambda e: e.tensor_scalar(
                scl_sb[:, 4 + m:5 + m], scl_sb[:, m:m + 1], 0.5, None, ALU.mult))(m),
                waits=[("dv", p.c("dv"))], inc=("dv", 1))
            p.ev("vector", (lambda m: lambda e: e.tensor_scalar(
                scl_sb[:, 8 + m:9 + m], scl_sb[:, 4 + m:5 + m], -1.0, None, ALU.mult))(m),
                waits=[("dv", p.c("dv"))], inc=("dv", 1))
            DVm = p.c("dv")
            p.ev("tensor", (lambda m: lambda e: e.matmul(
                pB[:, 0:1], oner_sb[:], scl_sb[:, 8 + m:9 + m], start=True, stop=True))(m),
                waits=[("dv", DVm), ("ac", p.c("ac"))], inc=("pe", 1))
            p.ev("scalar", (lambda m: lambda e: e.activation(
                thr_sb[:, 4 + m:5 + m], pB[:, 0:1], AF.Identity))(m),
                waits=[("pe", p.c("pe"))], inc=("ac", 1))
            p.ev("tensor", (lambda m: lambda e: e.matmul(
                pB[:, 0:1], oner_sb[:], scl_sb[:, 4 + m:5 + m], start=True, stop=True))(m),
                waits=[("ac", p.c("ac"))], inc=("pe", 1))
            p.ev("scalar", (lambda m: lambda e: e.activation(
                thr_sb[:, m:m + 1], pB[:, 0:1], AF.Identity))(m),
                waits=[("pe", p.c("pe"))], inc=("ac", 1))

        crep_v = BH[:, 0:16 * 512]
        xmix_v = BH[:, 16 * 512:48 * 512]
        for ch in range(2):
            tc0 = ch * 512
            for h in range(NH):
                for n in range(NS):
                    p.ev("sync", (lambda h, n, tc0: lambda e: e.dma_start(
                        out=cflat_sb, in_=cmat[h * NS + n: h * NS + n + 1, tc0:tc0 + 512]))(h, n, tc0),
                        waits=[("pe", p.c("pe"))], inc=("ld", 16))
                    p.ev("tensor", lambda e: e.matmul(pA[:], oner_sb[:], cflat_sb,
                                                      start=True, stop=True),
                         waits=[("ld", p.c("ld")), ("ac", p.c("ac")), ("dv", p.c("dv"))],
                         inc=("pe", 1))
                    p.ev("scalar", (lambda n: lambda e: e.activation(
                        crep_v[:, n * 512:(n + 1) * 512], pA[:], AF.Identity))(n),
                        waits=[("pe", p.c("pe"))], inc=("ac", 1))
                AC_CREP = p.c("ac")
                if (ch, h) == (0, 0):
                    gT = {0: {"ac": 0}, 1: {"ac": 0}}
                    gP = {0: {"dv": 0}, 1: {"dv": 0}}
                    gG = {0: {"pe": 0}, 1: {"pe": 0}}
                    gSC = {"dv": 0}
                for db in range(16):
                    blk = h * 16 + db
                    bpar = blk % 2
                    ctile = ctile2[bpar]; ytile = ytile2[bpar]; gtile = gtile2[bpar]
                    p.ev("sync", (lambda blk, tc0, ctile: lambda e: e.dma_start(
                        out=ctile, in_=cdt[blk * 128:(blk + 1) * 128, tc0:tc0 + 512]))(blk, tc0, ctile),
                        waits=[("ac", gT[bpar]["ac"]), ("dv", gSC["dv"])], inc=("ld", 16))
                    p.ev("sync", (lambda blk, tc0, ytile: lambda e: e.dma_start(
                        out=ytile, in_=y_loc[blk * 128:(blk + 1) * 128, tc0:tc0 + 512]))(blk, tc0, ytile),
                        inc=("ld", 16))
                    p.ev("sync", (lambda blk, tc0, gtile: lambda e: e.dma_start(
                        out=gtile, in_=gate[blk * 128:(blk + 1) * 128, tc0:tc0 + 512]))(blk, tc0, gtile),
                        inc=("ld", 16))
                    LD_T = p.c("ld")
                    for n in range(NS):
                        npar = n % 2
                        ptile = ptile2[npar]; g2t = g2t2[npar]
                        p.ev("scalar", (lambda blk, n, ptile, ctile: lambda e: e.activation(
                            ptile, ctile, AF.Exp,
                            scale=aneg_sbv[:, blk * NS + n: blk * NS + n + 1]))(blk, n, ptile, ctile),
                            waits=[("ld", LD_T), ("ac", AC_CREP), ("dv", gP[npar]["dv"])],
                            inc=("ac", 1))
                        AC_P = p.c("ac")
                        p.ev("vector", (lambda blk, n, ptile, g2t: lambda e: e.scalar_tensor_tensor(
                            g2t, ptile, h0_sbv[:, blk * NS + n: blk * NS + n + 1],
                            crep_v[:, n * 512:(n + 1) * 512], ALU.mult, ALU.mult))(blk, n, ptile, g2t),
                            waits=[("ac", AC_P), ("dv", DV_H0), ("pe", gG[npar]["pe"])],
                            inc=("dv", 1))
                        gP[npar]["dv"] = p.c("dv")
                        p.ev("tensor", (lambda n, g2t: lambda e: e.matmul(
                            pB[:], idnb_sb[:], g2t, start=(n == 0), stop=(n == NS - 1)))(n, g2t),
                            waits=[("dv", p.c("dv"))], inc=("pe", 1))
                        gG[npar]["pe"] = p.c("pe")
                    p.ev("vector", (lambda ytile: lambda e: e.tensor_add(sc1, ytile, pB[:]))(ytile),
                         waits=[("pe", p.c("pe")), ("dv", p.c("dv"))], inc=("dv", 1))
                    p.ev("vector", (lambda blk, gtile: lambda e: e.tensor_tensor(
                        xmix_v[:, blk * 512:(blk + 1) * 512], sc1, gtile, ALU.mult))(blk, gtile),
                        waits=[("dv", p.c("dv"))], inc=("dv", 1))
                    gT[bpar]["ac"] = 0
                    gT[bpar]["ac"] = AC_P
                    gSC["dv"] = p.c("dv")
            DV_XMIX = p.c("dv")
            for ob in range(16):
                p.ev("sync", (lambda ob: lambda e: e.dma_start(
                    out=wgs[:].rearrange("p (kb o) -> p kb o", kb=16),
                    in_=opw.rearrange("(kb p) o -> p kb o", p=128)[:, 0:16, ob * 128:(ob + 1) * 128]))(ob),
                    waits=[("pe", p.c("pe"))], inc=("ld", 16))
                p.ev("sync", (lambda ob: lambda e: e.dma_start(
                    out=wus[:].rearrange("p (kb o) -> p kb o", kb=16),
                    in_=opw.rearrange("(kb p) o -> p kb o", p=128)[:, 16:32, ob * 128:(ob + 1) * 128]))(ob),
                    inc=("ld", 16))
                LD_OP = p.c("ld")
                p.ev("gpsimd", lambda e: e.tensor_scalar(
                    wgs[:].bitcast(I32), wgs[:].bitcast(I32), 0x8000, None, ALU.add),
                    waits=[("ld", LD_OP)], inc=("po", 1))
                p.ev("gpsimd", lambda e: e.tensor_scalar(
                    wus[:].bitcast(I32), wus[:].bitcast(I32), 0x8000, None, ALU.add),
                    inc=("po", 1))
                PO_OP = p.c("po")
                p.ev("sync", (lambda ob, tc0: lambda e: e.dma_start(
                    out=x0t_sb[:], in_=x0T[ob * 128:(ob + 1) * 128, tc0:tc0 + 512]))(ob, tc0),
                    inc=("ld", 16))
                LD_X0 = p.c("ld")
                for kb in range(32):
                    buf2 = wgs if kb < 16 else wus
                    kbl = kb % 16
                    p.ev("tensor", (lambda kb, kbl, buf2: lambda e: e.matmul(
                        pC[:, 0:512], bf16_view(buf2[:, kbl * 128:(kbl + 1) * 128]),
                        xmix_v[:, kb * 512:(kb + 1) * 512],
                        start=(kb == 0), stop=(kb == 31)))(kb, kbl, buf2),
                        waits=([("ld", LD_OP), ("dv", DV_XMIX), ("dv", p.c("dv")),
                                ("po", PO_OP)] if kb == 0 else []),
                        inc=(("pe", 1) if kb == 31 else None))
                p.ev("vector", lambda e: e.tensor_add(sc1, pC[:, 0:512], x0t_sb[:]),
                     waits=[("pe", p.c("pe")), ("ld", LD_X0), ("so", p.c("so"))], inc=("dv", 1))
                DV_X1 = p.c("dv")
                p.ev("gpsimd", (lambda ob, tc0: lambda e: e.dma_start(
                    out=x1_i[ob * 128:(ob + 1) * 128, tc0:tc0 + 512], in_=sc1))(ob, tc0),
                    waits=[("dv", DV_X1)], inc=("so", 16))
                if stage == 2:
                    p.ev("gpsimd", (lambda ob, tc0: lambda e: e.dma_start(
                        out=x1dbg[ob * 128:(ob + 1) * 128, tc0:tc0 + 512], in_=sc1))(ob, tc0),
                        inc=("so", 16))
        if stage == 2:
            p.ev("gpsimd", lambda e: None, waits=[("so", p.c("so"))], inc=None)
            with nc.Block() as block:
                p.emit(block, sems)
            return nc

        x1c_v = BHf[:, 0:1024]
        inv_h = 1.0 / HQ
        R_M, R_VAR, R_A, R_B = (i * 512 for i in range(4))
        p.ev("vector", lambda e: e.memset(sxc_sb[:], 0.0), waits=[("dv", p.c("dv"))], inc=("dv", 1))

        def ln_rows(w, pe_stats):
            def rowstep(fn, waits):
                p.ev("vector", fn, waits=waits, inc=("dv", 1))
            rowstep(lambda e: e.tensor_scalar_mul(row_sb[:, R_M:R_M + w], pA[0:1, 0:w], inv_h),
                    [("pe", pe_stats)])
            rowstep(lambda e: e.tensor_scalar_mul(row_sb[:, R_VAR:R_VAR + w], pB[0:1, 0:w], inv_h),
                    [("dv", p.c("dv"))])
            rowstep(lambda e: e.tensor_mul(row_sb[:, R_B:R_B + w], row_sb[:, R_M:R_M + w],
                                           row_sb[:, R_M:R_M + w]), [("dv", p.c("dv"))])
            rowstep(lambda e: e.tensor_sub(row_sb[:, R_VAR:R_VAR + w], row_sb[:, R_VAR:R_VAR + w],
                                           row_sb[:, R_B:R_B + w]), [("dv", p.c("dv"))])
            rowstep(lambda e: e.tensor_scalar_add(row_sb[:, R_VAR:R_VAR + w],
                                                  row_sb[:, R_VAR:R_VAR + w], 1e-5),
                    [("dv", p.c("dv"))])
            p.ev("scalar",
                 lambda e: e.activation(row_sb[:, R_VAR:R_VAR + w], row_sb[:, R_VAR:R_VAR + w],
                                        AF.Sqrt),
                 waits=[("dv", p.c("dv"))], inc=("ac", 1))
            rowstep(lambda e: e.reciprocal(row_sb[:, R_A:R_A + w], row_sb[:, R_VAR:R_VAR + w]),
                    [("ac", p.c("ac"))])
            rowstep(lambda e: e.tensor_mul(row_sb[:, R_B:R_B + w], row_sb[:, R_M:R_M + w],
                                           row_sb[:, R_A:R_A + w]), [("dv", p.c("dv"))])
            rowstep(lambda e: e.tensor_scalar_mul(row_sb[:, R_B:R_B + w], row_sb[:, R_B:R_B + w], -1.0),
                    [("dv", p.c("dv"))])

        for (c0, w) in [(0, 512), (512, 512)]:
            for j in range(16):
                p.ev("sync", (lambda j, c0, w: lambda e: e.dma_start(
                    out=x1c_v[:, 0:w], in_=x1_i[j * 128:(j + 1) * 128, c0:c0 + w]))(j, c0, w),
                    waits=[("so", p.c("so")), ("ac", p.c("ac")), ("pe", p.c("pe")),
                           ("dv", p.c("dv"))], inc=("ld", 16))
                LD_X1 = p.c("ld")
                p.ev("scalar", (lambda w: lambda e: e.activation(
                    sq2_sb[:, 0:w], x1c_v[:, 0:w], AF.Square))(w),
                    waits=[("ld", LD_X1), ("dv", p.c("dv"))], inc=("ac", 1))
                p.ev("tensor", (lambda j, w: lambda e: e.matmul(
                    pA[0:1, 0:w], onec_sb[:], x1c_v[:, 0:w],
                    start=(j == 0), stop=(j == 15)))(j, w),
                    waits=[("ld", LD_X1)], inc=None)
                p.ev("tensor", (lambda j, w: lambda e: e.matmul(
                    pB[0:1, 0:w], onec_sb[:], sq2_sb[:, 0:w],
                    start=(j == 0), stop=(j == 15)))(j, w),
                    waits=[("ac", p.c("ac"))], inc=("pe", 1))
            PE_STATS = p.c("pe")
            ln_rows(w, PE_STATS)
            DV_ROW = p.c("dv")
            p.ev("tensor",
                 lambda e, w=w: e.matmul(pC[:, 0:w], oner_sb[:], row_sb[:, R_A:R_A + w],
                                         start=True, stop=True),
                 waits=[("dv", DV_ROW)], inc=None)
            p.ev("tensor",
                 lambda e, w=w: e.matmul(pC[:, 512:512 + w], oner_sb[:], row_sb[:, R_B:R_B + w],
                                         start=True, stop=True),
                 waits=[], inc=("pe", 1))
            PE_BC = p.c("pe")
            for j in range(16):
                p.ev("sync", (lambda j, c0, w: lambda e: e.dma_start(
                    out=x1c_v[:, 0:w], in_=x1_i[j * 128:(j + 1) * 128, c0:c0 + w]))(j, c0, w),
                    waits=[("dv", p.c("dv")), ("pe", p.c("pe"))], inc=("ld", 16))
                LD_X1B = p.c("ld")
                p.ev("vector", (lambda w: lambda e: e.tensor_mul(
                    sq2_sb[:, 0:w], x1c_v[:, 0:w], pC[:, 0:w]))(w),
                    waits=[("ld", LD_X1B), ("pe", PE_BC), ("ac", p.c("ac"))], inc=("dv", 1))
                p.ev("vector", (lambda w: lambda e: e.tensor_tensor(
                    sc1[0:128, 0:w], sq2_sb[:, 0:w], pC[:, 512:512 + w], ALU.add))(w),
                    waits=[("dv", p.c("dv")), ("pe", p.c("pe"))], inc=("dv", 1))
                DV_XF = p.c("dv")
                p.ev("vector", (lambda j, c0, w: lambda e: e.tensor_copy(
                    qx_sb[:, j * TPC + c0: j * TPC + c0 + w], sc1[0:128, 0:w]))(j, c0, w),
                    waits=[("dv", DV_XF)], inc=("dv", 1))
                for tt in range(4):
                    gt = (c0 // 128) + tt
                    p.ev("tensor", (lambda tt: lambda e: e.transpose(
                        pE[:].bitcast(F32), sc1[:, tt * 128:(tt + 1) * 128], idn_sb[:]))(tt),
                        waits=[("dv", DV_XF), ("dv", p.c("dv"))], inc=("pe", 1))
                    p.ev("vector", lambda e: e.reduce_max(
                        sxc_sb[:, 32:33], pE[:], axis=AX.X, apply_absolute_value=True),
                        waits=[("pe", p.c("pe")), ("dv", p.c("dv"))], inc=("dv", 1))
                    p.ev("vector", (lambda gt: lambda e: e.tensor_max(
                        sxc_sb[:, gt:gt + 1], sxc_sb[:, gt:gt + 1], sxc_sb[:, 32:33]))(gt),
                        waits=[("dv", p.c("dv"))], inc=("dv", 1))
        def colstep(fn):
            p.ev("vector", fn, waits=[("dv", p.c("dv"))], inc=("dv", 1))
        colstep(lambda e: e.tensor_scalar_max(sxc_sb[:, 0:8], sxc_sb[:, 0:8], 1e-5))
        colstep(lambda e: e.tensor_scalar_mul(sxc_sb[:, 8:16], sxc_sb[:, 0:8], 1.0 / 127.0))
        colstep(lambda e: e.reciprocal(sxc_sb[:, 16:24], sxc_sb[:, 8:16]))
        colstep(lambda e: e.tensor_copy(pk_sb[:, 0:8], sxc_sb[:, 16:24]))
        colstep(lambda e: e.tensor_copy(pk_sb[:, 8:16], sxc_sb[:, 8:16]))
        DV_PK = p.c("dv")
        p.ev("tensor", lambda e: e.transpose(pE[0:16, :], pk_sb[:], idn_sb[:]),
             waits=[("dv", DV_PK)], inc=("pe", 1))
        p.ev("scalar", lambda e: e.activation(pr_sb[:], pE[0:16, :], AF.Identity),
             waits=[("pe", p.c("pe"))], inc=("ac", 1))
        AC_PR = p.c("ac")
        for g in range(8):
            p.ev("gpsimd", (lambda g: lambda e: e.dma_start(
                out=BHf[0:1, 2048 + g * 128:2048 + (g + 1) * 128],
                in_=pr_sb[g:g + 1, :]))(g),
                waits=([("ac", AC_PR)] if g == 0 else []), inc=("so", 16))
            p.ev("gpsimd", (lambda g: lambda e: e.dma_start(
                out=BHf[0:1, 3072 + g * 128:3072 + (g + 1) * 128],
                in_=pr_sb[8 + g:9 + g, :]))(g),
                inc=("so", 16))
        SO_SX = p.c("so")
        for ch in range(2):
            tc0 = ch * 512
            p.ev("tensor", (lambda tc0: lambda e: e.matmul(
                pC[:, 0:512], oner_sb[:], BHf[0:1, 2048 + tc0:2048 + tc0 + 512],
                start=True, stop=True))(tc0),
                waits=[("so", SO_SX), ("dv", p.c("dv")), ("ac", p.c("ac"))], inc=("pe", 1))
            PE_SX = p.c("pe")
            for j in range(16):
                p.ev("vector", (lambda j, tc0: lambda e: e.tensor_tensor(
                    sc1, qx_sb[:, j * TPC + tc0: j * TPC + tc0 + 512], pC[:, 0:512], ALU.mult))(j, tc0),
                    waits=[("pe", PE_SX), ("dv", p.c("dv"))], inc=("dv", 1))
                p.ev("vector", (lambda j, tc0: lambda e: e.tensor_scalar(
                    qx_sb[:, j * TPC + tc0: j * TPC + tc0 + 512], sc1, MAGIC, MAGIC,
                    ALU.add, ALU.subtract))(j, tc0),
                    waits=[("dv", p.c("dv"))], inc=("dv", 1))
        DV_QX = p.c("dv")

        p.ev("vector", lambda e: e.tensor_scalar(
            row_sb[:, 0:TPC], BHf[0:1, 3072:3072 + TPC], scl_sb[0:1, 4:5], None, ALU.mult),
            waits=[("dv", p.c("dv"))], inc=("dv", 1))
        p.ev("vector", lambda e: e.tensor_scalar(
            row_sb[:, TPC:2 * TPC], BHf[0:1, 3072:3072 + TPC], scl_sb[0:1, 5:6], None, ALU.mult),
            waits=[("dv", p.c("dv"))], inc=("dv", 1))
        DV_SC = p.c("dv")
        for i4, (r0) in enumerate([0, 512, TPC, TPC + 512]):
            p.ev("tensor", (lambda i4, r0: lambda e: e.matmul(
                pC[:, 0:512], oner_sb[:], row_sb[:, r0:r0 + 512], start=True, stop=True))(i4, r0),
                waits=[("dv", DV_SC), ("ac", p.c("ac"))], inc=("pe", 1))
            p.ev("scalar", (lambda i4: lambda e: e.activation(
                scbc_sb[:, i4 * 512:(i4 + 1) * 512], pC[:, 0:512], AF.Identity))(i4),
                waits=[("pe", p.c("pe"))], inc=("ac", 1))
        AC_SCBC = p.c("ac")

        h_v = BH[:]
        p.ev("vector", lambda e: e.memset(sxc_sb[:, 24:32], 0.0),
             waits=[("dv", p.c("dv"))], inc=("dv", 1))
        for iob in range(64):
            p.ev("sync", (lambda iob: lambda e: e.dma_start(
                out=wgs[:].rearrange("p (kb c) -> p kb c", kb=16),
                in_=wgT.rearrange("(kb p) c -> p kb c", p=128)[:, :, iob * 128:(iob + 1) * 128]))(iob),
                waits=[("ac", p.c("ac"))], inc=("ld", 16))
            p.ev("sync", (lambda iob: lambda e: e.dma_start(
                out=wus[:].rearrange("p (kb c) -> p kb c", kb=16),
                in_=wuT.rearrange("(kb p) c -> p kb c", p=128)[:, :, iob * 128:(iob + 1) * 128]))(iob),
                inc=("ld", 16))
            LD_W = p.c("ld")
            p.ev("scalar", lambda e: e.activation(qw_sb[:, 0:2048], wgs[:], AF.Sign, bias=thr_sb[:, 4:5]),
                 waits=[("ld", LD_W), ("dv", p.c("dv")), ("pe", p.c("pe"))], inc=("ac", 1))
            p.ev("scalar", lambda e: e.activation(qs2, wgs[:], AF.Sign, bias=thr_sb[:, 0:1]),
                 inc=("ac", 1))
            p.ev("vector", lambda e: e.tensor_add(qw_sb[:, 0:2048], qw_sb[:, 0:2048], qs2),
                 waits=[("ac", p.c("ac"))], inc=("dv", 1))
            p.ev("scalar", lambda e: e.activation(qw_sb[:, 2048:4096], wus[:], AF.Sign, bias=thr_sb[:, 5:6]),
                 waits=[("dv", p.c("dv"))], inc=("ac", 1))
            p.ev("scalar", lambda e: e.activation(qs2, wus[:], AF.Sign, bias=thr_sb[:, 1:2]),
                 inc=("ac", 1))
            p.ev("vector", lambda e: e.tensor_add(qw_sb[:, 2048:4096], qw_sb[:, 2048:4096], qs2),
                 waits=[("ac", p.c("ac"))], inc=("dv", 1))
            DV_QW = p.c("dv")
            for kb in range(16):
                p.ev("tensor", (lambda kb: lambda e: e.matmul(
                    pA[:], qw_sb[:, kb * 128:(kb + 1) * 128],
                    qx_sb[:, kb * TPC:kb * TPC + 512],
                    start=(kb == 0), stop=(kb == 15)))(kb),
                    waits=([("dv", DV_QW), ("dv", DV_QX)] if kb == 0 else []), inc=None)
                p.ev("tensor", (lambda kb: lambda e: e.matmul(
                    pB[:], qw_sb[:, kb * 128:(kb + 1) * 128],
                    qx_sb[:, kb * TPC + 512:kb * TPC + 1024],
                    start=(kb == 0), stop=(kb == 15)))(kb), inc=None)
                p.ev("tensor", (lambda kb: lambda e: e.matmul(
                    pC[:, 0:512], qw_sb[:, 2048 + kb * 128:2048 + (kb + 1) * 128],
                    qx_sb[:, kb * TPC:kb * TPC + 512],
                    start=(kb == 0), stop=(kb == 15)))(kb), inc=None)
                p.ev("tensor", (lambda kb: lambda e: e.matmul(
                    pC[:, 512:1024], qw_sb[:, 2048 + kb * 128:2048 + (kb + 1) * 128],
                    qx_sb[:, kb * TPC + 512:kb * TPC + 1024],
                    start=(kb == 0), stop=(kb == 15)))(kb),
                    inc=(("pe", 1) if kb == 15 else None))
            PE_GU = p.c("pe")
            for c in range(2):
                gps = [pA, pB][c]
                ups_c0 = 512 * c
                p.ev("vector", (lambda gps, c: lambda e: e.tensor_tensor(
                    sc1, gps[:], scbc_sb[:, c * 512:(c + 1) * 512], ALU.mult))(gps, c),
                    waits=[("pe", PE_GU), ("dv", p.c("dv")), ("ac", AC_SCBC)], inc=("dv", 1))
                p.ev("scalar", lambda e: e.activation(sc3[:], sc1, AF.Silu),
                     waits=[("dv", p.c("dv"))], inc=("ac", 1))
                p.ev("vector", (lambda ups_c0, c: lambda e: e.tensor_tensor(
                    sc2, pC[:, ups_c0:ups_c0 + 512], scbc_sb[:, 1024 + c * 512:1024 + (c + 1) * 512],
                    ALU.mult))(ups_c0, c),
                    waits=[("dv", p.c("dv"))], inc=("dv", 1))
                p.ev("vector", (lambda iob, c: lambda e: e.tensor_tensor(
                    h_v[:, iob * TPC + c * 512: iob * TPC + (c + 1) * 512], sc3[:], sc2,
                    ALU.mult))(iob, c),
                    waits=[("ac", p.c("ac")), ("dv", p.c("dv"))], inc=("dv", 1))
                DV_H = p.c("dv")
                p.ev("scalar", (lambda iob, c: lambda e: e.activation(
                    sq2_sb[:], h_v[:, iob * TPC + c * 512: iob * TPC + (c + 1) * 512], AF.Square))(iob, c),
                    waits=[("dv", DV_H)], inc=("ac", 1))
                p.ev("tensor", (lambda iob, c: lambda e: e.matmul(
                    pD[0:1, c * 512:(c + 1) * 512], onec_sb[:], sq2_sb[:],
                    start=(iob == 0), stop=(iob == 63)))(iob, c),
                    waits=[("ac", p.c("ac"))], inc=("pe", 1))
                p.ev("vector", (lambda iob, c: lambda e: e.tensor_scalar(
                    sc1, h_v[:, iob * TPC + c * 512: iob * TPC + (c + 1) * 512],
                    wln_sb[:, iob:iob + 1], None, ALU.mult))(iob, c),
                    waits=[("dv", p.c("dv"))], inc=("dv", 1))
                DV_HW = p.c("dv")
                for tt in range(4):
                    gt = c * 4 + tt
                    p.ev("tensor", (lambda tt: lambda e: e.transpose(
                        pE[:].bitcast(F32), sc1[:, tt * 128:(tt + 1) * 128], idn_sb[:]))(tt),
                        waits=[("dv", DV_HW), ("dv", p.c("dv"))], inc=("pe", 1))
                    p.ev("vector", lambda e: e.reduce_max(
                        sxc_sb[:, 32:33], pE[:], axis=AX.X, apply_absolute_value=True),
                        waits=[("pe", p.c("pe")), ("dv", p.c("dv"))], inc=("dv", 1))
                    p.ev("vector", (lambda gt: lambda e: e.tensor_max(
                        sxc_sb[:, 24 + gt:25 + gt], sxc_sb[:, 24 + gt:25 + gt], sxc_sb[:, 32:33]))(gt),
                        waits=[("dv", p.c("dv"))], inc=("dv", 1))
        PE_SQ = p.c("pe")
        p.ev("vector", lambda e: e.tensor_scalar_mul(row_sb[:, 0:1024], pD[0:1, 0:1024], 1.0 / 8192.0),
             waits=[("pe", PE_SQ), ("dv", p.c("dv"))], inc=("dv", 1))
        p.ev("vector", lambda e: e.tensor_scalar_add(row_sb[:, 0:1024], row_sb[:, 0:1024], 1e-6),
             waits=[("dv", p.c("dv"))], inc=("dv", 1))
        p.ev("scalar", lambda e: e.activation(row_sb[:, 0:1024], row_sb[:, 0:1024], AF.Sqrt),
             waits=[("dv", p.c("dv"))], inc=("ac", 1))
        p.ev("vector", lambda e: e.reciprocal(row_sb[:, 1024:2048], row_sb[:, 0:1024]),
             waits=[("ac", p.c("ac"))], inc=("dv", 1))
        colstep(lambda e: e.tensor_scalar_max(sxc_sb[:, 24:32], sxc_sb[:, 24:32], 1e-5))
        colstep(lambda e: e.tensor_scalar_mul(sxc_sb[:, 40:48], sxc_sb[:, 24:32], 1.0 / 127.0))
        colstep(lambda e: e.reciprocal(sxc_sb[:, 32:40], sxc_sb[:, 40:48]))
        colstep(lambda e: e.tensor_copy(pk_sb[:, 0:8], sxc_sb[:, 32:40]))
        colstep(lambda e: e.tensor_copy(pk_sb[:, 8:16], sxc_sb[:, 40:48]))
        DV_PK2 = p.c("dv")
        p.ev("tensor", lambda e: e.transpose(pE[0:16, :], pk_sb[:], idn_sb[:]),
             waits=[("dv", DV_PK2)], inc=("pe", 1))
        p.ev("scalar", lambda e: e.activation(pr_sb[:], pE[0:16, :], AF.Identity),
             waits=[("pe", p.c("pe"))], inc=("ac", 1))
        AC_PR2 = p.c("ac")
        for g in range(8):
            p.ev("gpsimd", (lambda g: lambda e: e.dma_start(
                out=qxf[0:1, g * 128:(g + 1) * 128],
                in_=pr_sb[g:g + 1, :]))(g),
                waits=([("ac", AC_PR2), ("so", p.c("so")), ("pe", p.c("pe"))] if g == 0 else []),
                inc=("so", 16))
            p.ev("gpsimd", (lambda g: lambda e: e.dma_start(
                out=qxf[0:1, TPC + g * 128:TPC + (g + 1) * 128],
                in_=pr_sb[8 + g:9 + g, :]))(g),
                inc=("so", 16))
        SO_SH = p.c("so")
        p.ev("vector", lambda e: e.tensor_mul(row_sb[:, 0:TPC], row_sb[:, 1024:2048],
                                              qxf[0:1, 0:TPC]),
             waits=[("so", SO_SH), ("dv", p.c("dv"))], inc=("dv", 1))
        p.ev("vector", lambda e: e.tensor_scalar(
            row_sb[:, TPC:2 * TPC], qxf[0:1, TPC:2 * TPC], scl_sb[0:1, 6:7], None, ALU.mult),
            waits=[("dv", p.c("dv"))], inc=("dv", 1))
        DV_RS = p.c("dv")
        for i4, r0 in enumerate([0, 512, TPC, TPC + 512]):
            p.ev("tensor", (lambda r0: lambda e: e.matmul(
                pC[:, 0:512], oner_sb[:], row_sb[:, r0:r0 + 512], start=True, stop=True))(r0),
                waits=[("dv", DV_RS), ("ac", p.c("ac"))], inc=("pe", 1))
            p.ev("scalar", (lambda i4: lambda e: e.activation(
                scbc_sb[:, i4 * 512:(i4 + 1) * 512], pC[:, 0:512], AF.Identity))(i4),
                waits=[("pe", p.c("pe"))], inc=("ac", 1))
        AC_RSBC = p.c("ac")
        for iob in range(64):
            for c in range(2):
                p.ev("vector", (lambda iob, c: lambda e: e.scalar_tensor_tensor(
                    sc1, h_v[:, iob * TPC + c * 512: iob * TPC + (c + 1) * 512],
                    wln_sb[:, iob:iob + 1], scbc_sb[:, c * 512:(c + 1) * 512],
                    ALU.mult, ALU.mult))(iob, c),
                    waits=[("ac", AC_RSBC), ("dv", p.c("dv"))], inc=("dv", 1))
                p.ev("vector", (lambda iob, c: lambda e: e.tensor_scalar(
                    h_v[:, iob * TPC + c * 512: iob * TPC + (c + 1) * 512], sc1,
                    MAGIC, MAGIC, ALU.add, ALU.subtract))(iob, c),
                    waits=[("dv", p.c("dv"))], inc=("dv", 1))
        DV_QH = p.c("dv")

        for ob in range(16):
            for c in range(2):
                tc0 = c * 512
                for obq in range(4):
                    p.ev("sync", (lambda ob, obq: lambda e: e.dma_start(
                        out=wgs[:].rearrange("p (kb o) -> p kb o", kb=16),
                        in_=wdT.rearrange("(kb p) o -> p kb o", p=128)[:, obq * 16:(obq + 1) * 16,
                                                                      ob * 128:(ob + 1) * 128]))(ob, obq),
                        waits=[("ac", p.c("ac"))], inc=("ld", 16))
                    LD_WD = p.c("ld")
                    p.ev("scalar", lambda e: e.activation(qw_sb[:, 0:2048], wgs[:], AF.Sign, bias=thr_sb[:, 6:7]),
                         waits=[("ld", LD_WD), ("dv", p.c("dv")), ("pe", p.c("pe"))], inc=("ac", 1))
                    p.ev("scalar", lambda e: e.activation(qs2, wgs[:], AF.Sign, bias=thr_sb[:, 2:3]),
                         inc=("ac", 1))
                    p.ev("vector", lambda e: e.tensor_add(qw_sb[:, 0:2048], qw_sb[:, 0:2048], qs2),
                         waits=[("ac", p.c("ac"))], inc=("dv", 1))
                    DV_QD = p.c("dv")
                    for kbl in range(16):
                        kb = obq * 16 + kbl
                        p.ev("tensor", (lambda kb, kbl, tc0: lambda e: e.matmul(
                            pA[:], qw_sb[:, kbl * 128:(kbl + 1) * 128],
                            h_v[:, kb * TPC + tc0: kb * TPC + tc0 + 512],
                            start=(kb == 0), stop=(kb == 63)))(kb, kbl, tc0),
                            waits=([("dv", DV_QD), ("dv", DV_QH)] if kbl == 0 else []),
                            inc=(("pe", 1) if kbl == 15 else None))
                PE_DN = p.c("pe")
                p.ev("sync", (lambda ob, tc0: lambda e: e.dma_start(
                    out=x0t_sb[:], in_=x1_i[ob * 128:(ob + 1) * 128, tc0:tc0 + 512]))(ob, tc0),
                    waits=[("dv", p.c("dv"))], inc=("ld", 16))
                LD_X1T = p.c("ld")
                p.ev("vector", (lambda c: lambda e: e.tensor_tensor(
                    sc1, pA[:], scbc_sb[:, 1024 + c * 512:1024 + (c + 1) * 512], ALU.mult))(c),
                    waits=[("pe", PE_DN), ("dv", p.c("dv")), ("ac", AC_RSBC)], inc=("dv", 1))
                p.ev("vector", lambda e: e.tensor_add(sc2, sc1, x0t_sb[:]),
                     waits=[("ld", LD_X1T), ("dv", p.c("dv")), ("so", p.c("so"))], inc=("dv", 1))
                DV_X2 = p.c("dv")
                p.ev("gpsimd", (lambda ob, tc0: lambda e: e.dma_start(
                    out=x2o[ob * 128:(ob + 1) * 128, tc0:tc0 + 512], in_=sc2))(ob, tc0),
                    waits=[("dv", DV_X2)], inc=("so", 16))

        x2c_v = BHf[:, 0:1024]
        cos_v = BHf[:, 1024:2048]
        sin_v = BHf[:, 2048:3072]
        rop_v = BHf[:, 3072:4096]
        qf_v = BHf[:, 4096:5120]
        vf_v = BHf[:, 5120:6144]
        a_v = BHf[:, 6144:7168]
        p.ev("sync", lambda e: e.dma_start(out=cos_v, in_=cosT[:]),
             waits=[("pe", p.c("pe")), ("dv", p.c("dv")), ("so", p.c("so"))], inc=("ld", 16))
        p.ev("sync", lambda e: e.dma_start(out=sin_v, in_=sinT[:]), inc=("ld", 16))
        LD_CS = p.c("ld")
        for (c0, w) in [(0, 512), (512, 512)]:
            for j in range(16):
                p.ev("sync", (lambda j, c0, w: lambda e: e.dma_start(
                    out=x2c_v[:, 0:w], in_=x2o[j * 128:(j + 1) * 128, c0:c0 + w]))(j, c0, w),
                    waits=[("so", p.c("so")), ("ac", p.c("ac")), ("pe", p.c("pe")),
                           ("dv", p.c("dv"))], inc=("ld", 16))
                LD_X2 = p.c("ld")
                p.ev("scalar", (lambda w: lambda e: e.activation(
                    sq2_sb[:, 0:w], x2c_v[:, 0:w], AF.Square))(w),
                    waits=[("ld", LD_X2), ("dv", p.c("dv"))], inc=("ac", 1))
                p.ev("tensor", (lambda j, w: lambda e: e.matmul(
                    pA[0:1, 0:w], onec_sb[:], x2c_v[:, 0:w],
                    start=(j == 0), stop=(j == 15)))(j, w),
                    waits=[("ld", LD_X2)], inc=None)
                p.ev("tensor", (lambda j, w: lambda e: e.matmul(
                    pB[0:1, 0:w], onec_sb[:], sq2_sb[:, 0:w],
                    start=(j == 0), stop=(j == 15)))(j, w),
                    waits=[("ac", p.c("ac"))], inc=("pe", 1))
            PE_ST2 = p.c("pe")
            ln_rows(w, PE_ST2)
            DV_ROW2 = p.c("dv")
            p.ev("tensor",
                 lambda e, w=w: e.matmul(pC[:, 0:w], oner_sb[:], row_sb[:, R_A:R_A + w],
                                         start=True, stop=True),
                 waits=[("dv", DV_ROW2)], inc=None)
            p.ev("tensor",
                 lambda e, w=w: e.matmul(pC[:, 512:512 + w], oner_sb[:], row_sb[:, R_B:R_B + w],
                                         start=True, stop=True),
                 waits=[], inc=("pe", 1))
            PE_BC2 = p.c("pe")
            for j in range(16):
                p.ev("sync", (lambda j, c0, w: lambda e: e.dma_start(
                    out=x2c_v[:, 0:w], in_=x2o[j * 128:(j + 1) * 128, c0:c0 + w]))(j, c0, w),
                    waits=[("dv", p.c("dv")), ("pe", p.c("pe"))], inc=("ld", 16))
                LD_X2B = p.c("ld")
                p.ev("vector", (lambda w: lambda e: e.tensor_mul(
                    sq2_sb[:, 0:w], x2c_v[:, 0:w], pC[:, 0:w]))(w),
                    waits=[("ld", LD_X2B), ("pe", PE_BC2)], inc=("dv", 1))
                p.ev("vector", (lambda j, c0, w: lambda e: e.tensor_tensor(
                    qx_sb[:, j * TPC + c0: j * TPC + c0 + w], sq2_sb[:, 0:w],
                    pC[:, 512:512 + w], ALU.add))(j, c0, w),
                    waits=[("dv", p.c("dv"))], inc=("dv", 1))
        DV_LN2 = p.c("dv")

        for wsrc, dst, mode in [(wq, qO, "rope"), (wk, kO, "rope"), (wv, vO, "v")]:
            for ob in range(16):
                p.ev("sync", (lambda wsrc, ob: lambda e: e.dma_start(
                    out=wgs[:].rearrange("p (kb o) -> p kb o", kb=16),
                    in_=wsrc.rearrange("(kb p) o -> p kb o", p=128)[:, :, ob * 128:(ob + 1) * 128]))(wsrc, ob),
                    waits=[("pe", p.c("pe"))], inc=("ld", 16))
                LD_QW = p.c("ld")
                p.ev("gpsimd", lambda e: e.tensor_scalar(
                    wgs[:].bitcast(I32), wgs[:].bitcast(I32), 0x8000, None, ALU.add),
                    waits=[("ld", LD_QW)], inc=("po", 1))
                PO_QW = p.c("po")
                for kb in range(16):
                    p.ev("tensor", (lambda kb: lambda e: e.matmul(
                        pC[:, 0:512], bf16_view(wgs[:, kb * 128:(kb + 1) * 128]),
                        qx_sb[:, kb * TPC:kb * TPC + 512],
                        start=(kb == 0), stop=(kb == 15)))(kb),
                        waits=([("ld", LD_QW), ("dv", DV_LN2), ("ac", p.c("ac")),
                                ("dv", p.c("dv")), ("po", PO_QW)] if kb == 0 else []), inc=None)
                    p.ev("tensor", (lambda kb: lambda e: e.matmul(
                        pC[:, 512:1024], bf16_view(wgs[:, kb * 128:(kb + 1) * 128]),
                        qx_sb[:, kb * TPC + 512:kb * TPC + 1024],
                        start=(kb == 0), stop=(kb == 15)))(kb),
                        inc=(("pe", 1) if kb == 15 else None))
                PE_QKV = p.c("pe")
                if mode == "rope":
                    p.ev("scalar", lambda e: e.activation(qf_v, pC[:, 0:1024], AF.Identity),
                         waits=[("pe", PE_QKV), ("ld", LD_CS), ("dv", p.c("dv"))], inc=("ac", 1))
                    AC_QF = p.c("ac")
                    p.ev("vector", lambda e: e.tensor_scalar_mul(rop_v[0:64, :], qf_v[64:128, :], -1.0),
                         waits=[("ac", AC_QF), ("dv", p.c("dv"))], inc=("dv", 1))
                    p.ev("vector", lambda e: e.tensor_copy(rop_v[64:128, :], qf_v[0:64, :]),
                         waits=[("ac", AC_QF)], inc=("dv", 1))
                    p.ev("vector", lambda e: e.tensor_mul(a_v, qf_v, cos_v),
                         waits=[("ac", AC_QF)], inc=("dv", 1))
                    p.ev("vector", lambda e: e.tensor_mul(rop_v, rop_v, sin_v),
                         waits=[("dv", p.c("dv"))], inc=("dv", 1))
                    p.ev("vector", lambda e: e.tensor_add(a_v, a_v, rop_v),
                        waits=[("dv", p.c("dv"))], inc=("dv", 1))
                    p.ev("vector", lambda e: e.tensor_copy(sc3[:, 0:512], a_v[:, 0:512]),
                         waits=[("dv", p.c("dv")), ("so", p.c("so"))], inc=("dv", 1))
                    p.ev("vector", lambda e: e.tensor_copy(gtile, a_v[:, 512:1024]),
                         inc=("dv", 1))
                    DV_RP = p.c("dv")
                    p.ev("gpsimd", (lambda dst, ob: lambda e: e.dma_start(
                        out=dst[ob * 128:(ob + 1) * 128, 0:512], in_=sc3[:, 0:512]))(dst, ob),
                        waits=[("dv", DV_RP)], inc=("so", 16))
                    p.ev("gpsimd", (lambda dst, ob: lambda e: e.dma_start(
                        out=dst[ob * 128:(ob + 1) * 128, 512:1024], in_=gtile))(dst, ob),
                        inc=("so", 16))
                else:
                    p.ev("scalar", lambda e: e.activation(vf_v, pC[:, 0:1024], AF.Identity),
                         waits=[("pe", PE_QKV), ("dv", p.c("dv"))], inc=("ac", 1))
                    AC_VF = p.c("ac")
                    for tt in range(8):
                        p.ev("tensor", (lambda tt: lambda e: e.transpose(
                            pE[:].bitcast(F32), vf_v[:, tt * 128:(tt + 1) * 128], idn_sb[:]))(tt),
                            waits=[("ac", AC_VF), ("so", p.c("so")), ("ac", p.c("ac"))], inc=("pe", 1))
                        p.ev("scalar", lambda e: e.activation(sc3[:, 0:128], pE[:], AF.Identity),
                             waits=[("pe", p.c("pe")), ("so", p.c("so"))], inc=("ac", 1))
                        AC_VT = p.c("ac")
                        p.ev("gpsimd", (lambda ob, tt: lambda e: e.dma_start(
                            out=vO[tt * 128:(tt + 1) * 128, ob * 128:(ob + 1) * 128],
                            in_=sc3[:, 0:128]))(ob, tt),
                            waits=[("ac", AC_VT)], inc=("so", 16))
        p.ev("gpsimd", lambda e: None, waits=[("so", p.c("so"))], inc=None)

        with nc.Block() as block:
            p.emit(block, sems)
    return nc


def prep_B(ins, ares):
    opwT = np.ascontiguousarray(ins["L0_attn_out_proj_w"].T)
    wgT = np.ascontiguousarray(ins["L0_ffn_w_gate"].T)
    wuT = np.ascontiguousarray(ins["L0_ffn_w_up"].T)
    wdT = np.ascontiguousarray(ins["L0_ffn_w_down"].T)
    wlnc = np.ascontiguousarray(ins["L0_ffn_ln_weight"].reshape(64, 128).T)
    wq = np.ascontiguousarray(ins["L1_attn_Wq"])
    wk = np.ascontiguousarray(ins["L1_attn_Wk"])
    wv = np.ascontiguousarray(ins["L1_attn_Wv"])
    alog = None
    x0 = ins["L0_attn_hidden_states"]

    def blkmaj(a, inner):
        return np.ascontiguousarray(
            a.reshape(32, 128, inner).transpose(1, 0, 2).reshape(128, 32 * inner))
    alog = blkmaj(np.ascontiguousarray(ins["L0_attn_A_log"].reshape(IQ, NS)), NS)

    hd = 128
    inv_freq = 1.0 / (10000.0 ** (np.arange(0, hd, 2, dtype=np.float64)[: hd // 2] / hd))
    pos = np.arange(SQ, dtype=np.float64)
    ang = pos[:, None] * inv_freq[None, :]
    ang = np.concatenate([ang, ang], axis=-1)
    cosf = np.cos(ang).astype(np.float32).T
    sinf = np.sin(ang).astype(np.float32).T
    onec = np.ones((128, 1), np.float32)
    oner = np.ones((1, 128), np.float32)
    idn = np.eye(128, dtype=np.float32)

    in_maps = []
    for c in range(8):
        b, q = c // 4, c % 4
        t0 = q * TPC
        hprev = np.zeros((3 * 128, 32 * NS), np.float32)
        cprev = np.zeros((3 * 128, 32), np.float32)
        for slot in range(3):
            j = q - 3 + slot
            if j >= 0:
                src = ares[4 * b + j]
                hprev[slot * 128:(slot + 1) * 128] = np.asarray(src["hfin"], np.float32)
                cprev[slot * 128:(slot + 1) * 128] = np.asarray(src["cdtfin"], np.float32)
        r = ares[c]
        in_maps.append({
            "y_loc": np.asarray(r["y_loc"], np.float32),
            "cdt": np.asarray(r["cdt_o"], np.float32),
            "cmat": np.asarray(r["cmat"], np.float32),
            "gate": np.asarray(r["gate_o"]),
            "hprev": hprev, "cprev": cprev, "alog": alog,
            "x0T": np.ascontiguousarray(x0[b, t0:t0 + TPC, :].T),
            "opw": opwT, "wgT": wgT, "wuT": wuT, "wdT": wdT, "wlnc": wlnc,
            "wq": wq, "wk": wk, "wv": wv,
            "cosT": np.ascontiguousarray(cosf[:, t0:t0 + TPC]),
            "sinT": np.ascontiguousarray(sinf[:, t0:t0 + TPC]),
            "onec": onec, "oner": oner, "idn": idn,
        })
    return in_maps





def _finish_C(nc, p, sems, outO, x3_sb, es):
    for ob in range(16):
        p.ev("gpsimd", (lambda ob: lambda e: e.dma_start(
            out=outO[ob * 128:(ob + 1) * 128, :], in_=x3_sb[:, ob * TPC:(ob + 1) * TPC]))(ob),
            waits=([("dv", p.c("dv")), ("ac", p.c("ac")), ("pe", p.c("pe"))] if ob == 0 else []),
            inc=("so", 16))
    p.ev("gpsimd", lambda e: None, waits=[("so", p.c("so"))], inc=None)
    with nc.Block() as block:
        p.emit(block, sems)
    es.close()
    return nc

def build_C(stop=None):
    nc = bass.Bass("TRN2", target_bir_lowering=False)
    IN = lambda n, s, dt=F32: nc.dram_tensor(n, s, dt, kind="ExternalInput").ap()
    OUT = lambda n, s, dt=F32: nc.dram_tensor(n, s, dt, kind="ExternalOutput").ap()

    x2T = IN("x2T", [HQ, TPC])
    qT = IN("qT", [HQ, TPC], BF16)
    kF = IN("kF", [HQ, SQ], BF16)
    vF = IN("vF", [SQ, HQ], BF16)
    wo = IN("wo", [HQ, HQ])
    wr = IN("wr", [HQ, 8])
    wgm = IN("wgm", [8 * HQ, 1408])
    wum = IN("wum", [8 * HQ, 1408])
    wdm = IN("wdm", [8 * 1408, HQ])
    maskS = IN("maskS", [128, 512])
    maskTT = IN("maskTT", [128, 512])
    onec = IN("onec", [128, 1])
    oner = IN("oner", [1, 128])
    idn = IN("idn", [128, 128])
    outO = OUT("outO", [HQ, TPC])

    ISQ = 1.0 / float(np.sqrt(np.float32(128.0)))

    p = Prog()
    with ExitStack() as es:
        sb = lambda name, shape, dt=F32: es.enter_context(nc.sbuf_tensor(name, shape, dt))
        psalloc = lambda name, shape, dt=F32: es.enter_context(nc.psum_tensor(name, shape, dt))
        sems = {}
        for s in ["ld", "so", "pe", "ac", "dv", "po"]:
            sems[s] = es.enter_context(nc.semaphore(s))

        x3_sb = sb("x3_sb", [128, 16 * TPC])
        cxf_sb = sb("cxf_sb", [128, 16 * TPC], BF16)
        hf_v = cxf_sb[:, 0:11 * TPC]
        xln3_sb = sb("xln3_sb", [128, 16 * TPC], BF16)
        kv_sb = sb("kv_sb", [128, 2 * SQ + TPC], BF16)
        kv2_sb = sb("kv2_sb", [128, 2 * SQ + TPC], BF16)
        crows_sb = kv_sb[0:1, 0:8 * TPC]
        wp_sb = sb("wp_sb", [128, 4096])
        etf_sb = sb("etf_sb", [128, 2 * TPC])
        etb_sb = sb("etb_sb", [128, 2 * TPC], BF16)
        fac_sb = sb("fac_sb", [128, TPC], BF16)
        mask_sb = sb("mask_sb", [128, 512])
        maskt_sb = sb("maskt_sb", [128, 512])
        sm_sb = sb("sm_sb", [128, 128])
        sc1 = sb("c_sc1", [128, 512])
        sc3 = etb_sb[:, 0:512]
        cbc_sb = sb("cbc_sb", [128, 1024], BF16)
        rsb_sb = sb("rsb_sb", [8, 1024])
        rT_sb = etf_sb[:, 0:64]
        cmb_sb = etf_sb[:, 64:128]
        cmt_sb = etf_sb[0:64, 128:192].bitcast(BF16)
        onerb_sb = etb_sb[0:1, 512:640]
        mcol_sb = sb("mcol_sb", [128, 32])
        prow_sb = x3_sb[0:1, 0:TPC]
        pk_sb = sb("pk_sb", [128, 8])
        row_sb = etf_sb[0:1, 0:2048]
        onec_sb = sb("c_onec", [128, 1])
        oner_sb = sb("c_oner", [1, 128])
        idn_sb = sb("c_idn", [128, 128])

        pA = psalloc("pA", [128, 512])
        pB = psalloc("pB", [128, 512])
        pC = psalloc("pC", [128, 1024])
        pD = psalloc("pD", [128, 1024])
        pE = psalloc("pE", [128, 1024])

        mrun = lambda s: mcol_sb[:, s:s + 1]
        lrun = lambda s: mcol_sb[:, 8 + s:9 + s]
        facc = lambda s: mcol_sb[:, 16 + s:17 + s]
        tmpc = mcol_sb[:, 24:25]
        isqc = mcol_sb[:, 26:27]

        for dst, src2 in [(onec_sb[:], onec[:]), (oner_sb[:], oner[:]), (idn_sb[:], idn[:]),
                          (mask_sb[:], maskS[:]), (maskt_sb[:], maskTT[:])]:
            p.ev("sync", (lambda d, s: lambda e: e.dma_start(out=d, in_=s))(dst, src2),
                 inc=("ld", 16))
        LD0 = p.c("ld")
        p.ev("vector", lambda e: e.memset(isqc, ISQ), waits=[("ld", LD0)], inc=("dv", 1))
        p.ev("vector", lambda e: e.tensor_copy(onerb_sb, oner_sb[:]), inc=("dv", 1))

        if stop == "ld":
            return _finish_C(nc, p, sems, outO, x3_sb, es)
        gAB = {0: {"dv": 0, "ac": 0}, 1: {"dv": 0, "ac": 0}}
        gE = {0: {"dv": 0, "ac": 0}, 1: {"dv": 0, "ac": 0}}
        gD = {"ac": 0}
        pe_ctx_done = {0: 0, 1: 0}
        for h in range(16):
            kvb = [kv_sb, kv2_sb][h % 2]
            kview = kvb[:, 0:SQ]
            vview = kvb[:, SQ:2 * SQ]
            qview = kvb[:, 2 * SQ:2 * SQ + TPC]
            p.ev("sync", (lambda h, kvb: lambda e: e.dma_start(
                out=kvb[:, 0:SQ], in_=kF[h * 128:(h + 1) * 128, :]))(h, kvb),
                waits=[("pe", p.c("pe")), ("dv", p.c("dv"))], inc=("ld", 16))
            p.ev("sync", (lambda h, kvb: lambda e: e.dma_start(
                out=kvb[:, SQ:2 * SQ].rearrange("p (kc d) -> p kc d", d=128),
                in_=vF[:, h * 128:(h + 1) * 128].rearrange("(kc p) d -> p kc d", p=128)))(h, kvb),
                inc=("ld", 16))
            p.ev("sync", (lambda h, kvb: lambda e: e.dma_start(
                out=kvb[:, 2 * SQ:2 * SQ + TPC], in_=qT[h * 128:(h + 1) * 128, :]))(h, kvb),
                inc=("ld", 16))
            LD_KV = p.c("ld")
            for slot in range(8):
                p.ev("vector", (lambda slot: lambda e: e.memset(mrun(slot), -1e30))(slot),
                     waits=[("dv", p.c("dv"))], inc=("dv", 1))
                for kb in range(slot + 1):
                    par = kb % 2
                    ps = [pA, pB][par]
                    p.ev("tensor", (lambda slot, kb, qview, kview, ps: lambda e: e.matmul(
                        ps[:], qview[:, slot * 128:(slot + 1) * 128],
                        kview[:, kb * 512:(kb + 1) * 512], start=True, stop=True))(slot, kb, qview, kview, ps),
                        waits=[("ld", LD_KV), ("dv", gAB[par]["dv"]), ("ac", gAB[par]["ac"])],
                        inc=("pe", 1))
                    PE_S = p.c("pe")
                    if kb == slot:
                        p.ev("vector", (lambda ps: lambda e: e.tensor_add(sc1[:], ps[:], mask_sb[:]))(ps),
                             waits=[("pe", PE_S), ("dv", p.c("dv"))], inc=("dv", 1))
                        p.ev("vector", lambda e: e.reduce_max(tmpc, sc1[:], axis=AX.X),
                             waits=[("dv", p.c("dv"))], inc=("dv", 1))
                    else:
                        p.ev("vector", (lambda ps: lambda e: e.reduce_max(tmpc, ps[:], axis=AX.X))(ps),
                             waits=[("pe", PE_S), ("dv", p.c("dv"))], inc=("dv", 1))
                    gAB[par]["dv"] = p.c("dv")
                    p.ev("vector", (lambda slot: lambda e: e.tensor_max(
                        mrun(slot), mrun(slot), tmpc))(slot),
                        waits=[("dv", p.c("dv"))], inc=("dv", 1))
                if stop == "p1":
                    continue
                p.ev("vector", (lambda slot: lambda e: e.tensor_scalar_mul(
                    tmpc, mrun(slot), -ISQ))(slot),
                    waits=[("dv", p.c("dv"))], inc=("dv", 1))
                p.ev("vector", (lambda slot: lambda e: e.memset(lrun(slot), 0.0))(slot), inc=("dv", 1))
                DV_BN = p.c("dv")
                for kb in range(slot + 1):
                    par = kb % 2
                    ps = [pA, pB][par]
                    p.ev("tensor", (lambda slot, kb, qview, kview, ps: lambda e: e.matmul(
                        ps[:], qview[:, slot * 128:(slot + 1) * 128],
                        kview[:, kb * 512:(kb + 1) * 512], start=True, stop=True))(slot, kb, qview, kview, ps),
                        waits=[("ac", gAB[par]["ac"]), ("dv", max(DV_BN, gAB[par]["dv"]))],
                        inc=("pe", 1))
                    PE_S2 = p.c("pe")
                    if kb == slot:
                        p.ev("vector", (lambda ps: lambda e: e.tensor_add(sc1[:], ps[:], mask_sb[:]))(ps),
                             waits=[("pe", PE_S2), ("dv", p.c("dv")), ("ac", p.c("ac"))], inc=("dv", 1))
                        gAB[par]["dv"] = p.c("dv")
                        p.ev("scalar", (lambda slot: lambda e: e.activation(
                            etf_sb[:, 0:512], sc1[:], AF.Exp, scale=isqc, bias=tmpc,
                            accum_out=mcol_sb[:, 25:26]))(slot),
                            waits=[("dv", p.c("dv"))], inc=("ac", 1))
                        gAB[par]["ac"] = p.c("ac")
                    else:
                        p.ev("scalar", (lambda ps, slot: lambda e: e.activation(
                            etf_sb[:, 0:512], ps[:], AF.Exp, scale=isqc, bias=tmpc,
                            accum_out=mcol_sb[:, 25:26]))(ps, slot),
                            waits=[("pe", PE_S2), ("dv", p.c("dv"))], inc=("ac", 1))
                        gAB[par]["ac"] = p.c("ac")
                    p.ev("vector", (lambda slot: lambda e: e.tensor_add(
                        lrun(slot), lrun(slot), mcol_sb[:, 25:26]))(slot),
                        waits=[("ac", p.c("ac"))], inc=("dv", 1))
                p.ev("scalar", (lambda slot: lambda e: e.activation(
                    facc(slot), tmpc, AF.Exp))(slot),
                    waits=[("dv", p.c("dv"))], inc=("ac", 1))
                p.ev("vector", (lambda slot: lambda e: e.reciprocal(tmpc, lrun(slot)))(slot),
                     waits=[("ac", p.c("ac"))], inc=("dv", 1))
                p.ev("vector", (lambda slot: lambda e: e.tensor_mul(
                    facc(slot), facc(slot), tmpc))(slot),
                    waits=[("dv", p.c("dv"))], inc=("dv", 1))
            DV_FAC = p.c("dv")
            if stop in ("p12", "p1"):
                continue
            p.ev("vector", lambda e: e.tensor_copy(pk_sb[:], mcol_sb[:, 16:24]),
                 waits=[("dv", DV_FAC)], inc=("dv", 1))
            p.ev("tensor", lambda e: e.transpose(pE[0:8, 0:128], pk_sb[:], idn_sb[:]),
                 waits=[("dv", p.c("dv")), ("ac", p.c("ac"))], inc=("pe", 1))
            p.ev("scalar", lambda e: e.activation(rsb_sb[0:8, 0:128], pE[0:8, 0:128], AF.Identity),
                 waits=[("pe", p.c("pe"))], inc=("ac", 1))
            AC_FR = p.c("ac")
            for g in range(8):
                p.ev("gpsimd", (lambda g: lambda e: e.dma_start(
                    out=prow_sb[0:1, g * 128:(g + 1) * 128], in_=rsb_sb[g:g + 1, 0:128]))(g),
                    waits=([("ac", AC_FR), ("so", p.c("so"))] if g == 0 else []), inc=("so", 16))
            SO_FR = p.c("so")
            p.ev("tensor", lambda e: e.matmul(pE[:, 0:512], oner_sb[:], prow_sb[0:1, 0:512],
                                              start=True, stop=True),
                 waits=[("so", SO_FR), ("ac", p.c("ac"))], inc=None)
            p.ev("tensor", lambda e: e.matmul(pE[:, 512:1024], oner_sb[:], prow_sb[0:1, 512:1024],
                                              start=True, stop=True),
                 waits=[], inc=("pe", 1))
            p.ev("scalar", lambda e: e.activation(fac_sb[:], pE[:, 0:1024], AF.Identity),
                 waits=[("pe", p.c("pe"))], inc=("ac", 1))
            AC_FB = p.c("ac")
            gE[0]["ac"] = max(gE[0]["ac"], AC_FB)
            gE[1]["ac"] = max(gE[1]["ac"], AC_FB)
            for kb in range(8):
                nq = TPC - kb * 128
                q0 = kb * 128
                for cc in range(4):
                    par = (kb * 4 + cc) % 2
                    pse = pE[:, par * 512:(par + 1) * 512]
                    pse2 = pC[:, par * 512:(par + 1) * 512]
                    n1 = min(nq, 512)
                    p.ev("tensor", (lambda kb, cc, n1, q0, kview, qview, pse: lambda e: e.matmul(
                        pse[:, 0:n1],
                        kview[:, (kb * 4 + cc) * 128:(kb * 4 + cc + 1) * 128],
                        qview[:, q0:q0 + n1],
                        start=True, stop=True))(kb, cc, n1, q0, kview, qview, pse),
                        waits=[("ac", gE[par]["ac"]), ("dv", gE[par]["dv"]),
                               ("ac", gD["ac"] if (kb == 0 and cc == 0) else 0)], inc=("pe", 1))
                    if nq > 512:
                        p.ev("tensor", (lambda kb, cc, nq, q0, kview, qview, pse2: lambda e: e.matmul(
                            pse2[:, 0:nq - 512],
                            kview[:, (kb * 4 + cc) * 128:(kb * 4 + cc + 1) * 128],
                            qview[:, q0 + 512:q0 + nq],
                            start=True, stop=True))(kb, cc, nq, q0, kview, qview, pse2),
                            waits=[], inc=("pe", 1))
                    PE_ST = p.c("pe")
                    eslot = [etf_sb[:, 0:TPC], etf_sb[:, TPC:2 * TPC]][par]
                    bslot = [etb_sb[:, 0:TPC], etb_sb[:, TPC:2 * TPC]][par]
                    p.ev("vector", (lambda pse, cc: lambda e: e.tensor_add(
                        sm_sb[:], pse[:, 0:128], maskt_sb[:, cc * 128:(cc + 1) * 128]))(pse, cc),
                        waits=[("pe", PE_ST), ("ac", gE[par]["ac"]), ("dv", p.c("dv"))],
                        inc=("dv", 1))
                    DV_SM = p.c("dv")
                    p.ev("scalar", (lambda eslot: lambda e: e.activation(
                        eslot[:, 0:128], sm_sb[:], AF.Exp, scale=isqc))(eslot),
                        waits=[("dv", DV_SM)], inc=("ac", 1))
                    if nq > 128:
                        p.ev("scalar", (lambda pse, eslot, n1: lambda e: e.activation(
                            eslot[:, 128:n1], pse[:, 128:n1], AF.Exp,
                            scale=isqc))(pse, eslot, n1),
                            inc=("ac", 1))
                    if nq > 512:
                        p.ev("scalar", (lambda eslot, nq, pse2: lambda e: e.activation(
                            eslot[:, 512:nq], pse2[:, 0:nq - 512], AF.Exp, scale=isqc))(eslot, nq, pse2),
                            inc=("ac", 1))
                    gE[par]["ac"] = p.c("ac")
                    p.ev("vector", (lambda eslot, bslot, nq, q0: lambda e: e.tensor_mul(
                        bslot[:, 0:nq], eslot[:, 0:nq], fac_sb[:, q0:q0 + nq]))(eslot, bslot, nq, q0),
                        waits=[("ac", gE[par]["ac"]), ("ac", AC_FB), ("pe", pe_ctx_done.get(par, 0)),
                               ("dv", p.c("dv"))], inc=("dv", 1))
                    gE[par]["dv"] = p.c("dv")
                    for s in range(kb, 8):
                        p.ev("tensor", (lambda kb, cc, s, q0, vview, bslot: lambda e: e.matmul(
                            pD[:, s * 128:(s + 1) * 128],
                            vview[:, (kb * 4 + cc) * 128:(kb * 4 + cc + 1) * 128],
                            bslot[:, s * 128 - q0: (s + 1) * 128 - q0],
                            start=(kb == 0 and cc == 0), stop=(s == kb and cc == 3)))(kb, cc, s, q0, vview, bslot),
                            waits=([("dv", gE[par]["dv"])] if s == kb else []),
                            inc=(("pe", 1) if s == 7 else None))
                    pe_ctx_done[par] = p.c("pe")
            PE_CTX = p.c("pe")
            for s in range(8):
                p.ev("scalar", (lambda h, s: lambda e: e.activation(
                    cxf_sb[:, h * TPC + s * 128: h * TPC + (s + 1) * 128],
                    pD[:, s * 128:(s + 1) * 128], AF.Identity))(h, s),
                    waits=([("pe", PE_CTX)] if s == 0 else []), inc=("ac", 1))
            gD["ac"] = p.c("ac")
        AC_ATT = p.c("ac")
        if stop in ("attn", "p12", "p1"):
            return _finish_C(nc, p, sems, outO, x3_sb, es)

        for ob in range(16):
            for hb in range(2):
                w0 = [wp_sb[:, 0:1024], wp_sb[:, 1024:2048]][hb]
                p.ev("sync", (lambda ob, hb, w0: lambda e: e.dma_start(
                    out=w0.rearrange("p (kb o) -> p kb o", kb=8),
                    in_=wo.rearrange("(kb p) o -> p kb o", p=128)[:, hb * 8:(hb + 1) * 8,
                                                                 ob * 128:(ob + 1) * 128]))(ob, hb, w0),
                    waits=[("pe", max(0, p.c("pe") - 8))], inc=("ld", 16))
                LD_WO = p.c("ld")
                p.ev("vector", (lambda w0: lambda e: e.tensor_scalar(
                    w0.bitcast(I32), w0.bitcast(I32), 0x8000, None, ALU.add))(w0),
                    waits=[("ld", LD_WO), ("pe", p.c("pe"))], inc=("dv", 1))
                DV_RTN = p.c("dv")
                for c in range(2):
                    for kbl in range(8):
                        kb = hb * 8 + kbl
                        p.ev("tensor", (lambda w0, kb, kbl, c: lambda e: e.matmul(
                            pC[:, c * 512:(c + 1) * 512], bf16_view(w0[:, kbl * 128:(kbl + 1) * 128]),
                            cxf_sb[:, kb * TPC + c * 512: kb * TPC + c * 512 + 512],
                            start=(kb == 0), stop=(kb == 15)))(w0, kb, kbl, c),
                            waits=([("dv", DV_RTN), ("ac", AC_ATT)] if kbl == 0 else []),
                            inc=(("pe", 1) if kbl == 7 else None))
            PE_WO = p.c("pe")
            p.ev("sync", (lambda ob: lambda e: e.dma_start(
                out=sc1[:], in_=x2T[ob * 128:(ob + 1) * 128, 0:512]))(ob),
                waits=[("dv", p.c("dv")), ("pe", max(0, p.c("pe") - 4))], inc=("ld", 16))
            p.ev("sync", (lambda ob: lambda e: e.dma_start(
                out=etf_sb[:, 0:512], in_=x2T[ob * 128:(ob + 1) * 128, 512:1024]))(ob),
                inc=("ld", 16))
            LD_X2 = p.c("ld")
            p.ev("vector", (lambda ob: lambda e: e.tensor_add(
                x3_sb[:, ob * TPC: ob * TPC + 512], pC[:, 0:512], sc1[:]))(ob),
                waits=[("pe", PE_WO), ("ld", LD_X2)], inc=("dv", 1))
            p.ev("vector", (lambda ob: lambda e: e.tensor_add(
                x3_sb[:, ob * TPC + 512: ob * TPC + 1024], pC[:, 512:1024], etf_sb[:, 0:512]))(ob),
                waits=[("dv", p.c("dv"))], inc=("dv", 1))
        DV_X3 = p.c("dv")
        if stop == "wo":
            return _finish_C(nc, p, sems, outO, x3_sb, es)

        inv_h = 1.0 / HQ
        R_M, R_VAR, R_A, R_B = (i * 512 for i in range(4))
        for (c0, w) in [(0, 512), (512, 512)]:
            for j in range(16):
                p.ev("scalar", (lambda j, c0, w: lambda e: e.activation(
                    sc1[:, 0:w], x3_sb[:, j * TPC + c0: j * TPC + c0 + w], AF.Square))(j, c0, w),
                    waits=[("dv", DV_X3), ("dv", p.c("dv")), ("pe", max(0, p.c("pe") - 1))],
                    inc=("ac", 1))
                p.ev("tensor", (lambda j, c0, w: lambda e: e.matmul(
                    pA[0:1, 0:w], onec_sb[:], x3_sb[:, j * TPC + c0: j * TPC + c0 + w],
                    start=(j == 0), stop=(j == 15)))(j, c0, w),
                    waits=[("dv", DV_X3)], inc=None)
                p.ev("tensor", (lambda j, w: lambda e: e.matmul(
                    pB[0:1, 0:w], onec_sb[:], sc1[:, 0:w],
                    start=(j == 0), stop=(j == 15)))(j, w),
                    waits=[("ac", p.c("ac"))], inc=("pe", 1))
            PE_ST = p.c("pe")

            def rowstep(fn, waits):
                p.ev("vector", fn, waits=waits, inc=("dv", 1))
            rowstep(lambda e, w=w: e.tensor_scalar_mul(row_sb[:, R_M:R_M + w], pA[0:1, 0:w], inv_h),
                    [("pe", PE_ST)])
            rowstep(lambda e, w=w: e.tensor_scalar_mul(row_sb[:, R_VAR:R_VAR + w], pB[0:1, 0:w], inv_h),
                    [("dv", p.c("dv"))])
            rowstep(lambda e, w=w: e.tensor_mul(row_sb[:, R_B:R_B + w], row_sb[:, R_M:R_M + w],
                                                row_sb[:, R_M:R_M + w]), [("dv", p.c("dv"))])
            rowstep(lambda e, w=w: e.tensor_sub(row_sb[:, R_VAR:R_VAR + w], row_sb[:, R_VAR:R_VAR + w],
                                                row_sb[:, R_B:R_B + w]), [("dv", p.c("dv"))])
            rowstep(lambda e, w=w: e.tensor_scalar_add(row_sb[:, R_VAR:R_VAR + w],
                                                       row_sb[:, R_VAR:R_VAR + w], 1e-5),
                    [("dv", p.c("dv"))])
            p.ev("scalar",
                 lambda e, w=w: e.activation(row_sb[:, R_VAR:R_VAR + w], row_sb[:, R_VAR:R_VAR + w],
                                             AF.Sqrt),
                 waits=[("dv", p.c("dv"))], inc=("ac", 1))
            rowstep(lambda e, w=w: e.reciprocal(row_sb[:, R_A:R_A + w], row_sb[:, R_VAR:R_VAR + w]),
                    [("ac", p.c("ac"))])
            rowstep(lambda e, w=w: e.tensor_mul(row_sb[:, R_B:R_B + w], row_sb[:, R_M:R_M + w],
                                                row_sb[:, R_A:R_A + w]), [("dv", p.c("dv"))])
            rowstep(lambda e, w=w: e.tensor_scalar_mul(row_sb[:, R_B:R_B + w], row_sb[:, R_B:R_B + w],
                                                       -1.0), [("dv", p.c("dv"))])
            DV_ROW = p.c("dv")
            p.ev("tensor",
                 lambda e, w=w: e.matmul(pC[:, 0:w], oner_sb[:], row_sb[:, R_A:R_A + w],
                                         start=True, stop=True),
                 waits=[("dv", DV_ROW)], inc=None)
            p.ev("tensor",
                 lambda e, w=w: e.matmul(pC[:, 512:512 + w], oner_sb[:], row_sb[:, R_B:R_B + w],
                                         start=True, stop=True),
                 waits=[], inc=("pe", 1))
            PE_BC = p.c("pe")
            for j in range(16):
                p.ev("vector", (lambda j, c0, w: lambda e: e.tensor_mul(
                    sc1[:, 0:w], x3_sb[:, j * TPC + c0: j * TPC + c0 + w], pC[:, 0:w]))(j, c0, w),
                    waits=[("pe", PE_BC), ("ac", p.c("ac")), ("dv", p.c("dv"))], inc=("dv", 1))
                p.ev("vector", (lambda j, c0, w: lambda e: e.tensor_tensor(
                    xln3_sb[:, j * TPC + c0: j * TPC + c0 + w], sc1[:, 0:w],
                    pC[:, 512:512 + w], ALU.add))(j, c0, w),
                    waits=[("dv", p.c("dv"))], inc=("dv", 1))
        DV_LN3 = p.c("dv")
        if stop == "ln":
            return _finish_C(nc, p, sems, outO, x3_sb, es)

        p.ev("sync", lambda e: e.dma_start(out=wp_sb[:, 0:128].rearrange("p (kb r) -> p kb r", r=8),
                                           in_=wr.rearrange("(kb p) r -> p kb r", p=128)),
             waits=[("pe", p.c("pe"))], inc=("ld", 16))
        LD_WR = p.c("ld")
        p.ev("vector", lambda e: e.tensor_scalar(
            wp_sb[:, 0:128].bitcast(I32), wp_sb[:, 0:128].bitcast(I32), 0x8000, None, ALU.add),
            waits=[("ld", LD_WR), ("dv", p.c("dv"))], inc=("dv", 1))
        DV_WR = p.c("dv")
        for c in range(2):
            for kb in range(16):
                p.ev("tensor", (lambda kb, c: lambda e: e.matmul(
                    pD[0:8, c * 512:(c + 1) * 512], bf16_view(wp_sb[:, kb * 8:(kb + 1) * 8]),
                    xln3_sb[:, kb * TPC + c * 512: kb * TPC + c * 512 + 512],
                    start=(kb == 0), stop=(kb == 15)))(kb, c),
                    waits=([("dv", DV_WR), ("dv", DV_LN3)] if kb == 0 else []),
                    inc=(("pe", 1) if kb == 15 else None))
        PE_R = p.c("pe")
        p.ev("scalar", lambda e: e.activation(rsb_sb[:], pD[0:8, :], AF.Identity),
             waits=[("pe", PE_R)], inc=("ac", 1))
        AC_R = p.c("ac")
        for s in range(8):
            p.ev("tensor", (lambda s: lambda e: e.transpose(
                pE[:, 0:8], rsb_sb[0:8, s * 128:(s + 1) * 128], idn_sb[0:8, 0:8]))(s),
                waits=[("ac", AC_R), ("ac", p.c("ac"))], inc=("pe", 1))
            p.ev("scalar", (lambda s: lambda e: e.activation(
                rT_sb[:, s * 8:(s + 1) * 8], pE[:, 0:8], AF.Identity))(s),
                waits=[("pe", p.c("pe"))], inc=("ac", 1))
        AC_RT = p.c("ac")
        for s in range(8):
            r8 = rT_sb[:, s * 8:(s + 1) * 8]

            def step(fn, waits=None):
                p.ev("vector", fn, waits=(waits or [("dv", p.c("dv"))]), inc=("dv", 1))
            step(lambda e, r8=r8: e.reduce_max(mcol_sb[:, 0:1], r8, axis=AX.X),
                 [("ac", AC_RT), ("dv", p.c("dv"))])
            step(lambda e, r8=r8, s=s: e.tensor_scalar(cmb_sb[:, s * 8:(s + 1) * 8],
                                                       r8, mcol_sb[:, 0:1], None, ALU.is_equal))
            step(lambda e, s=s: e.tensor_scalar(sc1[:, 0:8], cmb_sb[:, s * 8:(s + 1) * 8],
                                                1e30, None, ALU.mult))
            step(lambda e, r8=r8: e.tensor_sub(sc1[:, 8:16], r8, sc1[:, 0:8]))
            step(lambda e: e.reduce_max(mcol_sb[:, 1:2], sc1[:, 8:16], axis=AX.X))
            step(lambda e: e.tensor_sub(mcol_sb[:, 2:3], mcol_sb[:, 0:1], mcol_sb[:, 1:2]))
            p.ev("scalar", lambda e: e.activation(mcol_sb[:, 3:4], mcol_sb[:, 2:3], AF.Sigmoid),
                 waits=[("dv", p.c("dv"))], inc=("ac", 1))
            step(lambda e: e.tensor_scalar(mcol_sb[:, 4:5], mcol_sb[:, 3:4], -1.0, 1.0,
                                           ALU.mult, ALU.add), [("ac", p.c("ac"))])
            step(lambda e, r8=r8: e.tensor_scalar(sc1[:, 0:8], r8, mcol_sb[:, 1:2], None, ALU.is_equal))
            step(lambda e, s=s: e.tensor_scalar(cmb_sb[:, s * 8:(s + 1) * 8],
                                                cmb_sb[:, s * 8:(s + 1) * 8], mcol_sb[:, 3:4],
                                                None, ALU.mult))
            step(lambda e, s=s: e.scalar_tensor_tensor(cmb_sb[:, s * 8:(s + 1) * 8], sc1[:, 0:8],
                                                       mcol_sb[:, 4:5], cmb_sb[:, s * 8:(s + 1) * 8],
                                                       ALU.mult, ALU.add))
        DV_CMB = p.c("dv")
        p.ev("tensor", lambda e: e.transpose(pE[0:64, 0:128], cmb_sb[:], idn_sb[:]),
             waits=[("dv", DV_CMB), ("ac", p.c("ac"))], inc=("pe", 1))
        p.ev("scalar", lambda e: e.activation(cmt_sb[:], pE[0:64, 0:128], AF.Identity),
             waits=[("pe", p.c("pe"))], inc=("ac", 1))
        AC_CMT = p.c("ac")
        for e8 in range(8):
            for s in range(8):
                p.ev("gpsimd", (lambda e8, s: lambda e: e.dma_start(
                    out=crows_sb[0:1, e8 * TPC + s * 128: e8 * TPC + (s + 1) * 128],
                    in_=cmt_sb[s * 8 + e8: s * 8 + e8 + 1, :]))(e8, s),
                    waits=([("ac", AC_CMT), ("pe", p.c("pe"))] if (e8 == 0 and s == 0) else []),
                    inc=("so", 16))
        SO_CROW = p.c("so")
        if stop == "router":
            return _finish_C(nc, p, sems, outO, x3_sb, es)

        for e8 in range(8):
            for c in range(2):
                p.ev("tensor", (lambda e8, c: lambda e: e.matmul(
                    pC[:, 0:512], onerb_sb, crows_sb[0:1, e8 * TPC + c * 512: e8 * TPC + (c + 1) * 512],
                    start=True, stop=True))(e8, c),
                    waits=[("so", SO_CROW), ("dv", p.c("dv")), ("ac", p.c("ac")),
                           ("pe", p.c("pe"))], inc=("pe", 1))
                p.ev("scalar", (lambda c: lambda e: e.activation(
                    cbc_sb[:, c * 512:(c + 1) * 512], pC[:, 0:512], AF.Identity))(c),
                    waits=[("pe", p.c("pe"))], inc=("ac", 1))
            AC_CBC = p.c("ac")
            for iob in range(11):
                for mat, wsrc, off in [(0, wgm, 0), (1, wum, 2048)]:
                    for hh in range(2):
                        wv = wp_sb[:, off + hh * 1024: off + (hh + 1) * 1024]
                        p.ev("sync", (lambda e8, iob, wsrc, hh, wv: lambda e: e.dma_start(
                            out=wv.rearrange("p (kb o) -> p kb o", kb=8),
                            in_=wsrc.rearrange("(ee kb p) o -> p ee kb o", ee=8, p=128)[
                                :, e8, hh * 8:(hh + 1) * 8, iob * 128:(iob + 1) * 128]))(e8, iob, wsrc, hh, wv),
                            waits=[("pe", max(0, p.c("pe") - 24))], inc=("ld", 16))
                        p.ev("vector", (lambda wv: lambda e: e.tensor_scalar(
                            wv.bitcast(I32), wv.bitcast(I32), 0x8000, None, ALU.add))(wv),
                            waits=[("ld", p.c("ld")), ("pe", p.c("pe"))], inc=("dv", 1))
                DV_W = p.c("dv")
                for kb in range(16):
                    woff = (kb // 8) * 1024 + (kb % 8) * 128
                    p.ev("tensor", (lambda kb, woff: lambda e: e.matmul(
                        pA[:], bf16_view(wp_sb[:, woff:woff + 128]),
                        xln3_sb[:, kb * TPC:kb * TPC + 512],
                        start=(kb == 0), stop=(kb == 15)))(kb, woff),
                        waits=([("dv", DV_W)] if kb == 0 else []), inc=None)
                    p.ev("tensor", (lambda kb, woff: lambda e: e.matmul(
                        pB[:], bf16_view(wp_sb[:, woff:woff + 128]),
                        xln3_sb[:, kb * TPC + 512:kb * TPC + 1024],
                        start=(kb == 0), stop=(kb == 15)))(kb, woff), inc=None)
                    p.ev("tensor", (lambda kb, woff: lambda e: e.matmul(
                        pC[:, 0:512], bf16_view(wp_sb[:, 2048 + woff:2048 + woff + 128]),
                        xln3_sb[:, kb * TPC:kb * TPC + 512],
                        start=(kb == 0), stop=(kb == 15)))(kb, woff), inc=None)
                    p.ev("tensor", (lambda kb, woff: lambda e: e.matmul(
                        pC[:, 512:1024], bf16_view(wp_sb[:, 2048 + woff:2048 + woff + 128]),
                        xln3_sb[:, kb * TPC + 512:kb * TPC + 1024],
                        start=(kb == 0), stop=(kb == 15)))(kb, woff),
                        inc=(("pe", 1) if kb == 15 else None))
                PE_GU = p.c("pe")
                for c in range(2):
                    gps = [pA, pB][c]
                    p.ev("scalar", (lambda gps: lambda e: e.activation(sc3[:], gps[:], AF.Silu))(gps),
                         waits=[("pe", PE_GU), ("ac", p.c("ac")), ("dv", p.c("dv"))], inc=("ac", 1))
                    p.ev("vector", (lambda iob, c: lambda e: e.tensor_tensor(
                        hf_v[:, iob * TPC + c * 512: iob * TPC + (c + 1) * 512], sc3[:],
                        pC[:, c * 512:(c + 1) * 512], ALU.mult))(iob, c),
                        waits=[("ac", p.c("ac"))], inc=("dv", 1))
            DV_H = p.c("dv")
            for ob in range(16):
                for hh in range(2):
                    nkb = 6 if hh == 0 else 5
                    wv = wp_sb[:, hh * 1024: hh * 1024 + nkb * 128]
                    p.ev("sync", (lambda e8, ob, hh, nkb, wv: lambda e: e.dma_start(
                        out=wv.rearrange("p (kb o) -> p kb o", kb=nkb),
                        in_=wdm.rearrange("(ee kb p) o -> p ee kb o", ee=8, p=128)[
                            :, e8, hh * 6:hh * 6 + nkb, ob * 128:(ob + 1) * 128]))(e8, ob, hh, nkb, wv),
                        waits=[("pe", max(0, p.c("pe") - 4)), ("dv", p.c("dv") - 2 if p.c("dv") >= 2 else 0)],
                        inc=("ld", 16))
                    p.ev("vector", (lambda wv: lambda e: e.tensor_scalar(
                        wv.bitcast(I32), wv.bitcast(I32), 0x8000, None, ALU.add))(wv),
                        waits=[("ld", p.c("ld")), ("pe", p.c("pe"))], inc=("dv", 1))
                    DV_WD = p.c("dv")
                    for kbl in range(nkb):
                        kb = hh * 6 + kbl
                        p.ev("tensor", (lambda kb, kbl, hh, wv: lambda e: e.matmul(
                            pD[:, 0:512], bf16_view(wv[:, kbl * 128:(kbl + 1) * 128]),
                            hf_v[:, kb * TPC: kb * TPC + 512],
                            start=(kb == 0), stop=(kb == 10)))(kb, kbl, hh, wv),
                            waits=([("dv", DV_WD), ("dv", DV_H)] if kbl == 0 else []), inc=None)
                        p.ev("tensor", (lambda kb, kbl, hh, wv: lambda e: e.matmul(
                            pD[:, 512:1024], bf16_view(wv[:, kbl * 128:(kbl + 1) * 128]),
                            hf_v[:, kb * TPC + 512: kb * TPC + 1024],
                            start=(kb == 0), stop=(kb == 10)))(kb, kbl, hh, wv),
                            inc=(("pe", 1) if kbl == nkb - 1 else None))
                PE_D = p.c("pe")
                for c in range(2):
                    p.ev("vector", (lambda c: lambda e: e.tensor_tensor(
                        sc1[:], pD[:, c * 512:(c + 1) * 512], cbc_sb[:, c * 512:(c + 1) * 512],
                        ALU.mult))(c),
                        waits=[("pe", PE_D), ("ac", AC_CBC), ("dv", p.c("dv"))], inc=("dv", 1))
                    p.ev("vector", (lambda ob, c: lambda e: e.tensor_add(
                        x3_sb[:, ob * TPC + c * 512: ob * TPC + (c + 1) * 512],
                        x3_sb[:, ob * TPC + c * 512: ob * TPC + (c + 1) * 512], sc1[:]))(ob, c),
                        waits=[("dv", p.c("dv"))], inc=("dv", 1))
        DV_OUT = p.c("dv")
        for ob in range(16):
            p.ev("gpsimd", (lambda ob: lambda e: e.dma_start(
                out=outO[ob * 128:(ob + 1) * 128, :], in_=x3_sb[:, ob * TPC:(ob + 1) * TPC]))(ob),
                waits=([("dv", DV_OUT)] if ob == 0 else []), inc=("so", 16))
        p.ev("gpsimd", lambda e: None, waits=[("so", p.c("so"))], inc=None)

        with nc.Block() as block:
            p.emit(block, sems)
    return nc


def prep_C(ins, bres):
    x2f, qf, kf, vf = {}, {}, {}, {}
    for b in range(2):
        x2f[b] = np.concatenate([np.asarray(bres[4 * b + q]["x2o"], np.float32) for q in range(4)], axis=1)
        qf[b] = np.concatenate([np.asarray(bres[4 * b + q]["qO"]) for q in range(4)], axis=1)
        kf[b] = np.ascontiguousarray(
            np.concatenate([np.asarray(bres[4 * b + q]["kO"]) for q in range(4)], axis=1))
        vf[b] = np.ascontiguousarray(
            np.concatenate([np.asarray(bres[4 * b + q]["vO"]) for q in range(4)], axis=0))
    wgm = np.ascontiguousarray(ins["L1_ffn_Wg"].reshape(8 * HQ, 1408))
    wum = np.ascontiguousarray(ins["L1_ffn_Wu"].reshape(8 * HQ, 1408))
    wdm = np.ascontiguousarray(ins["L1_ffn_Wd"].reshape(8 * 1408, HQ))
    wo = np.ascontiguousarray(ins["L1_attn_Wo"])
    wr = np.ascontiguousarray(ins["L1_ffn_Wr"])
    onec = np.ones((128, 1), np.float32)
    oner = np.ones((1, 128), np.float32)
    idn = np.eye(128, dtype=np.float32)

    in_maps = []
    for c in range(8):
        b, r = c // 4, c % 4
        idx = np.concatenate([np.arange(128 * (4 * i + r), 128 * (4 * i + r) + 128)
                              for i in range(8)])
        qi = np.arange(128)[:, None]
        kc = np.arange(512)[None, :]
        mask = np.where(kc <= 128 * r + qi, 0.0, -1e30).astype(np.float32)
        mt = np.zeros((128, 512), np.float32)
        for cc in range(4):
            kl = np.arange(128)[:, None] + cc * 128
            qq = np.arange(128)[None, :]
            mt[:, cc * 128:(cc + 1) * 128] = np.where(kl <= 128 * r + qq, 0.0, -1e30)
        in_maps.append({
            "x2T": np.ascontiguousarray(x2f[b][:, idx]),
            "qT": np.ascontiguousarray(qf[b][:, idx]),
            "kF": kf[b], "vF": vf[b],
            "wo": wo, "wr": wr, "wgm": wgm, "wum": wum, "wdm": wdm,
            "maskS": mask, "maskTT": mt, "onec": onec, "oner": oner, "idn": idn,
        })
    return in_maps


def kernel(**inputs):
    ra = run_bass_kernel_spmd(build_A(), prep_A(inputs), core_ids=list(range(8)))
    ares = [ra.results[c] for c in range(8)]
    rb = run_bass_kernel_spmd(build_B(), prep_B(inputs, ares), core_ids=list(range(8)))
    bres = [rb.results[c] for c in range(8)]
    rc = run_bass_kernel_spmd(build_C(), prep_C(inputs, bres), core_ids=list(range(8)))
    out = np.zeros((BQ, SQ, HQ), np.float32)
    for c in range(8):
        b, r = c // 4, c % 4
        idx = np.concatenate([np.arange(128 * (4 * i + r), 128 * (4 * i + r) + 128)
                              for i in range(8)])
        out[b, idx, :] = np.asarray(rc.results[c]["outO"], np.float32).T
    return out
